# version 16
# baseline (speedup 1.0000x reference)
import numpy as np
import concourse.bass as bass
import concourse.mybir as mybir

F32 = mybir.dt.float32
BF16 = mybir.dt.bfloat16
A = mybir.ActivationFunctionType
OP = mybir.AluOpType

H = 2048
SEQ = 4096
T = 512
C = 128
NCC = T // C
NK, NV, DK, DV = 16, 32, 128, 128
NG = 8
NI = H // 128
EPS_LN = 1e-5
EPS_RMS = 1e-6
LN128C = 4.852030263919617

ENGINES = ("tensor", "vector", "scalar", "gpsimd", "sync")
ENG_SEM = {"tensor": "pe", "vector": "dv", "scalar": "ac", "gpsimd": "gp"}


class Builder:
    def __init__(self, nc):
        self.nc = nc
        self.prog = {e: [] for e in ENGINES}
        self.sems = {}
        self.cnt = {}
        self.dirty_w = {e: set() for e in ENGINES}
        self.dirty_r = {e: set() for e in ENGINES}
        self.waited = {e: {} for e in ENGINES}

    def sem(self, name):
        self.sems[name] = self.nc.alloc_semaphore(name)
        self.cnt[name] = 0

    @staticmethod
    def _nm(x):
        if x is None:
            return None
        t = getattr(x, "tensor", x)
        return getattr(t, "name", str(t))

    def op(self, eng, fn, reads=(), writes=()):
        r = {self._nm(a) for a in reads} - {None}
        w = {self._nm(a) for a in writes} - {None}
        if (r & self.dirty_w[eng]) or (w & (self.dirty_w[eng] | self.dirty_r[eng])):
            self.prog[eng].append(dict(fn=lambda e: e.drain(), incs=[]))
            self.dirty_w[eng] = set()
            self.dirty_r[eng] = set()
        self.dirty_r[eng] |= r
        self.dirty_w[eng] |= w
        self.prog[eng].append(dict(fn=fn, incs=[]))

    def mark(self, eng, sname=None, n=1):
        sname = sname or ENG_SEM[eng]
        item = self.prog[eng][-1]
        for (s0, _n0) in item["incs"]:
            if s0 == sname:
                return self.cnt[sname]
        assert not item["incs"], "one sem update per instruction"
        item["incs"].append((sname, n))
        self.cnt[sname] += n
        return self.cnt[sname]

    def wait(self, eng, sname, value):
        if value is None or value <= 0:
            return
        if self.waited[eng].get(sname, -1) >= value:
            return
        self.waited[eng][sname] = value
        h = self.sems[sname]
        self.prog[eng].append(dict(fn=lambda e, h=h, v=value: e.wait_ge(h, v), incs=[]))

    def replay(self, eng, e):
        for item in self.prog[eng]:
            inst = item["fn"](e)
            for (sname, n) in item["incs"]:
                inst.then_inc(self.sems[sname], n)


def build(NT=8):
    nc = bass.Bass("TRN2", target_bir_lowering=False)

    x_in = nc.dram_tensor("x_in", [H, SEQ], F32, kind="ExternalInput").ap()
    wq_d = nc.dram_tensor("wq_d", [96, 128, NI, 128], F32, kind="ExternalInput").ap()
    wo_d = nc.dram_tensor("wo_d", [NG, 16, 128, 4, 128], F32, kind="ExternalInput").ap()
    wf_d = nc.dram_tensor("wf_d", [16, 128, NI, 128], F32, kind="ExternalInput").ap()
    wba_d = nc.dram_tensor("wba_d", [128, NI, 64], F32, kind="ExternalInput").ap()
    cw_d = nc.dram_tensor("cw_d", [128, 64, 4], F32, kind="ExternalInput").ap()
    alog_d = nc.dram_tensor("alog_d", [32, 1], F32, kind="ExternalInput").ap()
    dtb_d = nc.dram_tensor("dtb_d", [32, 1], F32, kind="ExternalInput").ap()
    nw_d = nc.dram_tensor("nw_d", [128, 1], F32, kind="ExternalInput").ap()
    fs_d = nc.dram_tensor("fs_d", [128, 16], F32, kind="ExternalInput").ap()
    fb_d = nc.dram_tensor("fb_d", [128, 16], F32, kind="ExternalInput").ap()
    out_d = nc.dram_tensor("out_d", [H, SEQ], F32, kind="ExternalOutput").ap()
    wq16h_d = nc.dram_tensor("wq16h_d", [112, 128, NI, 128], BF16, kind="Internal").ap()
    wq16l_d = nc.dram_tensor("wq16l_d", [112, 128, NI, 128], BF16, kind="Internal").ap()
    wo16h_d = nc.dram_tensor("wo16h_d", [NG, 16, 128, 4, 128], BF16, kind="Internal").ap()
    wo16l_d = nc.dram_tensor("wo16l_d", [NG, 16, 128, 4, 128], BF16, kind="Internal").ap()

    sb = lambda n, s: nc.alloc_sbuf_tensor(n, s, F32).ap()
    x_buf = sb("x_buf", [128, NI * T])
    lnx16h = nc.alloc_sbuf_tensor("lnx16h", [128, NI * T], BF16).ap()
    lnx16l = nc.alloc_sbuf_tensor("lnx16l", [128, NI * T], BF16).ap()
    qkvc = sb("qkvc", [128, 8 * T])
    attn_h = nc.alloc_sbuf_tensor("attn", [128, NI * T], F32)
    attn = attn_h.ap()
    attn16 = attn_h.bitcast(BF16).ap()
    o_buf = sb("o_buf", [128, 4 * T])
    u_buf_h = nc.alloc_sbuf_tensor("u_buf", [128, 4 * 520], F32)
    u_buf = u_buf_h.ap()
    u16 = u_buf_h.bitcast(BF16).ap()
    state = sb("state", [128, 32 * 128])
    carry = sb("carry", [128, 64 * 3])
    wq16h_sb = nc.alloc_sbuf_tensor("wq16h_sb", [128, 2 * NI * 128], BF16).ap()
    wq16l_sb = nc.alloc_sbuf_tensor("wq16l_sb", [128, 2 * NI * 128], BF16).ap()
    wo16h_sb = nc.alloc_sbuf_tensor("wo16h_sb", [128, 2 * 4 * 128], BF16).ap()
    wo16l_sb = nc.alloc_sbuf_tensor("wo16l_sb", [128, 2 * 4 * 128], BF16).ap()
    wba16h = nc.alloc_sbuf_tensor("wba16h", [128, NI * 64], BF16).ap()
    wba16l = nc.alloc_sbuf_tensor("wba16l", [128, NI * 64], BF16).ap()
    cw_sb = sb("cw_sb", [128, 64 * 4])
    mS = sb("mS", [128, 128])
    ident = sb("ident", [128, 128])
    onesP = sb("onesP", [128, 1])
    ones_all = sb("ones_all", [128, 128])
    beta32 = sb("beta32", [32, T])
    sp32 = sb("sp32", [32, T])
    g32 = sp32
    gc32 = sb("gc32", [32, T])
    a32 = gc32[0:32, 0:T]
    gct = sb("gct", [128, NCC * 32])
    bett = sb("bett", [128, NCC * 32])
    qt_sb = [sb("qt_sb0", [128, 128]), sb("qt_sb1", [128, 128])]
    kt_sb = [sb("kt_sb0", [128, 128]), sb("kt_sb1", [128, 128])]
    ktn = [sb("ktn0", [128, 128]), sb("ktn1", [128, 128])]
    Qf = qt_sb
    Kf = kt_sb
    rqk = sb("rqk", [128, 4])
    NVH = 4
    e_t = [sb(f"e_t{i}", [128, 128]) for i in range(NVH)]
    Dfull = [sb(f"Dfull{i}", [128, 128]) for i in range(NVH)]
    Dincl = [sb(f"Dincl{i}", [128, 128]) for i in range(NVH)]
    Dbc = [sb(f"Dbc{i}", [128, 128]) for i in range(NVH)]
    Mt = [sb(f"Mt{i}", [128, 128]) for i in range(NVH)]
    curA = [sb(f"curA{i}", [128, 128]) for i in range(NVH)]
    curB = [sb(f"curB{i}", [128, 128]) for i in range(NVH)]
    curT = [sb(f"curT{i}", [128, 128]) for i in range(NVH)]
    QDf = [sb(f"QDf{i}", [128, 128]) for i in range(NVH)]
    Ut = [sb(f"Ut{i}", [128, 128]) for i in range(NVH)]
    Bfv = e_t
    Kddt = Dfull
    KDf = Ut
    g16 = [dict(
        h0=Mt[i].tensor.bitcast(BF16).ap(),
        h1=QDf[i].tensor.bitcast(BF16).ap(),
        l0=Ut[i].tensor.bitcast(BF16).ap(),
        l1=curA[i].tensor.bitcast(BF16).ap(),
    ) for i in range(NVH)]
    sq1 = sb("sq1", [128, T])
    sq2 = sb("sq2", [128, T])
    convacc_ = [sq1, sq2]
    bag = sq1
    sqsc = sq2[:, 0:128]
    rowbuf = sb("rowbuf", [1, T])
    negA = sb("negA", [32, 1])
    tmpA = sb("tmpA", [32, 1])
    dtb = sb("dtb", [32, 1])
    nw = sb("nw", [128, 1])
    ffs = sb("ffs", [128, 16])
    ffb = sb("ffb", [128, 16])

    P0 = nc.alloc_psum_tensor("P0", [128, 512], F32).ap()
    P1 = nc.alloc_psum_tensor("P1", [128, 512], F32).ap()
    POOLB = [nc.alloc_psum_tensor(f"PB{i}", [128, 512], F32).ap() for i in range(6)]

    b = Builder(nc)
    for s in ("wq0", "wq1", "wo0", "wo1", "mdma", "adma", "xdma", "sdma",
              "pin0", "pin1", "pout0", "pout1", "pe", "dv", "ac", "gp"):
        b.sem(s)

    pbank_free = [None, None]
    bank_state = [dict(free=[], busy=False) for _ in POOLB]
    bank_rr = [0]

    def acquire_bank(writer_eng="tensor"):
        for _ in range(len(POOLB)):
            i = bank_rr[0] % len(POOLB)
            bank_rr[0] += 1
            if not bank_state[i]["busy"]:
                break
        else:
            raise RuntimeError("psum pool exhausted: all banks busy")
        bank_state[i]["busy"] = True
        for (sname, v) in bank_state[i]["free"]:
            b.wait(writer_eng, sname, v)
        bank_state[i]["free"] = []
        return i, POOLB[i]

    def free_bank(i, sname, v):
        bank_state[i]["free"].append((sname, v))
        bank_state[i]["busy"] = False

    def pemark():
        return b.mark("tensor", "pe")

    def phase_barrier(engines=("tensor", "vector", "scalar")):
        snap = {"pe": b.cnt["pe"], "dv": b.cnt["dv"], "ac": b.cnt["ac"]}
        for e_ in engines:
            for s_, v_ in snap.items():
                if ENG_SEM[e_] != s_:
                    b.wait(e_, s_, v_)

    wq_stream = []
    for tc in range(NT):
        for ci in range(112):
            wq_stream.append((wq16h_d[ci], wq16l_d[ci]))
    wo_stream = []
    for tc in range(NT):
        for g in range(NG):
            for j in range(16):
                wo_stream.append((wo16h_d[g, j], wo16l_d[g, j]))

    wq_ptr, wq_used, wq_done, wq_freed = [0], [0], {}, [None, None]
    wo_ptr, wo_used, wo_done, wo_freed = [0], [0], {}, [None, None]

    def _dma_next(ptr, stream, freed, buf, width, done, sembase):
        i = ptr[0]
        if i >= len(stream):
            return
        slot = i % 2
        if freed[slot] is not None:
            b.wait("sync", freed[slot][0], freed[slot][1])
        src = stream[i]
        if isinstance(src, tuple):
            hbuf, lbuf = buf
            dh = hbuf[:, slot * width:(slot + 1) * width]
            dl = lbuf[:, slot * width:(slot + 1) * width]
            b.op("sync", lambda e, d=dh, s=src[0]: e.dma_start(out=d, in_=s))
            b.mark("sync", sembase + str(slot), 16)
            b.op("sync", lambda e, d=dl, s=src[1]: e.dma_start(out=d, in_=s))
            done[i] = (sembase + str(slot), b.mark("sync", sembase + str(slot), 16))
        else:
            dst = buf[:, slot * width:(slot + 1) * width]
            b.op("sync", lambda e, d=dst, s=src: e.dma_start(out=d, in_=s))
            done[i] = (sembase + str(slot), b.mark("sync", sembase + str(slot), 16))
        ptr[0] += 1

    def wq_consume():
        idx = wq_used[0]
        while wq_ptr[0] <= idx + 1 and wq_ptr[0] < len(wq_stream):
            _dma_next(wq_ptr, wq_stream, wq_freed, (wq16h_sb, wq16l_sb), NI * 128, wq_done, "wq")
        slot = idx % 2
        return (wq16h_sb[:, slot * NI * 128:(slot + 1) * NI * 128],
                wq16l_sb[:, slot * NI * 128:(slot + 1) * NI * 128]), wq_done[idx]

    def wq_release(sname, v):
        wq_freed[wq_used[0] % 2] = (sname, v)
        wq_used[0] += 1

    def wo_consume():
        idx = wo_used[0]
        while wo_ptr[0] <= idx + 1 and wo_ptr[0] < len(wo_stream):
            _dma_next(wo_ptr, wo_stream, wo_freed, (wo16h_sb, wo16l_sb), 512, wo_done, "wo")
        slot = idx % 2
        return (wo16h_sb[:, slot * 512:(slot + 1) * 512],
                wo16l_sb[:, slot * 512:(slot + 1) * 512]), wo_done[idx]

    def wo_release(sname, v):
        wo_freed[wo_used[0] % 2] = (sname, v)
        wo_used[0] += 1

    b.op("gpsimd", lambda e: e.memset(curA[0][:], -1.0), writes=[curA[0]])
    b.op("gpsimd", lambda e: e.affine_select(mS[:], curA[0][:], pattern=[[1, 128]], base=0,
                                             channel_multiplier=-1, compare_op=OP.is_gt, fill=0.0),
         reads=[curA[0]], writes=[mS])
    b.op("gpsimd", lambda e: e.memset(curA[0][:], 1.0), writes=[curA[0]])
    b.op("gpsimd", lambda e: e.affine_select(ident[:], curA[0][:], pattern=[[1, 128]], base=0,
                                             channel_multiplier=-1, compare_op=OP.is_equal, fill=0.0),
         reads=[curA[0]], writes=[ident])
    b.op("gpsimd", lambda e: e.memset(onesP[:], 1.0), writes=[onesP])
    b.op("gpsimd", lambda e: e.memset(ones_all[:], 1.0), writes=[ones_all])
    b.op("gpsimd", lambda e: e.memset(state[:], 0.0), writes=[state])
    b.op("gpsimd", lambda e: e.memset(carry[:], 0.0), writes=[carry])
    gp_init = b.mark("gpsimd")
    dv_init = 0

    for dst, src in [(cw_sb[:], cw_d), (tmpA[:], alog_d), (dtb[:], dtb_d),
                     (nw[:], nw_d), (ffs[:], fs_d), (ffb[:], fb_d)]:
        b.op("gpsimd", lambda e, d=dst, s=src: e.dma_start(out=d, in_=s))
        b.mark("gpsimd", "mdma", 16)
    ld_misc = b.cnt["mdma"]
    b.wait("scalar", "mdma", ld_misc)
    b.op("scalar", lambda e: e.activation(out=negA[:], in_=tmpA[:], func=A.Exp),
         reads=[tmpA], writes=[negA])
    ac_negA = b.mark("scalar")
    b.wait("vector", "ac", ac_negA)
    b.op("vector", lambda e: e.tensor_scalar(out=negA[:], in0=negA[:], scalar1=-1.0, scalar2=None,
                                             op0=OP.mult, op1=OP.bypass),
         reads=[negA], writes=[negA])
    dv_negA = b.mark("vector")
    b.wait("tensor", "gp", gp_init)
    b.wait("tensor", "dv", dv_init)
    b.wait("scalar", "gp", gp_init)

    prol_dv = [0, 0]
    pin_cnt = [0, 0]
    pout_cnt = [0, 0]

    def prol_batch(cols, width):
        n = len(cols)
        slot = prol_batch.it % 2
        prol_batch.it += 1
        xbase = slot * 4096
        for k, (s, _h, _l) in enumerate(cols):
            if k == 0 and prol_dv[slot]:
                b.wait("sync", "dv", prol_dv[slot])
            b.op("sync", lambda e, d=x_buf[:, xbase + k * width: xbase + (k + 1) * width],
                 s=s: e.dma_start(out=d, in_=s))
            pin_cnt[slot] += 16
            b.mark("sync", "pin" + str(slot), 16)
        b.wait("vector", "pin" + str(slot), pin_cnt[slot])
        b.wait("vector", "pout" + str(slot), pout_cnt[slot])
        tot = n * width
        xs = x_buf[:, xbase:xbase + tot]
        hs = attn16[:, slot * 4096:slot * 4096 + tot]
        ls = attn16[:, 8192 + slot * 4096:8192 + slot * 4096 + tot]
        b.op("vector", lambda e, hs=hs, xs=xs: e.tensor_copy(hs, xs),
             reads=[x_buf], writes=[attn])
        b.op("vector", lambda e, ls=ls, xs=xs, hs=hs: e.tensor_tensor(
            out=ls, in0=xs, in1=hs, op=OP.subtract),
            reads=[x_buf, attn], writes=[attn])
        prol_dv[slot] = b.mark("vector")
        b.wait("sync", "dv", prol_dv[slot])
        for k, (_s, h, l) in enumerate(cols):
            b.op("sync", lambda e, d=h, s=attn16[:, slot * 4096 + k * width:
                                                  slot * 4096 + (k + 1) * width]: e.dma_start(out=d, in_=s))
            b.mark("sync", "pout" + str(slot), 16)
            b.op("sync", lambda e, d=l, s=attn16[:, 8192 + slot * 4096 + k * width:
                                                  8192 + slot * 4096 + (k + 1) * width]: e.dma_start(out=d, in_=s))
            b.mark("sync", "pout" + str(slot), 16)
            pout_cnt[slot] += 32
    prol_batch.it = 0

    for c0 in range(0, 112, 2):
        cols = []
        for ci in range(c0, min(c0 + 2, 112)):
            srcap = wq_d[ci] if ci < 96 else wf_d[ci - 96]
            cols.append((srcap, wq16h_d[ci], wq16l_d[ci]))
        prol_batch(cols, 2048)
    for gi0 in range(0, NG * 16, 8):
        cols = []
        for gi in range(gi0, gi0 + 8):
            g_, j_ = gi // 16, gi % 16
            cols.append((wo_d[g_, j_], wo16h_d[g_, j_], wo16l_d[g_, j_]))
        prol_batch(cols, 512)
    xsl = x_buf[:, 0:NI * 64]
    b.wait("sync", "dv", prol_dv[0])
    b.wait("sync", "dv", prol_dv[1])
    b.op("sync", lambda e, d=xsl: e.dma_start(out=d, in_=wba_d))
    pin_cnt[0] += 16
    b.mark("sync", "pin0", 16)
    b.wait("vector", "pin0", pin_cnt[0])
    b.op("vector", lambda e, x=xsl: e.tensor_copy(wba16h[:], x), reads=[x_buf], writes=[wba16h])
    b.op("vector", lambda e, x=xsl: e.tensor_tensor(out=wba16l[:], in0=x, in1=wba16h[:],
                                                    op=OP.subtract),
         reads=[x_buf, wba16h], writes=[wba16l])
    wba_ready = b.mark("vector")
    b.wait("sync", "pout0", pout_cnt[0])
    b.wait("sync", "pout1", pout_cnt[1])
    b.wait("sync", "dv", wba_ready)
    prologue_done = b.cnt["dv"]

    def ln_stats(src_buf, tile_deps=None):
        bx_i, bx = acquire_bank()
        bq_i, bq = acquire_bank()
        sq_marks = [None, None]
        for i in range(NI):
            sq = sq1 if i % 2 == 0 else sq2
            if tile_deps is not None:
                for (sn, sv) in tile_deps[i]:
                    b.wait("scalar", sn, sv)
                    b.wait("tensor", sn, sv)
            if sq_marks[i % 2] is not None:
                b.wait("scalar", "pe", sq_marks[i % 2])
            b.op("scalar", lambda e, s=sq, i=i: e.activation(
                out=s[:], in_=src_buf[:, i * T:(i + 1) * T], func=A.Square),
                reads=[src_buf], writes=[sq])
            acm = b.mark("scalar")
            b.op("tensor", lambda e, i=i: e.matmul(
                bx[0:1, 0:T], onesP[:], src_buf[:, i * T:(i + 1) * T],
                start=(i == 0), stop=(i == NI - 1)))
            b.wait("tensor", "ac", acm)
            b.op("tensor", lambda e, s=sq, i=i: e.matmul(
                bq[0:1, 0:T], onesP[:], s[:], start=(i == 0), stop=(i == NI - 1)))
            sq_marks[i % 2] = pemark()
        pe_done = b.cnt["pe"]
        b.wait("scalar", "pe", pe_done)
        b.op("scalar", lambda e: e.copy(rowbuf[:], bx[0:1, 0:T]), writes=[rowbuf])
        m1 = b.mark("scalar")
        free_bank(bx_i, "ac", m1)
        bX_i, bX = acquire_bank()
        b.wait("tensor", "ac", m1)
        b.op("tensor", lambda e: e.matmul(bX[:, 0:T], ones_all[0:1, :], rowbuf[:],
                                          start=True, stop=True))
        pmX = pemark()
        b.wait("scalar", "pe", pmX)
        b.op("scalar", lambda e: e.copy(rowbuf[:], bq[0:1, 0:T]), writes=[rowbuf])
        m2 = b.mark("scalar")
        free_bank(bq_i, "ac", m2)
        bQ_i, bQ = acquire_bank()
        b.wait("tensor", "ac", m2)
        b.op("tensor", lambda e: e.matmul(bQ[:, 0:T], ones_all[0:1, :], rowbuf[:],
                                          start=True, stop=True))
        pmQ = pemark()
        b.wait("vector", "pe", pmQ)
        b.op("vector", lambda e: e.tensor_scalar(out=sq1[:], in0=bX[:, 0:T], scalar1=1.0 / H,
                                                 scalar2=None, op0=OP.mult, op1=OP.bypass),
             writes=[sq1])
        mX = b.mark("vector")
        free_bank(bX_i, "dv", mX)
        b.op("vector", lambda e: e.tensor_tensor(out=sq2[:], in0=sq1[:], in1=sq1[:], op=OP.mult),
             reads=[sq1], writes=[sq2])
        b.op("vector", lambda e: e.scalar_tensor_tensor(
            out=bQ[:, 0:T], in0=bQ[:, 0:T], scalar=1.0 / H, in1=sq2[:],
            op0=OP.mult, op1=OP.subtract),
            reads=[sq2, bQ], writes=[bQ])
        b.op("vector", lambda e: e.tensor_scalar(out=bQ[:, 0:T], in0=bQ[:, 0:T], scalar1=EPS_LN,
                                                 scalar2=None, op0=OP.add, op1=OP.bypass),
             reads=[bQ], writes=[bQ])
        mv = b.mark("vector")
        b.wait("scalar", "dv", mv)
        b.op("scalar", lambda e: e.activation(out=bQ[:, 0:T], in_=bQ[:, 0:T], func=A.Ln),
             reads=[bQ], writes=[bQ])
        mln3 = b.mark("scalar")
        b.wait("vector", "ac", mln3)
        b.op("vector", lambda e: e.tensor_scalar(out=bQ[:, 0:T], in0=bQ[:, 0:T], scalar1=-0.5,
                                                 scalar2=None, op0=OP.mult, op1=OP.bypass),
             reads=[bQ], writes=[bQ])
        mv2 = b.mark("vector")
        b.wait("scalar", "dv", mv2)
        b.op("scalar", lambda e: e.activation(out=bQ[:, 0:T], in_=bQ[:, 0:T], func=A.Exp),
             reads=[bQ], writes=[bQ])
        mac = b.mark("scalar")
        return bQ_i, bQ, mac

    def ln_apply(src_buf, dst_buf, stats, residual=False, split=None):
        ln_apply.tile_marks = []
        bQ_i, bQ, mac = stats
        b.wait("vector", "ac", mac)
        for i in range(NI):
            b.op("vector", lambda e, i=i: e.tensor_tensor(
                out=sq2[:], in0=src_buf[:, i * T:(i + 1) * T], in1=sq1[:], op=OP.subtract),
                reads=[src_buf, sq1], writes=[sq2])
            if split is not None:
                hi, lo = split
                b.op("vector", lambda e, i=i: e.tensor_tensor(
                    out=sq2[:], in0=sq2[:], in1=bQ[:, 0:T], op=OP.mult),
                    reads=[sq2], writes=[sq2])
                b.op("vector", lambda e, i=i, hi=hi: e.tensor_copy(
                    hi[:, i * T:(i + 1) * T], sq2[:]),
                    reads=[sq2], writes=[hi])
                b.op("vector", lambda e, i=i, hi=hi, lo=lo: e.tensor_tensor(
                    out=lo[:, i * T:(i + 1) * T], in0=sq2[:], in1=hi[:, i * T:(i + 1) * T],
                    op=OP.subtract),
                    reads=[sq2, hi], writes=[lo])
                ln_apply.tile_marks.append(b.mark("vector"))
            elif residual:
                b.op("vector", lambda e, i=i: e.tensor_tensor(
                    out=sq2[:], in0=sq2[:], in1=bQ[:, 0:T], op=OP.mult),
                    reads=[sq2], writes=[sq2])
                b.op("vector", lambda e, i=i: e.tensor_tensor(
                    out=dst_buf[:, i * T:(i + 1) * T], in0=dst_buf[:, i * T:(i + 1) * T],
                    in1=sq2[:], op=OP.add),
                    reads=[dst_buf, sq2], writes=[dst_buf])
                ln_apply.tile_marks.append(b.mark("vector"))
            else:
                b.op("vector", lambda e, i=i: e.tensor_tensor(
                    out=dst_buf[:, i * T:(i + 1) * T], in0=sq2[:], in1=bQ[:, 0:T], op=OP.mult),
                    reads=[sq2], writes=[dst_buf])
        dv1 = b.mark("vector")
        free_bank(bQ_i, "dv", dv1)
        return dv1


    x_store_mark = 0

    for tc in range(NT):
        col0 = tc * T
        if tc == 0:
            b.wait("gpsimd", "dv", prologue_done)
        b.wait("gpsimd", "sdma", x_store_mark)
        for i in range(NI):
            b.op("gpsimd", lambda e, i=i, col0=col0: e.dma_start(
                out=x_buf[:, i * T:(i + 1) * T], in_=x_in[i * 128:(i + 1) * 128, col0:col0 + T]))
            b.mark("gpsimd", "xdma", 16)
        x_loaded = b.cnt["xdma"]
        for e_ in ("tensor", "scalar", "vector"):
            b.wait(e_, "xdma", x_loaded)

        st1 = ln_stats(x_buf)
        lnx_ready = ln_apply(x_buf, None, st1, split=(lnx16h, lnx16l))
        lnx_tm = list(ln_apply.tile_marks)

        bba_i, bba = acquire_bank()
        b.wait("tensor", "dv", wba_ready)
        for i in range(NI):
            b.wait("tensor", "dv", lnx_tm[i])
            wh = wba16h[:, i * 64:(i + 1) * 64]
            wl = wba16l[:, i * 64:(i + 1) * 64]
            mh = lnx16h[:, i * T:(i + 1) * T]
            ml = lnx16l[:, i * T:(i + 1) * T]
            b.op("tensor", lambda e, wh=wh, mh=mh, bba=bba, i=i: e.matmul(
                bba[0:64, 0:T], wh, mh, start=(i == 0), stop=False))
            b.op("tensor", lambda e, wh=wh, ml=ml, bba=bba: e.matmul(
                bba[0:64, 0:T], wh, ml, start=False, stop=False))
            b.op("tensor", lambda e, wl=wl, mh=mh, bba=bba, i=i: e.matmul(
                bba[0:64, 0:T], wl, mh, start=False, stop=(i == NI - 1)))
        ba_done = pemark()
        b.wait("scalar", "pe", ba_done)
        b.op("scalar", lambda e, bba=bba: e.activation(out=beta32[:], in_=bba[0:32, 0:T], func=A.Sigmoid),
             writes=[beta32])
        b.op("scalar", lambda e, bba=bba: e.copy(bag[32:64, 0:T], bba[32:64, 0:T]), writes=[bag])
        ba_ac = b.mark("scalar")
        free_bank(bba_i, "ac", ba_ac)
        b.wait("gpsimd", "ac", ba_ac)
        b.op("gpsimd", lambda e: e.dma_start(out=a32, in_=bag[32:64, 0:T]), reads=[bag])
        b.mark("gpsimd", "adma", 16)
        a_shift = b.cnt["adma"]
        b.wait("scalar", "adma", a_shift)
        b.op("scalar", lambda e: e.activation(out=sp32[:], in_=a32, func=A.Exp,
                                              bias=dtb[:], scale=1.0),
             reads=[gc32, dtb], writes=[sp32])
        b.op("scalar", lambda e: e.activation(out=sp32[:], in_=sp32[:], func=A.Ln,
                                              bias=1.0, scale=1.0),
             reads=[sp32], writes=[sp32])
        sp_ac = b.mark("scalar")
        b.wait("vector", "ac", sp_ac)
        b.wait("vector", "dv", dv_negA)
        b.op("vector", lambda e: e.tensor_scalar(out=g32[:], in0=sp32[:], scalar1=negA[:], scalar2=None,
                                                 op0=OP.mult, op1=OP.bypass),
             reads=[sp32, negA], writes=[g32])
        for cc in range(NCC):
            b.op("vector", lambda e, cc=cc: e.tensor_tensor_scan(
                out=gc32[:, cc * C:(cc + 1) * C], data0=g32[:, cc * C:(cc + 1) * C],
                data1=g32[:, cc * C:(cc + 1) * C], initial=0.0, op0=OP.add, op1=OP.bypass),
                reads=[g32], writes=[gc32])
        gates_dv = b.mark("vector")
        b.wait("tensor", "dv", gates_dv)
        for cc in range(NCC):
            for (srcb, dstb) in ((gc32, gct), (beta32, bett)):
                bt_i, bt = acquire_bank()
                b.op("tensor", lambda e, cc=cc, srcb=srcb, bt=bt: e.transpose(
                    bt[:, 0:32], srcb[0:32, cc * C:(cc + 1) * C], ident[0:32, 0:32]))
                pm = pemark()
                b.wait("scalar", "pe", pm)
                b.op("scalar", lambda e, cc=cc, dstb=dstb, bt=bt: e.copy(
                    dstb[:, cc * 32:(cc + 1) * 32], bt[:, 0:32]),
                    writes=[dstb])
                free_bank(bt_i, "ac", b.mark("scalar"))
        gates_ac = b.cnt["ac"]
        b.wait("vector", "ac", gates_ac)

        pending = []
        last_gw = None
        for g in range(NG):
            conv_marks = {}
            fill_iter = iter(pending)
            for jj in range(8):
                wap, wv = wq_consume()
                pb = jj % 2
                bank = P0 if pb == 0 else P1
                b.wait("tensor", wv[0], wv[1])
                if pbank_free[pb] is not None:
                    b.wait("tensor", pbank_free[pb][0], pbank_free[pb][1])
                whap, wlap = wap
                for i in range(NI):
                    b.wait("tensor", "dv", lnx_tm[i])
                    wh = whap[:, i * 128:(i + 1) * 128]
                    wl = wlap[:, i * 128:(i + 1) * 128]
                    mh = lnx16h[:, i * T:(i + 1) * T]
                    ml = lnx16l[:, i * T:(i + 1) * T]
                    b.op("tensor", lambda e, wh=wh, mh=mh, bank=bank, i=i: e.matmul(
                        bank[:, 0:T], wh, mh, start=(i == 0), stop=False))
                    b.op("tensor", lambda e, wh=wh, ml=ml, bank=bank: e.matmul(
                        bank[:, 0:T], wh, ml, start=False, stop=False))
                    b.op("tensor", lambda e, wl=wl, mh=mh, bank=bank, i=i: e.matmul(
                        bank[:, 0:T], wl, mh, start=False, stop=(i == NI - 1)))
                mm_done = pemark()
                wq_release("pe", mm_done)
                slot = jj % 2
                rawap = u_buf[:, slot * 520: slot * 520 + 520]
                jj_g = 8 * g + jj
                b.wait("scalar", "pe", mm_done)
                if jj >= 2:
                    b.wait("scalar", conv_marks[("ceng", jj - 2)], conv_marks[("conv", jj - 2)])
                b.op("scalar", lambda e, rawap=rawap, bank=bank: e.copy(rawap[:, 3:515], bank[:, 0:T]),
                     writes=[u_buf])
                evac_m = b.mark("scalar")
                conv_marks[("evac", jj)] = evac_m
                pbank_free[pb] = ("ac", evac_m)
                ceng = "vector"
                cs = ENG_SEM[ceng]
                b.wait(ceng, "ac", evac_m)
                if jj >= 2:
                    b.wait(ceng, conv_marks[("ceng", jj - 2)], conv_marks[("conv", jj - 2)])
                acc = convacc_[0] if ceng == "vector" else convacc_[1]
                b.op(ceng, lambda e, rawap=rawap, jj_g=jj_g: e.tensor_copy(
                    rawap[:, 0:3], carry[:, jj_g * 3:jj_g * 3 + 3]),
                    reads=[carry], writes=[u_buf])
                b.op(ceng, lambda e, rawap=rawap, jj_g=jj_g, acc=acc: e.tensor_scalar(
                    out=acc[:], in0=rawap[:, 0:T], scalar1=cw_sb[:, jj_g * 4:jj_g * 4 + 1],
                    scalar2=None, op0=OP.mult, op1=OP.bypass),
                    reads=[u_buf, cw_sb], writes=[acc])
                for tau in range(1, 4):
                    b.op(ceng, lambda e, rawap=rawap, jj_g=jj_g, acc=acc, tau=tau: e.scalar_tensor_tensor(
                        out=acc[:], in0=rawap[:, tau:tau + T],
                        scalar=cw_sb[:, jj_g * 4 + tau:jj_g * 4 + tau + 1],
                        in1=acc[:], op0=OP.mult, op1=OP.add),
                        reads=[u_buf, cw_sb, acc], writes=[acc])
                b.op(ceng, lambda e, rawap=rawap, jj_g=jj_g: e.tensor_copy(
                    carry[:, jj_g * 3:jj_g * 3 + 3], rawap[:, 512:515]),
                    reads=[u_buf], writes=[carry])
                cm = b.mark(ceng, cs)
                conv_marks[("conv", jj)] = cm
                conv_marks[("ceng", jj)] = cs
                b.wait("scalar", cs, cm)
                b.op("scalar", lambda e, jj=jj, acc=acc: e.activation(
                    out=qkvc[:, jj * T:(jj + 1) * T], in_=acc[:], func=A.Silu),
                    reads=[acc], writes=[qkvc])
                conv_marks[("silu", jj)] = b.mark("scalar")
                nfill = 1 if jj < 2 else 3
                for _ in range(nfill):
                    f = next(fill_iter, None)
                    if f is not None:
                        f()
            for f in fill_iter:
                f()
            pending = []
            qkv_ready = b.cnt["ac"]

            st_upd_last = [0]
            st_upd_sl = [0, 0, 0, 0]
            prior_finish_pe = [0]
            prior_fin_sl = [0, 0, 0, 0]
            o_done = [0, 0, 0, 0]
            z_evac = [None, None, None, None]

            def emit_zcol(jz):
                wap, wv = wq_consume()
                pb = jz % 2
                bank = P0 if pb == 0 else P1
                b.wait("tensor", wv[0], wv[1])
                if jz < 2:
                    b.wait("tensor", "ac", qkv_ready)
                if pbank_free[pb] is not None:
                    b.wait("tensor", pbank_free[pb][0], pbank_free[pb][1])
                whap, wlap = wap
                for i in range(NI):
                    b.wait("tensor", "dv", lnx_tm[i])
                    wh = whap[:, i * 128:(i + 1) * 128]
                    wl = wlap[:, i * 128:(i + 1) * 128]
                    mh = lnx16h[:, i * T:(i + 1) * T]
                    ml = lnx16l[:, i * T:(i + 1) * T]
                    b.op("tensor", lambda e, wh=wh, mh=mh, bank=bank, i=i: e.matmul(
                        bank[:, 0:T], wh, mh, start=(i == 0), stop=False))
                    b.op("tensor", lambda e, wh=wh, ml=ml, bank=bank: e.matmul(
                        bank[:, 0:T], wh, ml, start=False, stop=False))
                    b.op("tensor", lambda e, wl=wl, mh=mh, bank=bank, i=i: e.matmul(
                        bank[:, 0:T], wl, mh, start=False, stop=(i == NI - 1)))
                mm_done = pemark()
                wq_release("pe", mm_done)
                b.wait("scalar", "pe", mm_done)
                if jz < 2:
                    for jprev in (6, 7):
                        b.wait("scalar", conv_marks[("ceng", jprev)], conv_marks[("conv", jprev)])
                b.op("scalar", lambda e, jz=jz, bank=bank: e.copy(
                    u_buf[:, jz * 520:jz * 520 + 512], bank[:, 0:T]),
                    writes=[u_buf])
                z_evac[jz] = b.mark("scalar")
                pbank_free[pb] = ("ac", z_evac[jz])

            for cc in range(NCC):
                ccol = cc * C
                raw = {}
                for kh2 in range(2):
                    raw[kh2] = (qkvc[:, (4 * kh2) * T + ccol: (4 * kh2) * T + ccol + C],
                                qkvc[:, (4 * kh2 + 1) * T + ccol: (4 * kh2 + 1) * T + ccol + C])
                tbanks = {}
                for kh2 in range(2):
                    b.wait("tensor", "ac", conv_marks[("silu", 4 * kh2)])
                    b.wait("tensor", "ac", conv_marks[("silu", 4 * kh2 + 1)])
                    bq_i, bq = acquire_bank()
                    b.op("tensor", lambda e, a=raw[kh2][0], bq=bq: e.transpose(bq[:, 0:C], a, ident[:]))
                    pq = pemark()
                    bk_i, bk = acquire_bank()
                    b.op("tensor", lambda e, a=raw[kh2][1], bk=bk: e.transpose(bk[:, 0:C], a, ident[:]))
                    pk = pemark()
                    tbanks[kh2] = (bq_i, bq, pq, bk_i, bk, pk)
                for kh2 in range(2):
                    bq_i, bq, pq, bk_i, bk, pk = tbanks[kh2]
                    sqslice = sq2[:, kh2 * 128:(kh2 + 1) * 128]
                    b.wait("scalar", "pe", pq)
                    b.op("scalar", lambda e, bq=bq, kh2=kh2: e.copy(qt_sb[kh2][:], bq[:, 0:C]),
                         writes=[qt_sb[kh2]])
                    b.op("scalar", lambda e, bq=bq, s=sqslice, kh2=kh2: e.activation(
                        out=s, in_=bq[:, 0:C], func=A.Square, accum_out=rqk[:, 2 * kh2:2 * kh2 + 1]),
                        reads=[], writes=[sq2, rqk])
                    m1 = b.mark("scalar")
                    free_bank(bq_i, "ac", m1)
                    b.wait("scalar", "pe", pk)
                    b.op("scalar", lambda e, bk=bk, kh2=kh2: e.copy(kt_sb[kh2][:], bk[:, 0:C]),
                         writes=[kt_sb[kh2]])
                    b.op("scalar", lambda e, bk=bk, s=sqslice, kh2=kh2: e.activation(
                        out=s, in_=bk[:, 0:C], func=A.Square, accum_out=rqk[:, 2 * kh2 + 1:2 * kh2 + 2]),
                        reads=[], writes=[sq2, rqk])
                    m2 = b.mark("scalar")
                    free_bank(bk_i, "ac", m2)
                mqk = b.cnt["ac"]
                b.wait("vector", "ac", mqk)
                b.op("vector", lambda e: e.tensor_scalar(out=rqk[:], in0=rqk[:], scalar1=1.0 / DK,
                                                         scalar2=EPS_RMS, op0=OP.mult, op1=OP.add),
                     reads=[rqk], writes=[rqk])
                mrr = b.mark("vector")
                b.wait("scalar", "dv", mrr)
                b.op("scalar", lambda e: e.activation(out=rqk[:], in_=rqk[:], func=A.Ln),
                     reads=[rqk], writes=[rqk])
                mln = b.mark("scalar")
                b.wait("vector", "ac", mln)
                b.op("vector", lambda e: e.tensor_scalar(
                    out=rqk[:, 0:1], in0=rqk[:, 0:1], scalar1=-0.5, scalar2=-LN128C,
                    op0=OP.mult, op1=OP.add), reads=[rqk], writes=[rqk])
                b.op("vector", lambda e: e.tensor_scalar(
                    out=rqk[:, 1:2], in0=rqk[:, 1:2], scalar1=-0.5, scalar2=-LN128C / 2.0,
                    op0=OP.mult, op1=OP.add), reads=[rqk], writes=[rqk])
                b.op("vector", lambda e: e.tensor_scalar(
                    out=rqk[:, 2:3], in0=rqk[:, 2:3], scalar1=-0.5, scalar2=-LN128C,
                    op0=OP.mult, op1=OP.add), reads=[rqk], writes=[rqk])
                b.op("vector", lambda e: e.tensor_scalar(
                    out=rqk[:, 3:4], in0=rqk[:, 3:4], scalar1=-0.5, scalar2=-LN128C / 2.0,
                    op0=OP.mult, op1=OP.add), reads=[rqk], writes=[rqk])
                maff = b.mark("vector")
                b.wait("scalar", "dv", maff)
                b.op("scalar", lambda e: e.activation(out=rqk[:], in_=rqk[:], func=A.Exp),
                     reads=[rqk], writes=[rqk])
                msq4 = b.mark("scalar")
                b.wait("vector", "ac", msq4)
                for kh2 in range(2):
                    b.op("vector", lambda e, kh2=kh2: e.tensor_scalar(
                        out=qt_sb[kh2][:], in0=qt_sb[kh2][:], scalar1=rqk[:, 2 * kh2:2 * kh2 + 1],
                        scalar2=None, op0=OP.mult, op1=OP.bypass),
                        reads=[qt_sb[kh2], rqk], writes=[qt_sb[kh2]])
                    b.op("vector", lambda e, kh2=kh2: e.tensor_scalar(
                        out=ktn[kh2][:], in0=kt_sb[kh2][:], scalar1=rqk[:, 2 * kh2 + 1:2 * kh2 + 2],
                        scalar2=None, op0=OP.mult, op1=OP.bypass),
                        reads=[kt_sb[kh2], rqk], writes=[ktn[kh2]])
                mnorm = b.mark("vector")
                b.wait("tensor", "dv", mnorm)
                backb = {}
                for kh2 in range(2):
                    bq2_i, bq2 = acquire_bank()
                    b.op("tensor", lambda e, bq2=bq2, kh2=kh2: e.transpose(
                        bq2[:, 0:C], qt_sb[kh2][:], ident[:]))
                    pq2 = pemark()
                    bk2_i, bk2 = acquire_bank()
                    b.op("tensor", lambda e, bk2=bk2, kh2=kh2: e.transpose(
                        bk2[:, 0:C], ktn[kh2][:], ident[:]))
                    pk2 = pemark()
                    backb[kh2] = (bq2_i, bq2, pq2, bk2_i, bk2, pk2)
                for kh2 in range(2):
                    bq2_i, bq2, pq2, bk2_i, bk2, pk2 = backb[kh2]
                    b.wait("scalar", "pe", pq2)
                    b.op("scalar", lambda e, bq2=bq2, kh2=kh2: e.copy(Qf[kh2][:], bq2[:, 0:C]),
                         writes=[Qf[kh2]])
                    mq = b.mark("scalar")
                    free_bank(bq2_i, "ac", mq)
                    b.wait("scalar", "pe", pk2)
                    b.op("scalar", lambda e, bk2=bk2, kh2=kh2: e.copy(Kf[kh2][:], bk2[:, 0:C]),
                         writes=[Kf[kh2]])
                    mk = b.mark("scalar")
                    free_bank(bk2_i, "ac", mk)
                mqkf = b.cnt["ac"]
                SLK = [(0, 0), (1, 0), (2, 1), (3, 1)]
                gcols = {sl: gct[:, cc * 32 + 4 * g + sl:cc * 32 + 4 * g + sl + 1] for sl in range(4)}
                bcols = {sl: bett[:, cc * 32 + 4 * g + sl:cc * 32 + 4 * g + sl + 1] for sl in range(4)}
                b.wait("vector", "ac", gates_ac)
                b.wait("vector", "dv", gates_dv)
                for sl, kh2 in SLK:
                    b.op("vector", lambda e, sl=sl, gc_=gcols[sl]: e.tensor_scalar(
                        out=e_t[sl][:], in0=ones_all[:], scalar1=gc_, scalar2=None,
                        op0=OP.mult, op1=OP.bypass),
                        reads=[ones_all, gct], writes=[e_t[sl]])
                mgtb = b.mark("vector")
                for sl, kh2 in SLK:
                    b.op("vector", lambda e, sl=sl, bc_=bcols[sl]: e.tensor_scalar(
                        out=curT[sl][:], in0=ones_all[:], scalar1=bc_, scalar2=None,
                        op0=OP.mult, op1=OP.bypass),
                        reads=[ones_all, bett], writes=[curT[sl]])
                mbtb = b.mark("vector")
                b.wait("tensor", "dv", mgtb)
                bcb = {}
                for sl, kh2 in SLK:
                    bBC_i, bBC = acquire_bank()
                    b.op("tensor", lambda e, sl=sl, bBC=bBC: e.transpose(
                        bBC[:, 0:C], e_t[sl][:], ident[:]))
                    bcb[sl] = (bBC_i, bBC, pemark())
                for sl, kh2 in SLK:
                    bBC_i, bBC, pm = bcb[sl]
                    b.wait("vector", "pe", pm)
                    b.op("vector", lambda e, sl=sl, bBC=bBC, gc_=gcols[sl]: e.tensor_scalar(
                        out=e_t[sl][:], in0=bBC[:, 0:C], scalar1=gc_, scalar2=0.0,
                        op0=OP.subtract, op1=OP.min),
                        reads=[gct], writes=[e_t[sl]])
                    me = b.mark("vector")
                    b.wait("scalar", "dv", me)
                    b.op("scalar", lambda e, sl=sl: e.activation(out=Dfull[sl][:], in_=e_t[sl][:],
                                                                 func=A.Exp),
                         reads=[e_t[sl]], writes=[Dfull[sl]])
                    b.op("scalar", lambda e, sl=sl, bBC=bBC: e.activation(out=Dbc[sl][:],
                                                                          in_=bBC[:, 0:C], func=A.Exp),
                         writes=[Dbc[sl]])
                    mexp = b.mark("scalar")
                    free_bank(bBC_i, "ac", mexp)
                    bcb[sl] = mexp
                for sl, kh2 in SLK:
                    b.wait("gpsimd", "ac", bcb[sl])
                    b.op("gpsimd", lambda e, sl=sl: e.affine_select(
                        Dincl[sl][:], Dfull[sl][:], pattern=[[1, 128]], base=0,
                        channel_multiplier=-1, compare_op=OP.is_ge, fill=0.0),
                        reads=[Dfull[sl]], writes=[Dincl[sl]])
                mdincl = b.mark("gpsimd")
                b.wait("tensor", "dv", mbtb)
                bbb = {}
                for sl, kh2 in SLK:
                    bBB_i, bBB = acquire_bank()
                    b.op("tensor", lambda e, sl=sl, bBB=bBB: e.transpose(
                        bBB[:, 0:C], curT[sl][:], ident[:]))
                    bbb[sl] = (bBB_i, bBB, pemark())
                for sl, kh2 in SLK:
                    bBB_i, bBB, pmb = bbb[sl]
                    b.wait("vector", "pe", prior_fin_sl[sl])
                    vraw = qkvc[:, (4 * kh2 + 2 + (sl % 2)) * T + ccol:
                                (4 * kh2 + 2 + (sl % 2)) * T + ccol + C]
                    b.wait("vector", "pe", pmb)
                    b.wait("vector", "ac", bcb[sl])
                    b.wait("vector", "ac", conv_marks[("silu", 4 * kh2 + 2 + (sl % 2))])
                    b.op("vector", lambda e, sl=sl, bBB=bBB: e.tensor_tensor(
                        out=curB[sl][:], in0=Dfull[sl][:], in1=bBB[:, 0:C], op=OP.mult),
                        reads=[Dfull[sl]], writes=[curB[sl]])
                    b.op("vector", lambda e, sl=sl, bBB=bBB: e.tensor_tensor(
                        out=KDf[sl][:], in0=Dbc[sl][:], in1=bBB[:, 0:C], op=OP.mult),
                        reads=[Dbc[sl]], writes=[KDf[sl]])
                    b.op("vector", lambda e, sl=sl, bBB=bBB, vraw=vraw: e.tensor_tensor(
                        out=Bfv[sl][:], in0=vraw, in1=bBB[:, 0:C], op=OP.mult),
                        reads=[qkvc], writes=[Bfv[sl]])
                    mvv = b.mark("vector")
                    free_bank(bBB_i, "dv", mvv)
                mbp = b.cnt["dv"]
                b.wait("tensor", "dv", mbp)
                sc = {}
                for kh2 in range(2):
                    bG_i, bG = acquire_bank()
                    b.op("tensor", lambda e, bG=bG, kh2=kh2: e.matmul(
                        bG[:, 0:C], Kf[kh2][:], Kf[kh2][:], start=True, stop=True))
                    bP_i, bP = acquire_bank()
                    b.op("tensor", lambda e, bP=bP, kh2=kh2: e.matmul(
                        bP[:, 0:C], Kf[kh2][:], Qf[kh2][:], start=True, stop=True))
                    sc[kh2] = (bG_i, bG, bP_i, bP, pemark())
                b.wait("vector", "gp", mdincl)
                mprep_sl = [0, 0, 0, 0]
                for sl, kh2 in SLK:
                    bG_i, bG, bP_i, bP, pm_sc = sc[kh2]
                    b.wait("vector", "pe", pm_sc)
                    b.op("vector", lambda e, sl=sl, bP=bP: e.tensor_tensor(
                        out=Mt[sl][:], in0=bP[:, 0:C], in1=Dincl[sl][:], op=OP.mult),
                        reads=[Dincl[sl]], writes=[Mt[sl]])
                    b.op("vector", lambda e, sl=sl, bG=bG: e.tensor_tensor(
                        out=curT[sl][:], in0=bG[:, 0:C], in1=curB[sl][:], op=OP.mult),
                        reads=[curB[sl], curT[sl]], writes=[curT[sl]])
                    b.op("vector", lambda e, sl=sl: e.tensor_tensor(
                        out=curA[sl][:], in0=curT[sl][:], in1=mS[:], op=OP.mult),
                        reads=[curT[sl], mS], writes=[curA[sl]])
                    b.op("vector", lambda e, sl=sl, kh2=kh2: e.tensor_tensor(
                        out=KDf[sl][:], in0=Kf[kh2][:], in1=KDf[sl][:], op=OP.mult),
                        reads=[Kf[kh2], KDf[sl]], writes=[KDf[sl]])
                    b.op("vector", lambda e, sl=sl, kh2=kh2: e.tensor_tensor(
                        out=QDf[sl][:], in0=Qf[kh2][:], in1=Dbc[sl][:], op=OP.mult),
                        reads=[Qf[kh2], Dbc[sl]], writes=[QDf[sl]])
                    mprep_sl[sl] = b.mark("vector")
                mprep = mprep_sl[3]
                for kh2 in range(2):
                    bG_i, bG, bP_i, bP, _ = sc[kh2]
                    free_bank(bG_i, "dv", mprep_sl[2 * kh2 + 1])
                    free_bank(bP_i, "dv", mprep_sl[2 * kh2 + 1])
                b.wait("tensor", "dv", mprep)
                w0b = {}
                for sl, kh2 in SLK:
                    vh = 4 * g + sl
                    b.wait("tensor", "dv", st_upd_sl[sl])
                    bW_i, bW = acquire_bank()
                    b.op("tensor", lambda e, bW=bW, vh=vh, sl=sl: e.matmul(
                        bW[:, 0:C], state[:, vh * 128:(vh + 1) * 128], KDf[sl][:],
                        start=True, stop=True))
                    w0b[sl] = (bW_i, bW, pemark())
                for sl, kh2 in SLK:
                    bW_i, bW, pmw = w0b[sl]
                    b.wait("vector", "pe", pmw)
                    b.op("vector", lambda e, sl=sl, bW=bW: e.tensor_tensor(
                        out=Bfv[sl][:], in0=Bfv[sl][:], in1=bW[:, 0:C], op=OP.subtract),
                        reads=[Bfv[sl]], writes=[Bfv[sl]])
                    mv2 = b.mark("vector")
                    free_bank(bW_i, "dv", mv2)
                mbf = b.cnt["dv"]
                ut_evac = [0, 0, 0, 0]
                b.wait("tensor", "dv", mbf)
                btb_ = {}
                for sl, kh2 in SLK:
                    bT_i, bT = acquire_bank()
                    b.op("tensor", lambda e, bT=bT, sl=sl: e.transpose(bT[:, 0:C], Bfv[sl][:],
                                                                       ident[:]))
                    btb_[sl] = (bT_i, bT, pemark())
                for sl, kh2 in SLK:
                    bT_i, bT, pmt = btb_[sl]
                    b.wait("scalar", "pe", pmt)
                    b.op("scalar", lambda e, sl=sl, bT=bT: e.copy(Ut[sl][:], bT[:, 0:C]),
                         writes=[Ut[sl]])
                    ut_evac[sl] = b.mark("scalar")
                    free_bank(bT_i, "ac", ut_evac[sl])
                for sl, kh2 in SLK:
                    b.op("vector", lambda e, sl=sl, kh2=kh2: e.tensor_scalar(
                        out=Kddt[sl][:], in0=ktn[kh2][:], scalar1=Dincl[sl][:, C - 1:C],
                        scalar2=None, op0=OP.mult, op1=OP.bypass),
                        reads=[ktn[kh2], Dincl[sl]], writes=[Kddt[sl]])
                b.mark("vector")

                cur = [curA[i] for i in range(4)]
                nxt = [curB[i] for i in range(4)]
                uadd = [0, 0, 0, 0]
                sqcp = [0, 0, 0, 0]
                for L in range(7):
                    app_banks = []
                    for sl in range(4):
                        bA_i, bA = acquire_bank()
                        if L > 0:
                            b.wait("tensor", "dv", uadd[sl])
                            b.wait("tensor", "ac", sqcp[sl])
                        else:
                            b.wait("tensor", "dv", mprep_sl[sl])
                            b.wait("tensor", "ac", ut_evac[sl])
                        b.op("tensor", lambda e, bA=bA, sl=sl, cur=cur: e.matmul(
                            bA[:, 0:C], cur[sl][:], Ut[sl][:], start=True, stop=True))
                        app_banks.append((bA_i, bA, pemark()))
                    for sl in range(4):
                        bA_i, bA, pm = app_banks[sl]
                        b.wait("vector", "pe", pm)
                        b.op("vector", lambda e, sl=sl, bA=bA: e.tensor_tensor(
                            out=Ut[sl][:], in0=Ut[sl][:], in1=bA[:, 0:C], op=OP.add),
                            reads=[Ut[sl]], writes=[Ut[sl]])
                        uadd[sl] = b.mark("vector")
                        free_bank(bA_i, "dv", uadd[sl])
                    if L < 6:
                        tr_banks = []
                        for sl in range(4):
                            bR_i, bR = acquire_bank()
                            b.op("tensor", lambda e, bR=bR, sl=sl, cur=cur: e.transpose(
                                bR[:, 0:C], cur[sl][:], ident[:]))
                            tr_banks.append((bR_i, bR, pemark()))
                        tr_marks = []
                        for sl in range(4):
                            bR_i, bR, pm = tr_banks[sl]
                            b.wait("scalar", "pe", pm)
                            b.op("scalar", lambda e, sl=sl, bR=bR: e.copy(curT[sl][:], bR[:, 0:C]),
                                 writes=[curT[sl]])
                            mm_ = b.mark("scalar")
                            tr_marks.append(mm_)
                            free_bank(bR_i, "ac", mm_)
                        sq_banks = []
                        for sl in range(4):
                            bS_i, bS = acquire_bank()
                            b.wait("tensor", "ac", tr_marks[sl])
                            b.op("tensor", lambda e, sl=sl, bS=bS, cur=cur: e.matmul(
                                bS[:, 0:C], curT[sl][:], cur[sl][:], start=True, stop=True))
                            sq_banks.append((bS_i, bS, pemark()))
                        for sl in range(4):
                            bS_i, bS, pm = sq_banks[sl]
                            b.wait("scalar", "pe", pm)
                            b.op("scalar", lambda e, sl=sl, bS=bS, nxt=nxt: e.copy(nxt[sl][:],
                                                                                   bS[:, 0:C]),
                                 writes=[nxt[sl]])
                            sqcp[sl] = b.mark("scalar")
                            free_bank(bS_i, "ac", sqcp[sl])
                        cur, nxt = nxt, cur
                for half in range(2):
                    fb = {}
                    for sl in (2 * half, 2 * half + 1):
                        vh = 4 * g + sl
                        b.wait("tensor", "dv", uadd[sl])
                        bW_i, bW = acquire_bank()
                        b.op("tensor", lambda e, sl=sl, bW=bW: e.matmul(
                            bW[:, 0:C], Kddt[sl][:], Ut[sl][:], start=True, stop=True))
                        pm_ds = pemark()
                        bO_i, bO = acquire_bank()
                        b.op("tensor", lambda e, sl=sl, bO=bO: e.matmul(
                            bO[:, 0:C], Ut[sl][:], Mt[sl][:], start=True, stop=False))
                        b.op("tensor", lambda e, sl=sl, vh=vh, bO=bO: e.matmul(
                            bO[:, 0:C], state[:, vh * 128:(vh + 1) * 128], QDf[sl][:],
                            start=False, stop=True))
                        fb[sl] = (bW_i, bW, bO_i, bO, pemark())
                        prior_finish_pe[0] = fb[sl][4]
                        prior_fin_sl[sl] = fb[sl][4]
                    for sl in (2 * half, 2 * half + 1):
                        vh = 4 * g + sl
                        bW_i, bW, bO_i, bO, pm_o = fb[sl]
                        b.wait("vector", "pe", pm_o)
                        b.op("vector", lambda e, sl=sl, vh=vh, bW=bW: e.scalar_tensor_tensor(
                            out=state[:, vh * 128:(vh + 1) * 128],
                            in0=state[:, vh * 128:(vh + 1) * 128],
                            scalar=Dbc[sl][:, C - 1:C], in1=bW[:, 0:C], op0=OP.mult, op1=OP.add),
                            reads=[state, Dbc[sl]], writes=[state])
                        smu = b.mark("vector")
                        st_upd_last[0] = smu
                        st_upd_sl[sl] = smu
                        free_bank(bW_i, "dv", smu)
                        b.wait("scalar", "pe", pm_o)
                        b.op("scalar", lambda e, sl=sl, cc=cc, bO=bO: e.copy(
                            o_buf[:, sl * T + cc * C: sl * T + cc * C + C], bO[:, 0:C]),
                            writes=[o_buf])
                        o_done[sl] = b.mark("scalar")
                        free_bank(bO_i, "ac", o_done[sl])
                emit_zcol(cc)
            scan_ac = b.cnt["ac"]
            scan_dv = b.cnt["dv"]

            gw = dict(gate_dv=[], rowbuf_pe=[0], wout_evac=[], wout_pe=[])

            def make_gate(half, g=g, z_evac=z_evac, scan_ac=scan_ac, gw=gw, o_done=tuple(o_done)):
                gate_dv = gw["gate_dv"]
                rowbuf_pe = gw["rowbuf_pe"]
                for _one in [half]:
                        sls = (2 * half, 2 * half + 1)
                        rowb = {}
                        for sl in sls:
                            osl = o_buf[:, sl * T:(sl + 1) * T]
                            sqX = sq1 if sl % 2 == 0 else sq2
                            b.wait("scalar", "ac", o_done[sl])
                            b.op("scalar", lambda e, osl=osl, sqX=sqX: e.activation(
                                out=sqX[:], in_=osl, func=A.Square),
                                reads=[o_buf], writes=[sqX])
                            msq = b.mark("scalar")
                            brow_i, brow = acquire_bank()
                            b.wait("tensor", "ac", msq)
                            b.op("tensor", lambda e, brow=brow, sqX=sqX: e.matmul(
                                brow[0:1, 0:T], onesP[:], sqX[:], start=True, stop=True))
                            rowb[sl] = (brow_i, brow, pemark())
                        bcb2 = {}
                        for sl in sls:
                            brow_i, brow, pm_row = rowb[sl]
                            b.wait("scalar", "pe", pm_row)
                            b.wait("scalar", "pe", rowbuf_pe[0])
                            b.op("scalar", lambda e, brow=brow: e.copy(rowbuf[:], brow[0:1, 0:T]),
                                 writes=[rowbuf])
                            mrb = b.mark("scalar")
                            free_bank(brow_i, "ac", mrb)
                            bb_i, bb = acquire_bank()
                            b.wait("tensor", "ac", mrb)
                            b.op("tensor", lambda e, bb=bb: e.matmul(bb[:, 0:T], ones_all[0:1, :], rowbuf[:],
                                                                     start=True, stop=True))
                            bcb2[sl] = (bb_i, bb, pemark())
                            rowbuf_pe[0] = bcb2[sl][2]
                        for sl in sls:
                            bb_i, bb, pm_bb = bcb2[sl]
                            b.wait("vector", "pe", pm_bb)
                            b.op("vector", lambda e, bb=bb: e.tensor_scalar(
                                out=bb[:, 0:T], in0=bb[:, 0:T], scalar1=1.0 / DV, scalar2=EPS_RMS,
                                op0=OP.mult, op1=OP.add), reads=[bb], writes=[bb])
                        mrec = b.mark("vector")
                        b.wait("scalar", "dv", mrec)
                        for sl in sls:
                            bb_i, bb, _ = bcb2[sl]
                            b.op("scalar", lambda e, bb=bb: e.activation(out=bb[:, 0:T], in_=bb[:, 0:T],
                                                                         func=A.Ln), reads=[bb], writes=[bb])
                        mln2 = b.mark("scalar")
                        b.wait("vector", "ac", mln2)
                        for sl in sls:
                            bb_i, bb, _ = bcb2[sl]
                            b.op("vector", lambda e, bb=bb: e.tensor_scalar(
                                out=bb[:, 0:T], in0=bb[:, 0:T], scalar1=-0.5, scalar2=None,
                                op0=OP.mult, op1=OP.bypass), reads=[bb], writes=[bb])
                        maf2 = b.mark("vector")
                        b.wait("scalar", "dv", maf2)
                        for sl in sls:
                            bb_i, bb, _ = bcb2[sl]
                            zsl = u_buf[:, sl * 520:sl * 520 + 512]
                            b.op("scalar", lambda e, bb=bb: e.activation(out=bb[:, 0:T], in_=bb[:, 0:T],
                                                                         func=A.Exp), reads=[bb], writes=[bb])
                            b.wait("scalar", "ac", z_evac[sl])
                            b.op("scalar", lambda e, zsl=zsl: e.activation(out=zsl, in_=zsl, func=A.Silu),
                                 reads=[u_buf], writes=[u_buf])
                        msil = b.mark("scalar")
                        b.wait("vector", "ac", msil)
                        b.wait("vector", "mdma", ld_misc)
                        for sl in sls:
                            bb_i, bb, _ = bcb2[sl]
                            osl = o_buf[:, sl * T:(sl + 1) * T]
                            zsl = u_buf[:, sl * 520:sl * 520 + 512]
                            b.op("vector", lambda e, osl=osl, bb=bb: e.scalar_tensor_tensor(
                                out=osl, in0=osl, scalar=nw[:], in1=bb[:, 0:T], op0=OP.mult, op1=OP.mult),
                                reads=[o_buf, nw], writes=[o_buf])
                            mg1 = b.mark("vector")
                            free_bank(bb_i, "dv", mg1)
                            b.op("vector", lambda e, osl=osl, zsl=zsl: e.tensor_tensor(
                                out=osl, in0=osl, in1=zsl, op=OP.mult),
                                reads=[o_buf, u_buf], writes=[o_buf])
                            gv = g16[sl]
                            b.op("vector", lambda e, osl=osl, d=gv["h0"]: e.tensor_copy(
                                d[:, 0:256], osl[:, 0:256]), reads=[o_buf], writes=[Mt[sl]])
                            b.op("vector", lambda e, osl=osl, d=gv["h1"]: e.tensor_copy(
                                d[:, 0:256], osl[:, 256:512]), reads=[o_buf], writes=[QDf[sl]])
                            b.op("vector", lambda e, osl=osl, h=gv["h0"], d=gv["l0"]: e.tensor_tensor(
                                out=d[:, 0:256], in0=osl[:, 0:256], in1=h[:, 0:256], op=OP.subtract),
                                reads=[o_buf, Mt[sl]], writes=[Ut[sl]])
                            b.op("vector", lambda e, osl=osl, h=gv["h1"], d=gv["l1"]: e.tensor_tensor(
                                out=d[:, 0:256], in0=osl[:, 256:512], in1=h[:, 0:256], op=OP.subtract),
                                reads=[o_buf, QDf[sl]], writes=[curA[sl]])
                            gate_dv.append(b.mark("vector"))


            def make_wout(j, g=g, gw=gw):
                gate_dv = gw["gate_dv"]
                wout_evac = gw["wout_evac"]
                wout_pe = gw["wout_pe"]
                for _one in [j]:
                        wap, wv = wo_consume()
                        pb = j % 2
                        bank = P0 if pb == 0 else P1
                        b.wait("tensor", wv[0], wv[1])
                        b.wait("tensor", "dv", gate_dv[-1])
                        if pbank_free[pb] is not None:
                            b.wait("tensor", pbank_free[pb][0], pbank_free[pb][1])
                        whap, wlap = wap
                        for half in range(2):
                            nmm = 0
                            for i in range(4):
                                wh = whap[:, i * 128:(i + 1) * 128]
                                wl = wlap[:, i * 128:(i + 1) * 128]
                                mh = (g16[i]["h0"] if half == 0 else g16[i]["h1"])[:, 0:256]
                                ml = (g16[i]["l0"] if half == 0 else g16[i]["l1"])[:, 0:256]
                                hsl = slice(half * 256, half * 256 + 256)
                                b.op("tensor", lambda e, wh=wh, mh=mh, bank=bank, hsl=hsl, first=(nmm == 0):
                                     e.matmul(bank[:, hsl], wh, mh, start=first, stop=False))
                                nmm += 1
                                b.op("tensor", lambda e, wh=wh, ml=ml, bank=bank, hsl=hsl: e.matmul(
                                    bank[:, hsl], wh, ml, start=False, stop=False))
                                nmm += 1
                                b.op("tensor", lambda e, wl=wl, mh=mh, bank=bank, hsl=hsl,
                                     last=(i == 3): e.matmul(bank[:, hsl], wl, mh, start=False, stop=last))
                                nmm += 1
                        mm_done = pemark()
                        wout_pe.append(mm_done)
                        wo_release("pe", mm_done)
                        if g == 0:
                            b.wait("scalar", "pe", mm_done)
                            b.op("scalar", lambda e, j=j, bank=bank: e.copy(
                                attn[:, j * T:(j + 1) * T], bank[:, 0:T]), writes=[attn])
                            mk = ("ac", b.mark("scalar"))
                        else:
                            b.wait("vector", "pe", mm_done)
                            b.op("vector", lambda e, j=j, bank=bank: e.tensor_tensor(
                                out=attn[:, j * T:(j + 1) * T], in0=attn[:, j * T:(j + 1) * T],
                                in1=bank[:, 0:T], op=OP.add),
                                reads=[attn], writes=[attn])
                            mk = ("dv", b.mark("vector"))
                        wout_evac.append(mk)
                        pbank_free[pb] = mk


            make_gate(0)
            make_gate(1)
            pending = [(lambda j=j: make_wout(j)) for j in range(16)]
            last_gw = gw
        for f in pending:
            f()
        pending = []
        deps2 = [[last_gw["wout_evac"][j], ("pe", last_gw["wout_pe"][j])] for j in range(16)]
        st2 = ln_stats(attn, tile_deps=deps2)
        ln_apply(attn, x_buf, st2, residual=True)
        phase_barrier()
        st3 = ln_stats(x_buf)
        lnx_ready = ln_apply(x_buf, None, st3, split=(lnx16h, lnx16l))
        lnx_tm = list(ln_apply.tile_marks)
        ffn_evac = []
        for j in range(16):
            wap, wv = wq_consume()
            pb = j % 2
            bank = P0 if pb == 0 else P1
            b.wait("tensor", wv[0], wv[1])
            if pbank_free[pb] is not None:
                b.wait("tensor", pbank_free[pb][0], pbank_free[pb][1])
            whap, wlap = wap
            for i in range(NI):
                b.wait("tensor", "dv", lnx_tm[i])
                wh = whap[:, i * 128:(i + 1) * 128]
                wl = wlap[:, i * 128:(i + 1) * 128]
                mh = lnx16h[:, i * T:(i + 1) * T]
                ml = lnx16l[:, i * T:(i + 1) * T]
                b.op("tensor", lambda e, wh=wh, mh=mh, bank=bank, i=i: e.matmul(
                    bank[:, 0:T], wh, mh, start=(i == 0), stop=False))
                b.op("tensor", lambda e, wh=wh, ml=ml, bank=bank: e.matmul(
                    bank[:, 0:T], wh, ml, start=False, stop=False))
                b.op("tensor", lambda e, wl=wl, mh=mh, bank=bank, i=i: e.matmul(
                    bank[:, 0:T], wl, mh, start=False, stop=(i == NI - 1)))
            mm_done = pemark()
            wq_release("pe", mm_done)
            b.wait("scalar", "pe", mm_done)
            b.op("scalar", lambda e, j=j, bank=bank: e.activation(
                out=attn[:, j * T:(j + 1) * T], in_=bank[:, 0:T], func=A.Identity,
                scale=ffs[:, j:j + 1], bias=ffb[:, j:j + 1]),
                reads=[ffs, ffb], writes=[attn])
            ffn_evac.append(b.mark("scalar"))
            pbank_free[pb] = ("ac", ffn_evac[-1])
        deps4 = [[("ac", ffn_evac[j])] for j in range(16)]
        st4 = ln_stats(attn, tile_deps=deps4)
        res4 = ln_apply(attn, x_buf, st4, residual=True)
        res4_tm = list(ln_apply.tile_marks)
        for i in range(NI):
            b.wait("gpsimd", "dv", res4_tm[i])
            b.op("gpsimd", lambda e, i=i, col0=col0: e.dma_start(
                out=out_d[i * 128:(i + 1) * 128, col0:col0 + T], in_=x_buf[:, i * T:(i + 1) * T]))
            b.mark("gpsimd", "sdma", 16)
        x_store_mark = b.cnt["sdma"]

    b.wait("gpsimd", "sdma", b.cnt["sdma"])

    with nc.Block() as block:
        @block.sync
        def _(e):
            b.replay("sync", e)

        @block.tensor
        def _(e):
            b.replay("tensor", e)

        @block.vector
        def _(e):
            b.replay("vector", e)

        @block.scalar
        def _(e):
            b.replay("scalar", e)

        @block.gpsimd
        def _(e):
            b.replay("gpsimd", e)

    return nc


def host_prep_weights(W_qkvz, W_ba, conv_w, dt_bias, A_log, norm_w, W_out,
                      ffn_w, ffn_scale, ffn_bias):
    order = []
    for g in range(8):
        for kh2 in range(2):
            kh = 2 * g + kh2
            order += [6 * kh + 0, 6 * kh + 1, 6 * kh + 2, 6 * kh + 3]
        order += [6 * (2 * g) + 4, 6 * (2 * g) + 5, 6 * (2 * g + 1) + 4, 6 * (2 * g + 1) + 5]
    wq4 = W_qkvz.reshape(16, 128, 96, 128).transpose(2, 1, 0, 3)
    wq_dv = np.ascontiguousarray(wq4[order]).astype(np.float32)

    wo4 = W_out.reshape(8, 4, 128, 16, 128)
    wo_dv = np.ascontiguousarray(wo4.transpose(0, 3, 2, 1, 4)).astype(np.float32)

    wf_dv = np.ascontiguousarray(
        ffn_w.T.reshape(16, 128, 16, 128).transpose(2, 1, 0, 3)).astype(np.float32)

    perm = []
    for vh in range(32):
        perm.append((vh // 2) * 4 + vh % 2)
    for vh in range(32):
        perm.append((vh // 2) * 4 + 2 + vh % 2)
    wba_dv = np.ascontiguousarray(
        W_ba[:, perm].reshape(16, 128, 64).transpose(1, 0, 2)).astype(np.float32)

    cw_order = []
    for g in range(8):
        cw_order += [2 * g, 16 + 2 * g, 32 + 4 * g, 33 + 4 * g,
                     2 * g + 1, 16 + 2 * g + 1, 34 + 4 * g, 35 + 4 * g]
    cw_dv = np.ascontiguousarray(
        conv_w.reshape(64, 128, 4)[cw_order].transpose(1, 0, 2)).astype(np.float32)

    return dict(
        wq_d=wq_dv, wo_d=wo_dv, wf_d=wf_dv, wba_d=wba_dv, cw_d=cw_dv,
        alog_d=np.ascontiguousarray(A_log.reshape(32, 1)).astype(np.float32),
        dtb_d=np.ascontiguousarray(dt_bias.reshape(32, 1)).astype(np.float32),
        nw_d=np.ascontiguousarray(norm_w.reshape(128, 1)).astype(np.float32),
        fs_d=np.ascontiguousarray(ffn_scale.reshape(16, 128).T).astype(np.float32),
        fb_d=np.ascontiguousarray(ffn_bias.reshape(16, 128).T).astype(np.float32),
    )


from concourse.bass_utils import run_bass_kernel_spmd

_NC_CACHE = {}


def kernel(L0_attn_x, L0_attn_W_qkvz, L0_attn_W_ba, L0_attn_conv_w, L0_attn_dt_bias,
           L0_attn_A_log, L0_attn_norm_w, L0_attn_W_out, L0_ffn_weight,
           L0_ffn_scale_weight, L0_ffn_bias):
    wd = host_prep_weights(
        np.asarray(L0_attn_W_qkvz, dtype=np.float32),
        np.asarray(L0_attn_W_ba, dtype=np.float32),
        np.asarray(L0_attn_conv_w, dtype=np.float32),
        np.asarray(L0_attn_dt_bias, dtype=np.float32),
        np.asarray(L0_attn_A_log, dtype=np.float32),
        np.asarray(L0_attn_norm_w, dtype=np.float32),
        np.asarray(L0_attn_W_out, dtype=np.float32),
        np.asarray(L0_ffn_weight, dtype=np.float32),
        np.asarray(L0_ffn_scale_weight, dtype=np.float32),
        np.asarray(L0_ffn_bias, dtype=np.float32))
    x = np.asarray(L0_attn_x, dtype=np.float32)
    n_cores = 8
    in_maps = []
    for i in range(n_cores):
        m = dict(wd)
        m["x_in"] = np.ascontiguousarray(x[i].T)
        in_maps.append(m)
    nc = build(NT=8)
    res = run_bass_kernel_spmd(nc, in_maps, core_ids=list(range(n_cores)))
    out = np.stack([np.asarray(r["out_d"]).T for r in res.results], axis=0)
    return np.ascontiguousarray(out.astype(np.float32))


# version 17
# speedup vs baseline: 1.2167x; 1.2167x over previous
import numpy as np
import concourse.bass as bass
import concourse.mybir as mybir

F32 = mybir.dt.float32
BF16 = mybir.dt.bfloat16
A = mybir.ActivationFunctionType
OP = mybir.AluOpType

H = 2048
SEQ = 4096
T = 512
C = 128
NCC = T // C
NK, NV, DK, DV = 16, 32, 128, 128
NG = 8
NI = H // 128
EPS_LN = 1e-5
EPS_RMS = 1e-6
LN128C = 4.852030263919617

ENGINES = ("tensor", "vector", "scalar", "gpsimd", "sync")
ENG_SEM = {"tensor": "pe", "vector": "dv", "scalar": "ac", "gpsimd": "gp"}


class Builder:
    def __init__(self, nc):
        self.nc = nc
        self.prog = {e: [] for e in ENGINES}
        self.sems = {}
        self.cnt = {}
        self.dirty_w = {e: set() for e in ENGINES}
        self.dirty_r = {e: set() for e in ENGINES}
        self.waited = {e: {} for e in ENGINES}

    def sem(self, name):
        self.sems[name] = self.nc.alloc_semaphore(name)
        self.cnt[name] = 0

    @staticmethod
    def _nm(x):
        if x is None:
            return None
        t = getattr(x, "tensor", x)
        return getattr(t, "name", str(t))

    def op(self, eng, fn, reads=(), writes=()):
        r = {self._nm(a) for a in reads} - {None}
        w = {self._nm(a) for a in writes} - {None}
        if (r & self.dirty_w[eng]) or (w & (self.dirty_w[eng] | self.dirty_r[eng])):
            self.prog[eng].append(dict(fn=lambda e: e.drain(), incs=[]))
            self.dirty_w[eng] = set()
            self.dirty_r[eng] = set()
        self.dirty_r[eng] |= r
        self.dirty_w[eng] |= w
        self.prog[eng].append(dict(fn=fn, incs=[]))

    def mark(self, eng, sname=None, n=1):
        sname = sname or ENG_SEM[eng]
        item = self.prog[eng][-1]
        for (s0, _n0) in item["incs"]:
            if s0 == sname:
                return self.cnt[sname]
        assert not item["incs"], "one sem update per instruction"
        item["incs"].append((sname, n))
        self.cnt[sname] += n
        return self.cnt[sname]

    def wait(self, eng, sname, value):
        if value is None or value <= 0:
            return
        if self.waited[eng].get(sname, -1) >= value:
            return
        self.waited[eng][sname] = value
        h = self.sems[sname]
        self.prog[eng].append(dict(fn=lambda e, h=h, v=value: e.wait_ge(h, v), incs=[]))

    def replay(self, eng, e):
        for item in self.prog[eng]:
            inst = item["fn"](e)
            for (sname, n) in item["incs"]:
                inst.then_inc(self.sems[sname], n)


def build(NT=8):
    nc = bass.Bass("TRN2", target_bir_lowering=False)

    x_in = nc.dram_tensor("x_in", [H, SEQ], F32, kind="ExternalInput").ap()
    wq_d = nc.dram_tensor("wq_d", [96, 128, NI, 128], F32, kind="ExternalInput").ap()
    wo_d = nc.dram_tensor("wo_d", [NG, 16, 128, 4, 128], F32, kind="ExternalInput").ap()
    wf_d = nc.dram_tensor("wf_d", [16, 128, NI, 128], F32, kind="ExternalInput").ap()
    wba_d = nc.dram_tensor("wba_d", [128, NI, 64], F32, kind="ExternalInput").ap()
    cw_d = nc.dram_tensor("cw_d", [128, 64, 4], F32, kind="ExternalInput").ap()
    alog_d = nc.dram_tensor("alog_d", [32, 1], F32, kind="ExternalInput").ap()
    dtb_d = nc.dram_tensor("dtb_d", [32, 1], F32, kind="ExternalInput").ap()
    nw_d = nc.dram_tensor("nw_d", [128, 1], F32, kind="ExternalInput").ap()
    fs_d = nc.dram_tensor("fs_d", [128, 16], F32, kind="ExternalInput").ap()
    fb_d = nc.dram_tensor("fb_d", [128, 16], F32, kind="ExternalInput").ap()
    out_d = nc.dram_tensor("out_d", [H, SEQ], F32, kind="ExternalOutput").ap()
    wq16h_d = nc.dram_tensor("wq16h_d", [112, 128, NI, 128], BF16, kind="Internal").ap()
    wq16l_d = nc.dram_tensor("wq16l_d", [112, 128, NI, 128], BF16, kind="Internal").ap()
    wo16h_d = nc.dram_tensor("wo16h_d", [NG, 16, 128, 4, 128], BF16, kind="Internal").ap()
    wo16l_d = nc.dram_tensor("wo16l_d", [NG, 16, 128, 4, 128], BF16, kind="Internal").ap()

    sb = lambda n, s: nc.alloc_sbuf_tensor(n, s, F32).ap()
    x_buf = sb("x_buf", [128, NI * T])
    lnx16h = nc.alloc_sbuf_tensor("lnx16h", [128, NI * T], BF16).ap()
    lnx16l = nc.alloc_sbuf_tensor("lnx16l", [128, NI * T], BF16).ap()
    qkvc = sb("qkvc", [128, 8 * T])
    attn_h = nc.alloc_sbuf_tensor("attn", [128, NI * T], F32)
    attn = attn_h.ap()
    attn16 = attn_h.bitcast(BF16).ap()
    o_buf = sb("o_buf", [128, 4 * T])
    u_buf_h = nc.alloc_sbuf_tensor("u_buf", [128, 4 * 520], F32)
    u_buf = u_buf_h.ap()
    u16 = u_buf_h.bitcast(BF16).ap()
    state = sb("state", [128, 32 * 128])
    carry = sb("carry", [128, 64 * 3])
    wq16h_sb = nc.alloc_sbuf_tensor("wq16h_sb", [128, 2 * NI * 128], BF16).ap()
    wq16l_sb = nc.alloc_sbuf_tensor("wq16l_sb", [128, 2 * NI * 128], BF16).ap()
    wo16h_sb = nc.alloc_sbuf_tensor("wo16h_sb", [128, 2 * 4 * 128], BF16).ap()
    wo16l_sb = nc.alloc_sbuf_tensor("wo16l_sb", [128, 2 * 4 * 128], BF16).ap()
    wba16h = nc.alloc_sbuf_tensor("wba16h", [128, NI * 64], BF16).ap()
    wba16l = nc.alloc_sbuf_tensor("wba16l", [128, NI * 64], BF16).ap()
    cw_sb = sb("cw_sb", [128, 64 * 4])
    mS = sb("mS", [128, 128])
    ident = sb("ident", [128, 128])
    onesP = sb("onesP", [128, 1])
    ones_all = sb("ones_all", [128, 128])
    beta32 = sb("beta32", [32, T])
    sp32 = sb("sp32", [32, T])
    g32 = sp32
    gc32 = sb("gc32", [32, T])
    a32 = gc32[0:32, 0:T]
    gct = sb("gct", [128, NCC * 32])
    bett = sb("bett", [128, NCC * 32])
    qt_sb = [sb("qt_sb0", [128, 128]), sb("qt_sb1", [128, 128])]
    kt_sb = [sb("kt_sb0", [128, 128]), sb("kt_sb1", [128, 128])]
    ktn = [sb("ktn0", [128, 128]), sb("ktn1", [128, 128])]
    Qf = qt_sb
    Kf = kt_sb
    rqk = sb("rqk", [128, 4])
    NVH = 4
    e_t = [sb(f"e_t{i}", [128, 128]) for i in range(NVH)]
    Dfull = [sb(f"Dfull{i}", [128, 128]) for i in range(NVH)]
    Dincl = [sb(f"Dincl{i}", [128, 128]) for i in range(NVH)]
    Dbc = [sb(f"Dbc{i}", [128, 128]) for i in range(NVH)]
    Mt = [sb(f"Mt{i}", [128, 128]) for i in range(NVH)]
    curA = [sb(f"curA{i}", [128, 128]) for i in range(NVH)]
    curB = [sb(f"curB{i}", [128, 128]) for i in range(NVH)]
    curT = [sb(f"curT{i}", [128, 128]) for i in range(NVH)]
    QDf = [sb(f"QDf{i}", [128, 128]) for i in range(NVH)]
    Ut = [sb(f"Ut{i}", [128, 128]) for i in range(NVH)]
    Bfv = e_t
    Kddt = Dfull
    KDf = Ut
    g16 = [dict(
        h0=Mt[i].tensor.bitcast(BF16).ap(),
        h1=QDf[i].tensor.bitcast(BF16).ap(),
        l0=Ut[i].tensor.bitcast(BF16).ap(),
        l1=curA[i].tensor.bitcast(BF16).ap(),
    ) for i in range(NVH)]
    sq1 = sb("sq1", [128, T])
    sq2 = sb("sq2", [128, T])
    convacc_ = [sq1, sq2]
    bag = sq1
    sqsc = sq2[:, 0:128]
    rowbuf = sb("rowbuf", [1, T])
    negA = sb("negA", [32, 1])
    tmpA = sb("tmpA", [32, 1])
    dtb = sb("dtb", [32, 1])
    nw = sb("nw", [128, 1])
    ffs = sb("ffs", [128, 16])
    ffb = sb("ffb", [128, 16])

    P0 = nc.alloc_psum_tensor("P0", [128, 512], F32).ap()
    P1 = nc.alloc_psum_tensor("P1", [128, 512], F32).ap()
    POOLB = [nc.alloc_psum_tensor(f"PB{i}", [128, 512], F32).ap() for i in range(6)]

    b = Builder(nc)
    for s in ("wq0", "wq1", "wo0", "wo1", "mdma", "adma", "xdma", "sdma",
              "pin0", "pin1", "pout0", "pout1", "pe", "dv", "ac", "gp"):
        b.sem(s)

    pbank_free = [None, None]
    bank_state = [dict(free=[], busy=False) for _ in POOLB]
    bank_rr = [0]

    def acquire_bank(writer_eng="tensor"):
        for _ in range(len(POOLB)):
            i = bank_rr[0] % len(POOLB)
            bank_rr[0] += 1
            if not bank_state[i]["busy"]:
                break
        else:
            raise RuntimeError("psum pool exhausted: all banks busy")
        bank_state[i]["busy"] = True
        for (sname, v) in bank_state[i]["free"]:
            b.wait(writer_eng, sname, v)
        bank_state[i]["free"] = []
        return i, POOLB[i]

    def free_bank(i, sname, v):
        bank_state[i]["free"].append((sname, v))
        bank_state[i]["busy"] = False

    def pemark():
        return b.mark("tensor", "pe")

    def phase_barrier(engines=("tensor", "vector", "scalar")):
        snap = {"pe": b.cnt["pe"], "dv": b.cnt["dv"], "ac": b.cnt["ac"]}
        for e_ in engines:
            for s_, v_ in snap.items():
                if ENG_SEM[e_] != s_:
                    b.wait(e_, s_, v_)

    wq_stream = []
    for tc in range(NT):
        for ci in range(112):
            wq_stream.append((wq16h_d[ci], wq16l_d[ci]))
    wo_stream = []
    for tc in range(NT):
        for g in range(NG):
            for j in range(16):
                wo_stream.append((wo16h_d[g, j], wo16l_d[g, j]))

    wq_ptr, wq_used, wq_done, wq_freed = [0], [0], {}, [None, None]
    wo_ptr, wo_used, wo_done, wo_freed = [0], [0], {}, [None, None]

    def _dma_next(ptr, stream, freed, buf, width, done, sembase):
        i = ptr[0]
        if i >= len(stream):
            return
        slot = i % 2
        if freed[slot] is not None:
            b.wait("sync", freed[slot][0], freed[slot][1])
        src = stream[i]
        if isinstance(src, tuple):
            hbuf, lbuf = buf
            dh = hbuf[:, slot * width:(slot + 1) * width]
            dl = lbuf[:, slot * width:(slot + 1) * width]
            b.op("sync", lambda e, d=dh, s=src[0]: e.dma_start(out=d, in_=s))
            b.mark("sync", sembase + str(slot), 16)
            b.op("sync", lambda e, d=dl, s=src[1]: e.dma_start(out=d, in_=s))
            done[i] = (sembase + str(slot), b.mark("sync", sembase + str(slot), 16))
        else:
            dst = buf[:, slot * width:(slot + 1) * width]
            b.op("sync", lambda e, d=dst, s=src: e.dma_start(out=d, in_=s))
            done[i] = (sembase + str(slot), b.mark("sync", sembase + str(slot), 16))
        ptr[0] += 1

    def wq_consume():
        idx = wq_used[0]
        while wq_ptr[0] <= idx + 1 and wq_ptr[0] < len(wq_stream):
            _dma_next(wq_ptr, wq_stream, wq_freed, (wq16h_sb, wq16l_sb), NI * 128, wq_done, "wq")
        slot = idx % 2
        return (wq16h_sb[:, slot * NI * 128:(slot + 1) * NI * 128],
                wq16l_sb[:, slot * NI * 128:(slot + 1) * NI * 128]), wq_done[idx]

    def wq_release(sname, v):
        wq_freed[wq_used[0] % 2] = (sname, v)
        wq_used[0] += 1

    def wo_consume():
        idx = wo_used[0]
        while wo_ptr[0] <= idx + 1 and wo_ptr[0] < len(wo_stream):
            _dma_next(wo_ptr, wo_stream, wo_freed, (wo16h_sb, wo16l_sb), 512, wo_done, "wo")
        slot = idx % 2
        return (wo16h_sb[:, slot * 512:(slot + 1) * 512],
                wo16l_sb[:, slot * 512:(slot + 1) * 512]), wo_done[idx]

    def wo_release(sname, v):
        wo_freed[wo_used[0] % 2] = (sname, v)
        wo_used[0] += 1

    b.op("gpsimd", lambda e: e.memset(curA[0][:], -1.0), writes=[curA[0]])
    b.op("gpsimd", lambda e: e.affine_select(mS[:], curA[0][:], pattern=[[1, 128]], base=0,
                                             channel_multiplier=-1, compare_op=OP.is_gt, fill=0.0),
         reads=[curA[0]], writes=[mS])
    b.op("gpsimd", lambda e: e.memset(curA[0][:], 1.0), writes=[curA[0]])
    b.op("gpsimd", lambda e: e.affine_select(ident[:], curA[0][:], pattern=[[1, 128]], base=0,
                                             channel_multiplier=-1, compare_op=OP.is_equal, fill=0.0),
         reads=[curA[0]], writes=[ident])
    b.op("gpsimd", lambda e: e.memset(onesP[:], 1.0), writes=[onesP])
    b.op("gpsimd", lambda e: e.memset(ones_all[:], 1.0), writes=[ones_all])
    b.op("gpsimd", lambda e: e.memset(state[:], 0.0), writes=[state])
    b.op("gpsimd", lambda e: e.memset(carry[:], 0.0), writes=[carry])
    gp_init = b.mark("gpsimd")
    dv_init = 0

    for dst, src in [(cw_sb[:], cw_d), (tmpA[:], alog_d), (dtb[:], dtb_d),
                     (nw[:], nw_d), (ffs[:], fs_d), (ffb[:], fb_d)]:
        b.op("gpsimd", lambda e, d=dst, s=src: e.dma_start(out=d, in_=s))
        b.mark("gpsimd", "mdma", 16)
    ld_misc = b.cnt["mdma"]
    b.wait("scalar", "mdma", ld_misc)
    b.op("scalar", lambda e: e.activation(out=negA[:], in_=tmpA[:], func=A.Exp),
         reads=[tmpA], writes=[negA])
    ac_negA = b.mark("scalar")
    b.wait("vector", "ac", ac_negA)
    b.op("vector", lambda e: e.tensor_scalar(out=negA[:], in0=negA[:], scalar1=-1.0, scalar2=None,
                                             op0=OP.mult, op1=OP.bypass),
         reads=[negA], writes=[negA])
    dv_negA = b.mark("vector")
    b.wait("tensor", "gp", gp_init)
    b.wait("tensor", "dv", dv_init)
    b.wait("scalar", "gp", gp_init)

    prol_dv = [0, 0]
    pin_cnt = [0, 0]
    pout_cnt = [0, 0]

    def prol_batch(cols, width):
        n = len(cols)
        slot = prol_batch.it % 2
        prol_batch.it += 1
        xbase = slot * 4096
        for k, (s, _h, _l) in enumerate(cols):
            if k == 0 and prol_dv[slot]:
                b.wait("sync", "dv", prol_dv[slot])
            b.op("sync", lambda e, d=x_buf[:, xbase + k * width: xbase + (k + 1) * width],
                 s=s: e.dma_start(out=d, in_=s))
            pin_cnt[slot] += 16
            b.mark("sync", "pin" + str(slot), 16)
        b.wait("vector", "pin" + str(slot), pin_cnt[slot])
        b.wait("vector", "pout" + str(slot), pout_cnt[slot])
        tot = n * width
        xs = x_buf[:, xbase:xbase + tot]
        hs = attn16[:, slot * 4096:slot * 4096 + tot]
        ls = attn16[:, 8192 + slot * 4096:8192 + slot * 4096 + tot]
        b.op("vector", lambda e, hs=hs, xs=xs: e.tensor_copy(hs, xs),
             reads=[x_buf], writes=[attn])
        b.op("vector", lambda e, ls=ls, xs=xs, hs=hs: e.tensor_tensor(
            out=ls, in0=xs, in1=hs, op=OP.subtract),
            reads=[x_buf, attn], writes=[attn])
        prol_dv[slot] = b.mark("vector")
        b.wait("sync", "dv", prol_dv[slot])
        for k, (_s, h, l) in enumerate(cols):
            b.op("sync", lambda e, d=h, s=attn16[:, slot * 4096 + k * width:
                                                  slot * 4096 + (k + 1) * width]: e.dma_start(out=d, in_=s))
            b.mark("sync", "pout" + str(slot), 16)
            b.op("sync", lambda e, d=l, s=attn16[:, 8192 + slot * 4096 + k * width:
                                                  8192 + slot * 4096 + (k + 1) * width]: e.dma_start(out=d, in_=s))
            b.mark("sync", "pout" + str(slot), 16)
            pout_cnt[slot] += 32
    prol_batch.it = 0

    for c0 in range(0, 112, 2):
        cols = []
        for ci in range(c0, min(c0 + 2, 112)):
            srcap = wq_d[ci] if ci < 96 else wf_d[ci - 96]
            cols.append((srcap, wq16h_d[ci], wq16l_d[ci]))
        prol_batch(cols, 2048)
    for gi0 in range(0, NG * 16, 8):
        cols = []
        for gi in range(gi0, gi0 + 8):
            g_, j_ = gi // 16, gi % 16
            cols.append((wo_d[g_, j_], wo16h_d[g_, j_], wo16l_d[g_, j_]))
        prol_batch(cols, 512)
    xsl = x_buf[:, 0:NI * 64]
    b.wait("sync", "dv", prol_dv[0])
    b.wait("sync", "dv", prol_dv[1])
    b.op("sync", lambda e, d=xsl: e.dma_start(out=d, in_=wba_d))
    pin_cnt[0] += 16
    b.mark("sync", "pin0", 16)
    b.wait("vector", "pin0", pin_cnt[0])
    b.op("vector", lambda e, x=xsl: e.tensor_copy(wba16h[:], x), reads=[x_buf], writes=[wba16h])
    b.op("vector", lambda e, x=xsl: e.tensor_tensor(out=wba16l[:], in0=x, in1=wba16h[:],
                                                    op=OP.subtract),
         reads=[x_buf, wba16h], writes=[wba16l])
    wba_ready = b.mark("vector")
    b.wait("sync", "pout0", pout_cnt[0])
    b.wait("sync", "pout1", pout_cnt[1])
    b.wait("sync", "dv", wba_ready)
    prologue_done = b.cnt["dv"]

    def ln_stats(src_buf, tile_deps=None):
        bx_i, bx = acquire_bank()
        bq_i, bq = acquire_bank()
        sq_marks = [None, None]
        for i in range(NI):
            sq = sq1 if i % 2 == 0 else sq2
            if tile_deps is not None:
                for (sn, sv) in tile_deps[i]:
                    b.wait("scalar", sn, sv)
                    b.wait("tensor", sn, sv)
            if sq_marks[i % 2] is not None:
                b.wait("scalar", "pe", sq_marks[i % 2])
            b.op("scalar", lambda e, s=sq, i=i: e.activation(
                out=s[:], in_=src_buf[:, i * T:(i + 1) * T], func=A.Square),
                reads=[src_buf], writes=[sq])
            acm = b.mark("scalar")
            b.op("tensor", lambda e, i=i: e.matmul(
                bx[0:1, 0:T], onesP[:], src_buf[:, i * T:(i + 1) * T],
                start=(i == 0), stop=(i == NI - 1)))
            b.wait("tensor", "ac", acm)
            b.op("tensor", lambda e, s=sq, i=i: e.matmul(
                bq[0:1, 0:T], onesP[:], s[:], start=(i == 0), stop=(i == NI - 1)))
            sq_marks[i % 2] = pemark()
        pe_done = b.cnt["pe"]
        b.wait("scalar", "pe", pe_done)
        b.op("scalar", lambda e: e.copy(rowbuf[:], bx[0:1, 0:T]), writes=[rowbuf])
        m1 = b.mark("scalar")
        free_bank(bx_i, "ac", m1)
        bX_i, bX = acquire_bank()
        b.wait("tensor", "ac", m1)
        b.op("tensor", lambda e: e.matmul(bX[:, 0:T], ones_all[0:1, :], rowbuf[:],
                                          start=True, stop=True))
        pmX = pemark()
        b.wait("scalar", "pe", pmX)
        b.op("scalar", lambda e: e.copy(rowbuf[:], bq[0:1, 0:T]), writes=[rowbuf])
        m2 = b.mark("scalar")
        free_bank(bq_i, "ac", m2)
        bQ_i, bQ = acquire_bank()
        b.wait("tensor", "ac", m2)
        b.op("tensor", lambda e: e.matmul(bQ[:, 0:T], ones_all[0:1, :], rowbuf[:],
                                          start=True, stop=True))
        pmQ = pemark()
        b.wait("vector", "pe", pmQ)
        b.op("vector", lambda e: e.tensor_scalar(out=sq1[:], in0=bX[:, 0:T], scalar1=1.0 / H,
                                                 scalar2=None, op0=OP.mult, op1=OP.bypass),
             writes=[sq1])
        mX = b.mark("vector")
        free_bank(bX_i, "dv", mX)
        b.op("vector", lambda e: e.tensor_tensor(out=sq2[:], in0=sq1[:], in1=sq1[:], op=OP.mult),
             reads=[sq1], writes=[sq2])
        b.op("vector", lambda e: e.scalar_tensor_tensor(
            out=bQ[:, 0:T], in0=bQ[:, 0:T], scalar=1.0 / H, in1=sq2[:],
            op0=OP.mult, op1=OP.subtract),
            reads=[sq2, bQ], writes=[bQ])
        b.op("vector", lambda e: e.tensor_scalar(out=bQ[:, 0:T], in0=bQ[:, 0:T], scalar1=EPS_LN,
                                                 scalar2=None, op0=OP.add, op1=OP.bypass),
             reads=[bQ], writes=[bQ])
        mv = b.mark("vector")
        b.wait("scalar", "dv", mv)
        b.op("scalar", lambda e: e.activation(out=bQ[:, 0:T], in_=bQ[:, 0:T], func=A.Ln),
             reads=[bQ], writes=[bQ])
        mln3 = b.mark("scalar")
        b.wait("vector", "ac", mln3)
        b.op("vector", lambda e: e.tensor_scalar(out=bQ[:, 0:T], in0=bQ[:, 0:T], scalar1=-0.5,
                                                 scalar2=None, op0=OP.mult, op1=OP.bypass),
             reads=[bQ], writes=[bQ])
        mv2 = b.mark("vector")
        b.wait("scalar", "dv", mv2)
        b.op("scalar", lambda e: e.activation(out=bQ[:, 0:T], in_=bQ[:, 0:T], func=A.Exp),
             reads=[bQ], writes=[bQ])
        mac = b.mark("scalar")
        return bQ_i, bQ, mac

    def ln_apply(src_buf, dst_buf, stats, residual=False, split=None):
        ln_apply.tile_marks = []
        bQ_i, bQ, mac = stats
        b.wait("vector", "ac", mac)
        for i in range(NI):
            b.op("vector", lambda e, i=i: e.tensor_tensor(
                out=sq2[:], in0=src_buf[:, i * T:(i + 1) * T], in1=sq1[:], op=OP.subtract),
                reads=[src_buf, sq1], writes=[sq2])
            if split is not None:
                hi, lo = split
                b.op("vector", lambda e, i=i: e.tensor_tensor(
                    out=sq2[:], in0=sq2[:], in1=bQ[:, 0:T], op=OP.mult),
                    reads=[sq2], writes=[sq2])
                b.op("vector", lambda e, i=i, hi=hi: e.tensor_copy(
                    hi[:, i * T:(i + 1) * T], sq2[:]),
                    reads=[sq2], writes=[hi])
                b.op("vector", lambda e, i=i, hi=hi, lo=lo: e.tensor_tensor(
                    out=lo[:, i * T:(i + 1) * T], in0=sq2[:], in1=hi[:, i * T:(i + 1) * T],
                    op=OP.subtract),
                    reads=[sq2, hi], writes=[lo])
                ln_apply.tile_marks.append(b.mark("vector"))
            elif residual:
                b.op("vector", lambda e, i=i: e.tensor_tensor(
                    out=sq2[:], in0=sq2[:], in1=bQ[:, 0:T], op=OP.mult),
                    reads=[sq2], writes=[sq2])
                b.op("vector", lambda e, i=i: e.tensor_tensor(
                    out=dst_buf[:, i * T:(i + 1) * T], in0=dst_buf[:, i * T:(i + 1) * T],
                    in1=sq2[:], op=OP.add),
                    reads=[dst_buf, sq2], writes=[dst_buf])
                ln_apply.tile_marks.append(b.mark("vector"))
            else:
                b.op("vector", lambda e, i=i: e.tensor_tensor(
                    out=dst_buf[:, i * T:(i + 1) * T], in0=sq2[:], in1=bQ[:, 0:T], op=OP.mult),
                    reads=[sq2], writes=[dst_buf])
        dv1 = b.mark("vector")
        free_bank(bQ_i, "dv", dv1)
        return dv1


    x_store_mark = 0

    mprep_sl = [0, 0, 0, 0]
    for tc in range(NT):
        col0 = tc * T
        if tc == 0:
            b.wait("gpsimd", "dv", prologue_done)
        b.wait("gpsimd", "sdma", x_store_mark)
        for i in range(NI):
            b.op("gpsimd", lambda e, i=i, col0=col0: e.dma_start(
                out=x_buf[:, i * T:(i + 1) * T], in_=x_in[i * 128:(i + 1) * 128, col0:col0 + T]))
            b.mark("gpsimd", "xdma", 16)
        x_loaded = b.cnt["xdma"]
        for e_ in ("tensor", "scalar", "vector"):
            b.wait(e_, "xdma", x_loaded)

        st1 = ln_stats(x_buf)
        lnx_ready = ln_apply(x_buf, None, st1, split=(lnx16h, lnx16l))
        lnx_tm = list(ln_apply.tile_marks)

        bba_i, bba = acquire_bank()
        b.wait("tensor", "dv", wba_ready)
        for i in range(NI):
            b.wait("tensor", "dv", lnx_tm[i])
            wh = wba16h[:, i * 64:(i + 1) * 64]
            wl = wba16l[:, i * 64:(i + 1) * 64]
            mh = lnx16h[:, i * T:(i + 1) * T]
            ml = lnx16l[:, i * T:(i + 1) * T]
            b.op("tensor", lambda e, wh=wh, mh=mh, bba=bba, i=i: e.matmul(
                bba[0:64, 0:T], wh, mh, start=(i == 0), stop=False))
            b.op("tensor", lambda e, wh=wh, ml=ml, bba=bba: e.matmul(
                bba[0:64, 0:T], wh, ml, start=False, stop=False))
            b.op("tensor", lambda e, wl=wl, mh=mh, bba=bba, i=i: e.matmul(
                bba[0:64, 0:T], wl, mh, start=False, stop=(i == NI - 1)))
        ba_done = pemark()
        b.wait("scalar", "pe", ba_done)
        b.op("scalar", lambda e, bba=bba: e.activation(out=beta32[:], in_=bba[0:32, 0:T], func=A.Sigmoid),
             writes=[beta32])
        b.op("scalar", lambda e, bba=bba: e.copy(bag[32:64, 0:T], bba[32:64, 0:T]), writes=[bag])
        ba_ac = b.mark("scalar")
        free_bank(bba_i, "ac", ba_ac)
        b.wait("gpsimd", "ac", ba_ac)
        b.op("gpsimd", lambda e: e.dma_start(out=a32, in_=bag[32:64, 0:T]), reads=[bag])
        b.mark("gpsimd", "adma", 16)
        a_shift = b.cnt["adma"]
        b.wait("scalar", "adma", a_shift)
        b.op("scalar", lambda e: e.activation(out=sp32[:], in_=a32, func=A.Exp,
                                              bias=dtb[:], scale=1.0),
             reads=[gc32, dtb], writes=[sp32])
        b.op("scalar", lambda e: e.activation(out=sp32[:], in_=sp32[:], func=A.Ln,
                                              bias=1.0, scale=1.0),
             reads=[sp32], writes=[sp32])
        sp_ac = b.mark("scalar")
        b.wait("vector", "ac", sp_ac)
        b.wait("vector", "dv", dv_negA)
        b.op("vector", lambda e: e.tensor_scalar(out=g32[:], in0=sp32[:], scalar1=negA[:], scalar2=None,
                                                 op0=OP.mult, op1=OP.bypass),
             reads=[sp32, negA], writes=[g32])
        for cc in range(NCC):
            b.op("vector", lambda e, cc=cc: e.tensor_tensor_scan(
                out=gc32[:, cc * C:(cc + 1) * C], data0=g32[:, cc * C:(cc + 1) * C],
                data1=g32[:, cc * C:(cc + 1) * C], initial=0.0, op0=OP.add, op1=OP.bypass),
                reads=[g32], writes=[gc32])
        gates_dv = b.mark("vector")
        b.wait("tensor", "dv", gates_dv)
        for cc in range(NCC):
            for (srcb, dstb) in ((gc32, gct), (beta32, bett)):
                bt_i, bt = acquire_bank()
                b.op("tensor", lambda e, cc=cc, srcb=srcb, bt=bt: e.transpose(
                    bt[:, 0:32], srcb[0:32, cc * C:(cc + 1) * C], ident[0:32, 0:32]))
                pm = pemark()
                b.wait("scalar", "pe", pm)
                b.op("scalar", lambda e, cc=cc, dstb=dstb, bt=bt: e.copy(
                    dstb[:, cc * 32:(cc + 1) * 32], bt[:, 0:32]),
                    writes=[dstb])
                free_bank(bt_i, "ac", b.mark("scalar"))
        gates_ac = b.cnt["ac"]
        b.wait("vector", "ac", gates_ac)

        pending = []
        last_gw = None
        for g in range(NG):
            conv_marks = {}
            fill_iter = iter(pending)
            prep_msq4 = {}

            def emit_qkprep(cc, conv_marks=conv_marks, prep_msq4=prep_msq4):
                ccol = cc * C
                raw = {}
                for kh2 in range(2):
                    raw[kh2] = (qkvc[:, (4 * kh2) * T + ccol: (4 * kh2) * T + ccol + C],
                                qkvc[:, (4 * kh2 + 1) * T + ccol: (4 * kh2 + 1) * T + ccol + C])
                tbanks = {}
                for kh2 in range(2):
                    b.wait("tensor", "ac", conv_marks[("silu", 4 * kh2)])
                    b.wait("tensor", "ac", conv_marks[("silu", 4 * kh2 + 1)])
                    bq_i, bq = acquire_bank()
                    b.op("tensor", lambda e, a=raw[kh2][0], bq=bq: e.transpose(bq[:, 0:C], a, ident[:]))
                    pq = pemark()
                    bk_i, bk = acquire_bank()
                    b.op("tensor", lambda e, a=raw[kh2][1], bk=bk: e.transpose(bk[:, 0:C], a, ident[:]))
                    pk = pemark()
                    tbanks[kh2] = (bq_i, bq, pq, bk_i, bk, pk)
                for kh2 in range(2):
                    bq_i, bq, pq, bk_i, bk, pk = tbanks[kh2]
                    sqslice = sq2[:, kh2 * 128:(kh2 + 1) * 128]
                    b.wait("scalar", "pe", pq)
                    b.wait("scalar", "dv", mprep_sl[2 * kh2 + 1])
                    b.op("scalar", lambda e, bq=bq, kh2=kh2: e.copy(qt_sb[kh2][:], bq[:, 0:C]),
                         writes=[qt_sb[kh2]])
                    b.op("scalar", lambda e, bq=bq, s=sqslice, kh2=kh2: e.activation(
                        out=s, in_=bq[:, 0:C], func=A.Square, accum_out=rqk[:, 2 * kh2:2 * kh2 + 1]),
                        reads=[], writes=[sq2, rqk])
                    m1 = b.mark("scalar")
                    free_bank(bq_i, "ac", m1)
                    b.wait("scalar", "pe", pk)
                    b.op("scalar", lambda e, bk=bk, kh2=kh2: e.copy(kt_sb[kh2][:], bk[:, 0:C]),
                         writes=[kt_sb[kh2]])
                    b.op("scalar", lambda e, bk=bk, s=sqslice, kh2=kh2: e.activation(
                        out=s, in_=bk[:, 0:C], func=A.Square, accum_out=rqk[:, 2 * kh2 + 1:2 * kh2 + 2]),
                        reads=[], writes=[sq2, rqk])
                    m2 = b.mark("scalar")
                    free_bank(bk_i, "ac", m2)
                mqk = b.cnt["ac"]
                b.wait("vector", "ac", mqk)
                b.op("vector", lambda e: e.tensor_scalar(out=rqk[:], in0=rqk[:], scalar1=1.0 / DK,
                                                         scalar2=EPS_RMS, op0=OP.mult, op1=OP.add),
                     reads=[rqk], writes=[rqk])
                mrr = b.mark("vector")
                b.wait("scalar", "dv", mrr)
                b.op("scalar", lambda e: e.activation(out=rqk[:], in_=rqk[:], func=A.Ln),
                     reads=[rqk], writes=[rqk])
                mln = b.mark("scalar")
                b.wait("vector", "ac", mln)
                b.op("vector", lambda e: e.tensor_scalar(
                    out=rqk[:, 0:1], in0=rqk[:, 0:1], scalar1=-0.5, scalar2=-LN128C,
                    op0=OP.mult, op1=OP.add), reads=[rqk], writes=[rqk])
                b.op("vector", lambda e: e.tensor_scalar(
                    out=rqk[:, 1:2], in0=rqk[:, 1:2], scalar1=-0.5, scalar2=-LN128C / 2.0,
                    op0=OP.mult, op1=OP.add), reads=[rqk], writes=[rqk])
                b.op("vector", lambda e: e.tensor_scalar(
                    out=rqk[:, 2:3], in0=rqk[:, 2:3], scalar1=-0.5, scalar2=-LN128C,
                    op0=OP.mult, op1=OP.add), reads=[rqk], writes=[rqk])
                b.op("vector", lambda e: e.tensor_scalar(
                    out=rqk[:, 3:4], in0=rqk[:, 3:4], scalar1=-0.5, scalar2=-LN128C / 2.0,
                    op0=OP.mult, op1=OP.add), reads=[rqk], writes=[rqk])
                maff = b.mark("vector")
                b.wait("scalar", "dv", maff)
                b.op("scalar", lambda e: e.activation(out=rqk[:], in_=rqk[:], func=A.Exp),
                     reads=[rqk], writes=[rqk])
                prep_msq4[cc] = b.mark("scalar")

            for jj in range(8):
                if jj == 6:
                    emit_qkprep(0)
                wap, wv = wq_consume()
                pb = jj % 2
                bank = P0 if pb == 0 else P1
                b.wait("tensor", wv[0], wv[1])
                if pbank_free[pb] is not None:
                    b.wait("tensor", pbank_free[pb][0], pbank_free[pb][1])
                whap, wlap = wap
                for i in range(NI):
                    b.wait("tensor", "dv", lnx_tm[i])
                    wh = whap[:, i * 128:(i + 1) * 128]
                    wl = wlap[:, i * 128:(i + 1) * 128]
                    mh = lnx16h[:, i * T:(i + 1) * T]
                    ml = lnx16l[:, i * T:(i + 1) * T]
                    b.op("tensor", lambda e, wh=wh, mh=mh, bank=bank, i=i: e.matmul(
                        bank[:, 0:T], wh, mh, start=(i == 0), stop=False))
                    b.op("tensor", lambda e, wh=wh, ml=ml, bank=bank: e.matmul(
                        bank[:, 0:T], wh, ml, start=False, stop=False))
                    b.op("tensor", lambda e, wl=wl, mh=mh, bank=bank, i=i: e.matmul(
                        bank[:, 0:T], wl, mh, start=False, stop=(i == NI - 1)))
                mm_done = pemark()
                wq_release("pe", mm_done)
                slot = jj % 2
                rawap = u_buf[:, slot * 520: slot * 520 + 520]
                jj_g = 8 * g + jj
                b.wait("scalar", "pe", mm_done)
                if jj >= 2:
                    b.wait("scalar", conv_marks[("ceng", jj - 2)], conv_marks[("conv", jj - 2)])
                b.op("scalar", lambda e, rawap=rawap, bank=bank: e.copy(rawap[:, 3:515], bank[:, 0:T]),
                     writes=[u_buf])
                evac_m = b.mark("scalar")
                conv_marks[("evac", jj)] = evac_m
                pbank_free[pb] = ("ac", evac_m)
                ceng = "vector"
                cs = ENG_SEM[ceng]
                b.wait(ceng, "ac", evac_m)
                if jj >= 2:
                    b.wait(ceng, conv_marks[("ceng", jj - 2)], conv_marks[("conv", jj - 2)])
                acc = convacc_[0] if ceng == "vector" else convacc_[1]
                b.op(ceng, lambda e, rawap=rawap, jj_g=jj_g: e.tensor_copy(
                    rawap[:, 0:3], carry[:, jj_g * 3:jj_g * 3 + 3]),
                    reads=[carry], writes=[u_buf])
                b.op(ceng, lambda e, rawap=rawap, jj_g=jj_g, acc=acc: e.tensor_scalar(
                    out=acc[:], in0=rawap[:, 0:T], scalar1=cw_sb[:, jj_g * 4:jj_g * 4 + 1],
                    scalar2=None, op0=OP.mult, op1=OP.bypass),
                    reads=[u_buf, cw_sb], writes=[acc])
                for tau in range(1, 4):
                    b.op(ceng, lambda e, rawap=rawap, jj_g=jj_g, acc=acc, tau=tau: e.scalar_tensor_tensor(
                        out=acc[:], in0=rawap[:, tau:tau + T],
                        scalar=cw_sb[:, jj_g * 4 + tau:jj_g * 4 + tau + 1],
                        in1=acc[:], op0=OP.mult, op1=OP.add),
                        reads=[u_buf, cw_sb, acc], writes=[acc])
                b.op(ceng, lambda e, rawap=rawap, jj_g=jj_g: e.tensor_copy(
                    carry[:, jj_g * 3:jj_g * 3 + 3], rawap[:, 512:515]),
                    reads=[u_buf], writes=[carry])
                cm = b.mark(ceng, cs)
                conv_marks[("conv", jj)] = cm
                conv_marks[("ceng", jj)] = cs
                b.wait("scalar", cs, cm)
                b.op("scalar", lambda e, jj=jj, acc=acc: e.activation(
                    out=qkvc[:, jj * T:(jj + 1) * T], in_=acc[:], func=A.Silu),
                    reads=[acc], writes=[qkvc])
                conv_marks[("silu", jj)] = b.mark("scalar")
                nfill = 1 if jj < 2 else 3
                for _ in range(nfill):
                    f = next(fill_iter, None)
                    if f is not None:
                        f()
            for f in fill_iter:
                f()
            pending = []
            qkv_ready = b.cnt["ac"]

            st_upd_last = [0]
            st_upd_sl = [0, 0, 0, 0]
            prior_finish_pe = [0]
            prior_fin_sl = [0, 0, 0, 0]
            o_done = [0, 0, 0, 0]
            z_evac = [None, None, None, None]

            def emit_zcol(jz):
                wap, wv = wq_consume()
                pb = jz % 2
                bank = P0 if pb == 0 else P1
                b.wait("tensor", wv[0], wv[1])
                if jz < 2:
                    b.wait("tensor", "ac", qkv_ready)
                if pbank_free[pb] is not None:
                    b.wait("tensor", pbank_free[pb][0], pbank_free[pb][1])
                whap, wlap = wap
                for i in range(NI):
                    b.wait("tensor", "dv", lnx_tm[i])
                    wh = whap[:, i * 128:(i + 1) * 128]
                    wl = wlap[:, i * 128:(i + 1) * 128]
                    mh = lnx16h[:, i * T:(i + 1) * T]
                    ml = lnx16l[:, i * T:(i + 1) * T]
                    b.op("tensor", lambda e, wh=wh, mh=mh, bank=bank, i=i: e.matmul(
                        bank[:, 0:T], wh, mh, start=(i == 0), stop=False))
                    b.op("tensor", lambda e, wh=wh, ml=ml, bank=bank: e.matmul(
                        bank[:, 0:T], wh, ml, start=False, stop=False))
                    b.op("tensor", lambda e, wl=wl, mh=mh, bank=bank, i=i: e.matmul(
                        bank[:, 0:T], wl, mh, start=False, stop=(i == NI - 1)))
                mm_done = pemark()
                wq_release("pe", mm_done)
                b.wait("scalar", "pe", mm_done)
                if jz < 2:
                    for jprev in (6, 7):
                        b.wait("scalar", conv_marks[("ceng", jprev)], conv_marks[("conv", jprev)])
                b.op("scalar", lambda e, jz=jz, bank=bank: e.copy(
                    u_buf[:, jz * 520:jz * 520 + 512], bank[:, 0:T]),
                    writes=[u_buf])
                z_evac[jz] = b.mark("scalar")
                pbank_free[pb] = ("ac", z_evac[jz])

            for cc in range(NCC):
                ccol = cc * C
                msq4 = prep_msq4[cc]
                b.wait("vector", "ac", msq4)
                for kh2 in range(2):
                    b.op("vector", lambda e, kh2=kh2: e.tensor_scalar(
                        out=qt_sb[kh2][:], in0=qt_sb[kh2][:], scalar1=rqk[:, 2 * kh2:2 * kh2 + 1],
                        scalar2=None, op0=OP.mult, op1=OP.bypass),
                        reads=[qt_sb[kh2], rqk], writes=[qt_sb[kh2]])
                    b.op("vector", lambda e, kh2=kh2: e.tensor_scalar(
                        out=ktn[kh2][:], in0=kt_sb[kh2][:], scalar1=rqk[:, 2 * kh2 + 1:2 * kh2 + 2],
                        scalar2=None, op0=OP.mult, op1=OP.bypass),
                        reads=[kt_sb[kh2], rqk], writes=[ktn[kh2]])
                mnorm = b.mark("vector")
                b.wait("tensor", "dv", mnorm)
                backb = {}
                for kh2 in range(2):
                    bq2_i, bq2 = acquire_bank()
                    b.op("tensor", lambda e, bq2=bq2, kh2=kh2: e.transpose(
                        bq2[:, 0:C], qt_sb[kh2][:], ident[:]))
                    pq2 = pemark()
                    bk2_i, bk2 = acquire_bank()
                    b.op("tensor", lambda e, bk2=bk2, kh2=kh2: e.transpose(
                        bk2[:, 0:C], ktn[kh2][:], ident[:]))
                    pk2 = pemark()
                    backb[kh2] = (bq2_i, bq2, pq2, bk2_i, bk2, pk2)
                for kh2 in range(2):
                    bq2_i, bq2, pq2, bk2_i, bk2, pk2 = backb[kh2]
                    b.wait("scalar", "pe", pq2)
                    b.op("scalar", lambda e, bq2=bq2, kh2=kh2: e.copy(Qf[kh2][:], bq2[:, 0:C]),
                         writes=[Qf[kh2]])
                    mq = b.mark("scalar")
                    free_bank(bq2_i, "ac", mq)
                    b.wait("scalar", "pe", pk2)
                    b.op("scalar", lambda e, bk2=bk2, kh2=kh2: e.copy(Kf[kh2][:], bk2[:, 0:C]),
                         writes=[Kf[kh2]])
                    mk = b.mark("scalar")
                    free_bank(bk2_i, "ac", mk)
                mqkf = b.cnt["ac"]
                SLK = [(0, 0), (1, 0), (2, 1), (3, 1)]
                gcols = {sl: gct[:, cc * 32 + 4 * g + sl:cc * 32 + 4 * g + sl + 1] for sl in range(4)}
                bcols = {sl: bett[:, cc * 32 + 4 * g + sl:cc * 32 + 4 * g + sl + 1] for sl in range(4)}
                b.wait("vector", "ac", gates_ac)
                b.wait("vector", "dv", gates_dv)
                for sl, kh2 in SLK:
                    b.op("vector", lambda e, sl=sl, gc_=gcols[sl]: e.tensor_scalar(
                        out=e_t[sl][:], in0=ones_all[:], scalar1=gc_, scalar2=None,
                        op0=OP.mult, op1=OP.bypass),
                        reads=[ones_all, gct], writes=[e_t[sl]])
                mgtb = b.mark("vector")
                for sl, kh2 in SLK:
                    b.op("vector", lambda e, sl=sl, bc_=bcols[sl]: e.tensor_scalar(
                        out=curT[sl][:], in0=ones_all[:], scalar1=bc_, scalar2=None,
                        op0=OP.mult, op1=OP.bypass),
                        reads=[ones_all, bett], writes=[curT[sl]])
                mbtb = b.mark("vector")
                b.wait("tensor", "dv", mgtb)
                bcb = {}
                for sl, kh2 in SLK:
                    bBC_i, bBC = acquire_bank()
                    b.op("tensor", lambda e, sl=sl, bBC=bBC: e.transpose(
                        bBC[:, 0:C], e_t[sl][:], ident[:]))
                    bcb[sl] = (bBC_i, bBC, pemark())
                for sl, kh2 in SLK:
                    bBC_i, bBC, pm = bcb[sl]
                    b.wait("vector", "pe", pm)
                    b.op("vector", lambda e, sl=sl, bBC=bBC, gc_=gcols[sl]: e.tensor_scalar(
                        out=e_t[sl][:], in0=bBC[:, 0:C], scalar1=gc_, scalar2=0.0,
                        op0=OP.subtract, op1=OP.min),
                        reads=[gct], writes=[e_t[sl]])
                    me = b.mark("vector")
                    b.wait("scalar", "dv", me)
                    b.op("scalar", lambda e, sl=sl: e.activation(out=Dfull[sl][:], in_=e_t[sl][:],
                                                                 func=A.Exp),
                         reads=[e_t[sl]], writes=[Dfull[sl]])
                    b.op("scalar", lambda e, sl=sl, bBC=bBC: e.activation(out=Dbc[sl][:],
                                                                          in_=bBC[:, 0:C], func=A.Exp),
                         writes=[Dbc[sl]])
                    mexp = b.mark("scalar")
                    free_bank(bBC_i, "ac", mexp)
                    bcb[sl] = mexp
                for sl, kh2 in SLK:
                    b.wait("gpsimd", "ac", bcb[sl])
                    b.op("gpsimd", lambda e, sl=sl: e.affine_select(
                        Dincl[sl][:], Dfull[sl][:], pattern=[[1, 128]], base=0,
                        channel_multiplier=-1, compare_op=OP.is_ge, fill=0.0),
                        reads=[Dfull[sl]], writes=[Dincl[sl]])
                mdincl = b.mark("gpsimd")
                b.wait("tensor", "dv", mbtb)
                bbb = {}
                for sl, kh2 in SLK:
                    bBB_i, bBB = acquire_bank()
                    b.op("tensor", lambda e, sl=sl, bBB=bBB: e.transpose(
                        bBB[:, 0:C], curT[sl][:], ident[:]))
                    bbb[sl] = (bBB_i, bBB, pemark())
                for sl, kh2 in SLK:
                    bBB_i, bBB, pmb = bbb[sl]
                    b.wait("vector", "pe", prior_fin_sl[sl])
                    vraw = qkvc[:, (4 * kh2 + 2 + (sl % 2)) * T + ccol:
                                (4 * kh2 + 2 + (sl % 2)) * T + ccol + C]
                    b.wait("vector", "pe", pmb)
                    b.wait("vector", "ac", bcb[sl])
                    b.wait("vector", "ac", conv_marks[("silu", 4 * kh2 + 2 + (sl % 2))])
                    b.op("vector", lambda e, sl=sl, bBB=bBB: e.tensor_tensor(
                        out=curB[sl][:], in0=Dfull[sl][:], in1=bBB[:, 0:C], op=OP.mult),
                        reads=[Dfull[sl]], writes=[curB[sl]])
                    b.op("vector", lambda e, sl=sl, bBB=bBB: e.tensor_tensor(
                        out=KDf[sl][:], in0=Dbc[sl][:], in1=bBB[:, 0:C], op=OP.mult),
                        reads=[Dbc[sl]], writes=[KDf[sl]])
                    b.op("vector", lambda e, sl=sl, bBB=bBB, vraw=vraw: e.tensor_tensor(
                        out=Bfv[sl][:], in0=vraw, in1=bBB[:, 0:C], op=OP.mult),
                        reads=[qkvc], writes=[Bfv[sl]])
                    mvv = b.mark("vector")
                    free_bank(bBB_i, "dv", mvv)
                mbp = b.cnt["dv"]
                b.wait("tensor", "dv", mbp)
                sc = {}
                for kh2 in range(2):
                    bG_i, bG = acquire_bank()
                    b.op("tensor", lambda e, bG=bG, kh2=kh2: e.matmul(
                        bG[:, 0:C], Kf[kh2][:], Kf[kh2][:], start=True, stop=True))
                    bP_i, bP = acquire_bank()
                    b.op("tensor", lambda e, bP=bP, kh2=kh2: e.matmul(
                        bP[:, 0:C], Kf[kh2][:], Qf[kh2][:], start=True, stop=True))
                    sc[kh2] = (bG_i, bG, bP_i, bP, pemark())
                b.wait("vector", "gp", mdincl)
                mprep_sl = [0, 0, 0, 0]
                for sl, kh2 in SLK:
                    bG_i, bG, bP_i, bP, pm_sc = sc[kh2]
                    b.wait("vector", "pe", pm_sc)
                    b.op("vector", lambda e, sl=sl, bP=bP: e.tensor_tensor(
                        out=Mt[sl][:], in0=bP[:, 0:C], in1=Dincl[sl][:], op=OP.mult),
                        reads=[Dincl[sl]], writes=[Mt[sl]])
                    b.op("vector", lambda e, sl=sl, bG=bG: e.tensor_tensor(
                        out=curT[sl][:], in0=bG[:, 0:C], in1=curB[sl][:], op=OP.mult),
                        reads=[curB[sl], curT[sl]], writes=[curT[sl]])
                    b.op("vector", lambda e, sl=sl: e.tensor_tensor(
                        out=curA[sl][:], in0=curT[sl][:], in1=mS[:], op=OP.mult),
                        reads=[curT[sl], mS], writes=[curA[sl]])
                    b.op("vector", lambda e, sl=sl, kh2=kh2: e.tensor_tensor(
                        out=KDf[sl][:], in0=Kf[kh2][:], in1=KDf[sl][:], op=OP.mult),
                        reads=[Kf[kh2], KDf[sl]], writes=[KDf[sl]])
                    b.op("vector", lambda e, sl=sl, kh2=kh2: e.tensor_tensor(
                        out=QDf[sl][:], in0=Qf[kh2][:], in1=Dbc[sl][:], op=OP.mult),
                        reads=[Qf[kh2], Dbc[sl]], writes=[QDf[sl]])
                    mprep_sl[sl] = b.mark("vector")
                mprep = mprep_sl[3]
                for kh2 in range(2):
                    bG_i, bG, bP_i, bP, _ = sc[kh2]
                    free_bank(bG_i, "dv", mprep_sl[2 * kh2 + 1])
                    free_bank(bP_i, "dv", mprep_sl[2 * kh2 + 1])
                b.wait("tensor", "dv", mprep)
                w0b = {}
                for sl, kh2 in SLK:
                    vh = 4 * g + sl
                    b.wait("tensor", "dv", st_upd_sl[sl])
                    bW_i, bW = acquire_bank()
                    b.op("tensor", lambda e, bW=bW, vh=vh, sl=sl: e.matmul(
                        bW[:, 0:C], state[:, vh * 128:(vh + 1) * 128], KDf[sl][:],
                        start=True, stop=True))
                    w0b[sl] = (bW_i, bW, pemark())
                for sl, kh2 in SLK:
                    bW_i, bW, pmw = w0b[sl]
                    b.wait("vector", "pe", pmw)
                    b.op("vector", lambda e, sl=sl, bW=bW: e.tensor_tensor(
                        out=Bfv[sl][:], in0=Bfv[sl][:], in1=bW[:, 0:C], op=OP.subtract),
                        reads=[Bfv[sl]], writes=[Bfv[sl]])
                    mv2 = b.mark("vector")
                    free_bank(bW_i, "dv", mv2)
                mbf = b.cnt["dv"]
                ut_evac = [0, 0, 0, 0]
                b.wait("tensor", "dv", mbf)
                btb_ = {}
                for sl, kh2 in SLK:
                    bT_i, bT = acquire_bank()
                    b.op("tensor", lambda e, bT=bT, sl=sl: e.transpose(bT[:, 0:C], Bfv[sl][:],
                                                                       ident[:]))
                    btb_[sl] = (bT_i, bT, pemark())
                for sl, kh2 in SLK:
                    bT_i, bT, pmt = btb_[sl]
                    b.wait("scalar", "pe", pmt)
                    b.op("scalar", lambda e, sl=sl, bT=bT: e.copy(Ut[sl][:], bT[:, 0:C]),
                         writes=[Ut[sl]])
                    ut_evac[sl] = b.mark("scalar")
                    free_bank(bT_i, "ac", ut_evac[sl])
                for sl, kh2 in SLK:
                    b.op("vector", lambda e, sl=sl, kh2=kh2: e.tensor_scalar(
                        out=Kddt[sl][:], in0=ktn[kh2][:], scalar1=Dincl[sl][:, C - 1:C],
                        scalar2=None, op0=OP.mult, op1=OP.bypass),
                        reads=[ktn[kh2], Dincl[sl]], writes=[Kddt[sl]])
                b.mark("vector")

                cur = [curA[i] for i in range(4)]
                nxt = [curB[i] for i in range(4)]
                uadd = [0, 0, 0, 0]
                sqcp = [0, 0, 0, 0]
                for L in range(7):
                    app_banks = []
                    for sl in range(4):
                        bA_i, bA = acquire_bank()
                        if L > 0:
                            b.wait("tensor", "dv", uadd[sl])
                            b.wait("tensor", "ac", sqcp[sl])
                        else:
                            b.wait("tensor", "dv", mprep_sl[sl])
                            b.wait("tensor", "ac", ut_evac[sl])
                        b.op("tensor", lambda e, bA=bA, sl=sl, cur=cur: e.matmul(
                            bA[:, 0:C], cur[sl][:], Ut[sl][:], start=True, stop=True))
                        app_banks.append((bA_i, bA, pemark()))
                    for sl in range(4):
                        bA_i, bA, pm = app_banks[sl]
                        b.wait("vector", "pe", pm)
                        b.op("vector", lambda e, sl=sl, bA=bA: e.tensor_tensor(
                            out=Ut[sl][:], in0=Ut[sl][:], in1=bA[:, 0:C], op=OP.add),
                            reads=[Ut[sl]], writes=[Ut[sl]])
                        uadd[sl] = b.mark("vector")
                        free_bank(bA_i, "dv", uadd[sl])
                    if L < 6:
                        tr_banks = []
                        for sl in range(4):
                            bR_i, bR = acquire_bank()
                            b.op("tensor", lambda e, bR=bR, sl=sl, cur=cur: e.transpose(
                                bR[:, 0:C], cur[sl][:], ident[:]))
                            tr_banks.append((bR_i, bR, pemark()))
                        tr_marks = []
                        for sl in range(4):
                            bR_i, bR, pm = tr_banks[sl]
                            b.wait("scalar", "pe", pm)
                            b.op("scalar", lambda e, sl=sl, bR=bR: e.copy(curT[sl][:], bR[:, 0:C]),
                                 writes=[curT[sl]])
                            mm_ = b.mark("scalar")
                            tr_marks.append(mm_)
                            free_bank(bR_i, "ac", mm_)
                        sq_banks = []
                        for sl in range(4):
                            bS_i, bS = acquire_bank()
                            b.wait("tensor", "ac", tr_marks[sl])
                            b.op("tensor", lambda e, sl=sl, bS=bS, cur=cur: e.matmul(
                                bS[:, 0:C], curT[sl][:], cur[sl][:], start=True, stop=True))
                            sq_banks.append((bS_i, bS, pemark()))
                        for sl in range(4):
                            bS_i, bS, pm = sq_banks[sl]
                            b.wait("scalar", "pe", pm)
                            b.op("scalar", lambda e, sl=sl, bS=bS, nxt=nxt: e.copy(nxt[sl][:],
                                                                                   bS[:, 0:C]),
                                 writes=[nxt[sl]])
                            sqcp[sl] = b.mark("scalar")
                            free_bank(bS_i, "ac", sqcp[sl])
                        cur, nxt = nxt, cur
                for half in range(2):
                    fb = {}
                    for sl in (2 * half, 2 * half + 1):
                        vh = 4 * g + sl
                        b.wait("tensor", "dv", uadd[sl])
                        bW_i, bW = acquire_bank()
                        b.op("tensor", lambda e, sl=sl, bW=bW: e.matmul(
                            bW[:, 0:C], Kddt[sl][:], Ut[sl][:], start=True, stop=True))
                        pm_ds = pemark()
                        bO_i, bO = acquire_bank()
                        b.op("tensor", lambda e, sl=sl, bO=bO: e.matmul(
                            bO[:, 0:C], Ut[sl][:], Mt[sl][:], start=True, stop=False))
                        b.op("tensor", lambda e, sl=sl, vh=vh, bO=bO: e.matmul(
                            bO[:, 0:C], state[:, vh * 128:(vh + 1) * 128], QDf[sl][:],
                            start=False, stop=True))
                        fb[sl] = (bW_i, bW, bO_i, bO, pemark())
                        prior_finish_pe[0] = fb[sl][4]
                        prior_fin_sl[sl] = fb[sl][4]
                    for sl in (2 * half, 2 * half + 1):
                        vh = 4 * g + sl
                        bW_i, bW, bO_i, bO, pm_o = fb[sl]
                        b.wait("vector", "pe", pm_o)
                        b.op("vector", lambda e, sl=sl, vh=vh, bW=bW: e.scalar_tensor_tensor(
                            out=state[:, vh * 128:(vh + 1) * 128],
                            in0=state[:, vh * 128:(vh + 1) * 128],
                            scalar=Dbc[sl][:, C - 1:C], in1=bW[:, 0:C], op0=OP.mult, op1=OP.add),
                            reads=[state, Dbc[sl]], writes=[state])
                        smu = b.mark("vector")
                        st_upd_last[0] = smu
                        st_upd_sl[sl] = smu
                        free_bank(bW_i, "dv", smu)
                        b.wait("scalar", "pe", pm_o)
                        b.op("scalar", lambda e, sl=sl, cc=cc, bO=bO: e.copy(
                            o_buf[:, sl * T + cc * C: sl * T + cc * C + C], bO[:, 0:C]),
                            writes=[o_buf])
                        o_done[sl] = b.mark("scalar")
                        free_bank(bO_i, "ac", o_done[sl])
                if cc + 1 < NCC:
                    emit_qkprep(cc + 1)
                emit_zcol(cc)
            scan_ac = b.cnt["ac"]
            scan_dv = b.cnt["dv"]

            gw = dict(gate_dv=[], rowbuf_pe=[0], wout_evac=[], wout_pe=[])

            def make_gate(half, g=g, z_evac=z_evac, scan_ac=scan_ac, gw=gw, o_done=tuple(o_done)):
                gate_dv = gw["gate_dv"]
                rowbuf_pe = gw["rowbuf_pe"]
                for _one in [half]:
                        sls = (2 * half, 2 * half + 1)
                        rowb = {}
                        for sl in sls:
                            osl = o_buf[:, sl * T:(sl + 1) * T]
                            sqX = sq1 if sl % 2 == 0 else sq2
                            b.wait("scalar", "ac", o_done[sl])
                            b.op("scalar", lambda e, osl=osl, sqX=sqX: e.activation(
                                out=sqX[:], in_=osl, func=A.Square),
                                reads=[o_buf], writes=[sqX])
                            msq = b.mark("scalar")
                            brow_i, brow = acquire_bank()
                            b.wait("tensor", "ac", msq)
                            b.op("tensor", lambda e, brow=brow, sqX=sqX: e.matmul(
                                brow[0:1, 0:T], onesP[:], sqX[:], start=True, stop=True))
                            rowb[sl] = (brow_i, brow, pemark())
                        bcb2 = {}
                        for sl in sls:
                            brow_i, brow, pm_row = rowb[sl]
                            b.wait("scalar", "pe", pm_row)
                            b.wait("scalar", "pe", rowbuf_pe[0])
                            b.op("scalar", lambda e, brow=brow: e.copy(rowbuf[:], brow[0:1, 0:T]),
                                 writes=[rowbuf])
                            mrb = b.mark("scalar")
                            free_bank(brow_i, "ac", mrb)
                            bb_i, bb = acquire_bank()
                            b.wait("tensor", "ac", mrb)
                            b.op("tensor", lambda e, bb=bb: e.matmul(bb[:, 0:T], ones_all[0:1, :], rowbuf[:],
                                                                     start=True, stop=True))
                            bcb2[sl] = (bb_i, bb, pemark())
                            rowbuf_pe[0] = bcb2[sl][2]
                        for sl in sls:
                            bb_i, bb, pm_bb = bcb2[sl]
                            b.wait("vector", "pe", pm_bb)
                            b.op("vector", lambda e, bb=bb: e.tensor_scalar(
                                out=bb[:, 0:T], in0=bb[:, 0:T], scalar1=1.0 / DV, scalar2=EPS_RMS,
                                op0=OP.mult, op1=OP.add), reads=[bb], writes=[bb])
                        mrec = b.mark("vector")
                        b.wait("scalar", "dv", mrec)
                        for sl in sls:
                            bb_i, bb, _ = bcb2[sl]
                            b.op("scalar", lambda e, bb=bb: e.activation(out=bb[:, 0:T], in_=bb[:, 0:T],
                                                                         func=A.Ln), reads=[bb], writes=[bb])
                        mln2 = b.mark("scalar")
                        b.wait("vector", "ac", mln2)
                        for sl in sls:
                            bb_i, bb, _ = bcb2[sl]
                            b.op("vector", lambda e, bb=bb: e.tensor_scalar(
                                out=bb[:, 0:T], in0=bb[:, 0:T], scalar1=-0.5, scalar2=None,
                                op0=OP.mult, op1=OP.bypass), reads=[bb], writes=[bb])
                        maf2 = b.mark("vector")
                        b.wait("scalar", "dv", maf2)
                        for sl in sls:
                            bb_i, bb, _ = bcb2[sl]
                            zsl = u_buf[:, sl * 520:sl * 520 + 512]
                            b.op("scalar", lambda e, bb=bb: e.activation(out=bb[:, 0:T], in_=bb[:, 0:T],
                                                                         func=A.Exp), reads=[bb], writes=[bb])
                            b.wait("scalar", "ac", z_evac[sl])
                            b.op("scalar", lambda e, zsl=zsl: e.activation(out=zsl, in_=zsl, func=A.Silu),
                                 reads=[u_buf], writes=[u_buf])
                        msil = b.mark("scalar")
                        b.wait("vector", "ac", msil)
                        b.wait("vector", "mdma", ld_misc)
                        for sl in sls:
                            bb_i, bb, _ = bcb2[sl]
                            osl = o_buf[:, sl * T:(sl + 1) * T]
                            zsl = u_buf[:, sl * 520:sl * 520 + 512]
                            b.op("vector", lambda e, osl=osl, bb=bb: e.scalar_tensor_tensor(
                                out=osl, in0=osl, scalar=nw[:], in1=bb[:, 0:T], op0=OP.mult, op1=OP.mult),
                                reads=[o_buf, nw], writes=[o_buf])
                            mg1 = b.mark("vector")
                            free_bank(bb_i, "dv", mg1)
                            b.op("vector", lambda e, osl=osl, zsl=zsl: e.tensor_tensor(
                                out=osl, in0=osl, in1=zsl, op=OP.mult),
                                reads=[o_buf, u_buf], writes=[o_buf])
                            gv = g16[sl]
                            b.op("vector", lambda e, osl=osl, d=gv["h0"]: e.tensor_copy(
                                d[:, 0:256], osl[:, 0:256]), reads=[o_buf], writes=[Mt[sl]])
                            b.op("vector", lambda e, osl=osl, d=gv["h1"]: e.tensor_copy(
                                d[:, 0:256], osl[:, 256:512]), reads=[o_buf], writes=[QDf[sl]])
                            b.op("vector", lambda e, osl=osl, h=gv["h0"], d=gv["l0"]: e.tensor_tensor(
                                out=d[:, 0:256], in0=osl[:, 0:256], in1=h[:, 0:256], op=OP.subtract),
                                reads=[o_buf, Mt[sl]], writes=[Ut[sl]])
                            b.op("vector", lambda e, osl=osl, h=gv["h1"], d=gv["l1"]: e.tensor_tensor(
                                out=d[:, 0:256], in0=osl[:, 256:512], in1=h[:, 0:256], op=OP.subtract),
                                reads=[o_buf, QDf[sl]], writes=[curA[sl]])
                            gate_dv.append(b.mark("vector"))


            def make_wout(j, g=g, gw=gw):
                gate_dv = gw["gate_dv"]
                wout_evac = gw["wout_evac"]
                wout_pe = gw["wout_pe"]
                for _one in [j]:
                        wap, wv = wo_consume()
                        pb = j % 2
                        bank = P0 if pb == 0 else P1
                        b.wait("tensor", wv[0], wv[1])
                        b.wait("tensor", "dv", gate_dv[-1])
                        if pbank_free[pb] is not None:
                            b.wait("tensor", pbank_free[pb][0], pbank_free[pb][1])
                        whap, wlap = wap
                        for half in range(2):
                            nmm = 0
                            for i in range(4):
                                wh = whap[:, i * 128:(i + 1) * 128]
                                wl = wlap[:, i * 128:(i + 1) * 128]
                                mh = (g16[i]["h0"] if half == 0 else g16[i]["h1"])[:, 0:256]
                                ml = (g16[i]["l0"] if half == 0 else g16[i]["l1"])[:, 0:256]
                                hsl = slice(half * 256, half * 256 + 256)
                                b.op("tensor", lambda e, wh=wh, mh=mh, bank=bank, hsl=hsl, first=(nmm == 0):
                                     e.matmul(bank[:, hsl], wh, mh, start=first, stop=False))
                                nmm += 1
                                b.op("tensor", lambda e, wh=wh, ml=ml, bank=bank, hsl=hsl: e.matmul(
                                    bank[:, hsl], wh, ml, start=False, stop=False))
                                nmm += 1
                                b.op("tensor", lambda e, wl=wl, mh=mh, bank=bank, hsl=hsl,
                                     last=(i == 3): e.matmul(bank[:, hsl], wl, mh, start=False, stop=last))
                                nmm += 1
                        mm_done = pemark()
                        wout_pe.append(mm_done)
                        wo_release("pe", mm_done)
                        if g == 0:
                            b.wait("scalar", "pe", mm_done)
                            b.op("scalar", lambda e, j=j, bank=bank: e.copy(
                                attn[:, j * T:(j + 1) * T], bank[:, 0:T]), writes=[attn])
                            mk = ("ac", b.mark("scalar"))
                        else:
                            b.wait("vector", "pe", mm_done)
                            b.op("vector", lambda e, j=j, bank=bank: e.tensor_tensor(
                                out=attn[:, j * T:(j + 1) * T], in0=attn[:, j * T:(j + 1) * T],
                                in1=bank[:, 0:T], op=OP.add),
                                reads=[attn], writes=[attn])
                            mk = ("dv", b.mark("vector"))
                        wout_evac.append(mk)
                        pbank_free[pb] = mk


            make_gate(0)
            make_gate(1)
            pending = [(lambda j=j: make_wout(j)) for j in range(16)]
            last_gw = gw
        for f in pending:
            f()
        pending = []
        deps2 = [[last_gw["wout_evac"][j], ("pe", last_gw["wout_pe"][j])] for j in range(16)]
        st2 = ln_stats(attn, tile_deps=deps2)
        ln_apply(attn, x_buf, st2, residual=True)
        phase_barrier()
        st3 = ln_stats(x_buf)
        lnx_ready = ln_apply(x_buf, None, st3, split=(lnx16h, lnx16l))
        lnx_tm = list(ln_apply.tile_marks)
        ffn_evac = []
        for j in range(16):
            wap, wv = wq_consume()
            pb = j % 2
            bank = P0 if pb == 0 else P1
            b.wait("tensor", wv[0], wv[1])
            if pbank_free[pb] is not None:
                b.wait("tensor", pbank_free[pb][0], pbank_free[pb][1])
            whap, wlap = wap
            for i in range(NI):
                b.wait("tensor", "dv", lnx_tm[i])
                wh = whap[:, i * 128:(i + 1) * 128]
                wl = wlap[:, i * 128:(i + 1) * 128]
                mh = lnx16h[:, i * T:(i + 1) * T]
                ml = lnx16l[:, i * T:(i + 1) * T]
                b.op("tensor", lambda e, wh=wh, mh=mh, bank=bank, i=i: e.matmul(
                    bank[:, 0:T], wh, mh, start=(i == 0), stop=False))
                b.op("tensor", lambda e, wh=wh, ml=ml, bank=bank: e.matmul(
                    bank[:, 0:T], wh, ml, start=False, stop=False))
                b.op("tensor", lambda e, wl=wl, mh=mh, bank=bank, i=i: e.matmul(
                    bank[:, 0:T], wl, mh, start=False, stop=(i == NI - 1)))
            mm_done = pemark()
            wq_release("pe", mm_done)
            b.wait("scalar", "pe", mm_done)
            b.op("scalar", lambda e, j=j, bank=bank: e.activation(
                out=attn[:, j * T:(j + 1) * T], in_=bank[:, 0:T], func=A.Identity,
                scale=ffs[:, j:j + 1], bias=ffb[:, j:j + 1]),
                reads=[ffs, ffb], writes=[attn])
            ffn_evac.append(b.mark("scalar"))
            pbank_free[pb] = ("ac", ffn_evac[-1])
        deps4 = [[("ac", ffn_evac[j])] for j in range(16)]
        st4 = ln_stats(attn, tile_deps=deps4)
        res4 = ln_apply(attn, x_buf, st4, residual=True)
        res4_tm = list(ln_apply.tile_marks)
        for i in range(NI):
            b.wait("gpsimd", "dv", res4_tm[i])
            b.op("gpsimd", lambda e, i=i, col0=col0: e.dma_start(
                out=out_d[i * 128:(i + 1) * 128, col0:col0 + T], in_=x_buf[:, i * T:(i + 1) * T]))
            b.mark("gpsimd", "sdma", 16)
        x_store_mark = b.cnt["sdma"]

    b.wait("gpsimd", "sdma", b.cnt["sdma"])

    with nc.Block() as block:
        @block.sync
        def _(e):
            b.replay("sync", e)

        @block.tensor
        def _(e):
            b.replay("tensor", e)

        @block.vector
        def _(e):
            b.replay("vector", e)

        @block.scalar
        def _(e):
            b.replay("scalar", e)

        @block.gpsimd
        def _(e):
            b.replay("gpsimd", e)

    return nc


def host_prep_weights(W_qkvz, W_ba, conv_w, dt_bias, A_log, norm_w, W_out,
                      ffn_w, ffn_scale, ffn_bias):
    order = []
    for g in range(8):
        for kh2 in range(2):
            kh = 2 * g + kh2
            order += [6 * kh + 0, 6 * kh + 1, 6 * kh + 2, 6 * kh + 3]
        order += [6 * (2 * g) + 4, 6 * (2 * g) + 5, 6 * (2 * g + 1) + 4, 6 * (2 * g + 1) + 5]
    wq4 = W_qkvz.reshape(16, 128, 96, 128).transpose(2, 1, 0, 3)
    wq_dv = np.ascontiguousarray(wq4[order]).astype(np.float32)

    wo4 = W_out.reshape(8, 4, 128, 16, 128)
    wo_dv = np.ascontiguousarray(wo4.transpose(0, 3, 2, 1, 4)).astype(np.float32)

    wf_dv = np.ascontiguousarray(
        ffn_w.T.reshape(16, 128, 16, 128).transpose(2, 1, 0, 3)).astype(np.float32)

    perm = []
    for vh in range(32):
        perm.append((vh // 2) * 4 + vh % 2)
    for vh in range(32):
        perm.append((vh // 2) * 4 + 2 + vh % 2)
    wba_dv = np.ascontiguousarray(
        W_ba[:, perm].reshape(16, 128, 64).transpose(1, 0, 2)).astype(np.float32)

    cw_order = []
    for g in range(8):
        cw_order += [2 * g, 16 + 2 * g, 32 + 4 * g, 33 + 4 * g,
                     2 * g + 1, 16 + 2 * g + 1, 34 + 4 * g, 35 + 4 * g]
    cw_dv = np.ascontiguousarray(
        conv_w.reshape(64, 128, 4)[cw_order].transpose(1, 0, 2)).astype(np.float32)

    return dict(
        wq_d=wq_dv, wo_d=wo_dv, wf_d=wf_dv, wba_d=wba_dv, cw_d=cw_dv,
        alog_d=np.ascontiguousarray(A_log.reshape(32, 1)).astype(np.float32),
        dtb_d=np.ascontiguousarray(dt_bias.reshape(32, 1)).astype(np.float32),
        nw_d=np.ascontiguousarray(norm_w.reshape(128, 1)).astype(np.float32),
        fs_d=np.ascontiguousarray(ffn_scale.reshape(16, 128).T).astype(np.float32),
        fb_d=np.ascontiguousarray(ffn_bias.reshape(16, 128).T).astype(np.float32),
    )


from concourse.bass_utils import run_bass_kernel_spmd

_NC_CACHE = {}


def kernel(L0_attn_x, L0_attn_W_qkvz, L0_attn_W_ba, L0_attn_conv_w, L0_attn_dt_bias,
           L0_attn_A_log, L0_attn_norm_w, L0_attn_W_out, L0_ffn_weight,
           L0_ffn_scale_weight, L0_ffn_bias):
    wd = host_prep_weights(
        np.asarray(L0_attn_W_qkvz, dtype=np.float32),
        np.asarray(L0_attn_W_ba, dtype=np.float32),
        np.asarray(L0_attn_conv_w, dtype=np.float32),
        np.asarray(L0_attn_dt_bias, dtype=np.float32),
        np.asarray(L0_attn_A_log, dtype=np.float32),
        np.asarray(L0_attn_norm_w, dtype=np.float32),
        np.asarray(L0_attn_W_out, dtype=np.float32),
        np.asarray(L0_ffn_weight, dtype=np.float32),
        np.asarray(L0_ffn_scale_weight, dtype=np.float32),
        np.asarray(L0_ffn_bias, dtype=np.float32))
    x = np.asarray(L0_attn_x, dtype=np.float32)
    n_cores = 8
    in_maps = []
    for i in range(n_cores):
        m = dict(wd)
        m["x_in"] = np.ascontiguousarray(x[i].T)
        in_maps.append(m)
    nc = build(NT=8)
    res = run_bass_kernel_spmd(nc, in_maps, core_ids=list(range(n_cores)))
    out = np.stack([np.asarray(r["out_d"]).T for r in res.results], axis=0)
    return np.ascontiguousarray(out.astype(np.float32))


# version 18
# speedup vs baseline: 1.2368x; 1.0165x over previous
import numpy as np
import concourse.bass as bass
import concourse.mybir as mybir

F32 = mybir.dt.float32
BF16 = mybir.dt.bfloat16
A = mybir.ActivationFunctionType
OP = mybir.AluOpType

H = 2048
SEQ = 4096
T = 512
C = 128
NCC = T // C
NK, NV, DK, DV = 16, 32, 128, 128
NG = 8
NI = H // 128
EPS_LN = 1e-5
EPS_RMS = 1e-6
LN128C = 4.852030263919617

ENGINES = ("tensor", "vector", "scalar", "gpsimd", "sync")
ENG_SEM = {"tensor": "pe", "vector": "dv", "scalar": "ac", "gpsimd": "gp"}


class Builder:
    def __init__(self, nc):
        self.nc = nc
        self.prog = {e: [] for e in ENGINES}
        self.sems = {}
        self.cnt = {}
        self.dirty_w = {e: set() for e in ENGINES}
        self.dirty_r = {e: set() for e in ENGINES}
        self.waited = {e: {} for e in ENGINES}

    def sem(self, name):
        self.sems[name] = self.nc.alloc_semaphore(name)
        self.cnt[name] = 0

    @staticmethod
    def _nm(x):
        if x is None:
            return None
        t = getattr(x, "tensor", x)
        return getattr(t, "name", str(t))

    def op(self, eng, fn, reads=(), writes=()):
        r = {self._nm(a) for a in reads} - {None}
        w = {self._nm(a) for a in writes} - {None}
        if (r & self.dirty_w[eng]) or (w & (self.dirty_w[eng] | self.dirty_r[eng])):
            self.prog[eng].append(dict(fn=lambda e: e.drain(), incs=[]))
            self.dirty_w[eng] = set()
            self.dirty_r[eng] = set()
        self.dirty_r[eng] |= r
        self.dirty_w[eng] |= w
        self.prog[eng].append(dict(fn=fn, incs=[]))

    def mark(self, eng, sname=None, n=1):
        sname = sname or ENG_SEM[eng]
        item = self.prog[eng][-1]
        for (s0, _n0) in item["incs"]:
            if s0 == sname:
                return self.cnt[sname]
        assert not item["incs"], "one sem update per instruction"
        item["incs"].append((sname, n))
        self.cnt[sname] += n
        return self.cnt[sname]

    def wait(self, eng, sname, value):
        if value is None or value <= 0:
            return
        if self.waited[eng].get(sname, -1) >= value:
            return
        self.waited[eng][sname] = value
        h = self.sems[sname]
        self.prog[eng].append(dict(fn=lambda e, h=h, v=value: e.wait_ge(h, v), incs=[]))

    def replay(self, eng, e):
        for item in self.prog[eng]:
            inst = item["fn"](e)
            for (sname, n) in item["incs"]:
                inst.then_inc(self.sems[sname], n)


def build(NT=8):
    nc = bass.Bass("TRN2", target_bir_lowering=False)

    x_in = nc.dram_tensor("x_in", [H, SEQ], F32, kind="ExternalInput").ap()
    wq_d = nc.dram_tensor("wq_d", [96, 128, NI, 128], F32, kind="ExternalInput").ap()
    wo_d = nc.dram_tensor("wo_d", [NG, 16, 128, 4, 128], F32, kind="ExternalInput").ap()
    wf_d = nc.dram_tensor("wf_d", [16, 128, NI, 128], F32, kind="ExternalInput").ap()
    wba_d = nc.dram_tensor("wba_d", [128, NI, 64], F32, kind="ExternalInput").ap()
    cw_d = nc.dram_tensor("cw_d", [128, 64, 4], F32, kind="ExternalInput").ap()
    alog_d = nc.dram_tensor("alog_d", [32, 1], F32, kind="ExternalInput").ap()
    dtb_d = nc.dram_tensor("dtb_d", [32, 1], F32, kind="ExternalInput").ap()
    nw_d = nc.dram_tensor("nw_d", [128, 1], F32, kind="ExternalInput").ap()
    fs_d = nc.dram_tensor("fs_d", [128, 16], F32, kind="ExternalInput").ap()
    fb_d = nc.dram_tensor("fb_d", [128, 16], F32, kind="ExternalInput").ap()
    out_d = nc.dram_tensor("out_d", [H, SEQ], F32, kind="ExternalOutput").ap()
    wq16h_d = nc.dram_tensor("wq16h_d", [112, 128, NI, 128], BF16, kind="Internal").ap()
    wq16l_d = nc.dram_tensor("wq16l_d", [112, 128, NI, 128], BF16, kind="Internal").ap()
    wo16h_d = nc.dram_tensor("wo16h_d", [NG, 16, 128, 4, 128], BF16, kind="Internal").ap()
    wo16l_d = nc.dram_tensor("wo16l_d", [NG, 16, 128, 4, 128], BF16, kind="Internal").ap()

    sb = lambda n, s: nc.alloc_sbuf_tensor(n, s, F32).ap()
    x_buf = sb("x_buf", [128, NI * T])
    lnx16h = nc.alloc_sbuf_tensor("lnx16h", [128, NI * T], BF16).ap()
    lnx16l = nc.alloc_sbuf_tensor("lnx16l", [128, NI * T], BF16).ap()
    qkvc = sb("qkvc", [128, 8 * T])
    attn_h = nc.alloc_sbuf_tensor("attn", [128, NI * T], F32)
    attn = attn_h.ap()
    attn16 = attn_h.bitcast(BF16).ap()
    o_buf = sb("o_buf", [128, 4 * T])
    u_buf_h = nc.alloc_sbuf_tensor("u_buf", [128, 4 * 520], F32)
    u_buf = u_buf_h.ap()
    u16 = u_buf_h.bitcast(BF16).ap()
    state = sb("state", [128, 32 * 128])
    carry = sb("carry", [128, 64 * 3])
    wq16h_sb = nc.alloc_sbuf_tensor("wq16h_sb", [128, 2 * NI * 128], BF16).ap()
    wq16l_sb = nc.alloc_sbuf_tensor("wq16l_sb", [128, 2 * NI * 128], BF16).ap()
    wo16h_sb = nc.alloc_sbuf_tensor("wo16h_sb", [128, 2 * 4 * 128], BF16).ap()
    wo16l_sb = nc.alloc_sbuf_tensor("wo16l_sb", [128, 2 * 4 * 128], BF16).ap()
    wba16h = nc.alloc_sbuf_tensor("wba16h", [128, NI * 64], BF16).ap()
    wba16l = nc.alloc_sbuf_tensor("wba16l", [128, NI * 64], BF16).ap()
    cw_sb = sb("cw_sb", [128, 64 * 4])
    mS = sb("mS", [128, 128])
    ident = sb("ident", [128, 128])
    onesP = sb("onesP", [128, 1])
    ones_all = sb("ones_all", [128, 128])
    beta32 = sb("beta32", [32, T])
    sp32 = sb("sp32", [32, T])
    g32 = sp32
    gc32 = sb("gc32", [32, T])
    a32 = gc32[0:32, 0:T]
    gct = sb("gct", [128, NCC * 32])
    bett = sb("bett", [128, NCC * 32])
    qt_sb = [sb("qt_sb0", [128, 128]), sb("qt_sb1", [128, 128])]
    kt_sb = [sb("kt_sb0", [128, 128]), sb("kt_sb1", [128, 128])]
    ktn = [sb("ktn0", [128, 128]), sb("ktn1", [128, 128])]
    Qf = qt_sb
    Kf = kt_sb
    rqk = sb("rqk", [128, 4])
    NVH = 4
    e_t = [sb(f"e_t{i}", [128, 128]) for i in range(NVH)]
    Dfull = [sb(f"Dfull{i}", [128, 128]) for i in range(NVH)]
    Dincl = [sb(f"Dincl{i}", [128, 128]) for i in range(NVH)]
    Dbc = [sb(f"Dbc{i}", [128, 128]) for i in range(NVH)]
    Mt = [sb(f"Mt{i}", [128, 128]) for i in range(NVH)]
    curA = [sb(f"curA{i}", [128, 128]) for i in range(NVH)]
    curB = [sb(f"curB{i}", [128, 128]) for i in range(NVH)]
    curT = [sb(f"curT{i}", [128, 128]) for i in range(NVH)]
    QDf = [sb(f"QDf{i}", [128, 128]) for i in range(NVH)]
    Ut = [sb(f"Ut{i}", [128, 128]) for i in range(NVH)]
    Bfv = e_t
    Kddt = Dfull
    KDf = Ut
    g16 = [dict(
        h0=Mt[i].tensor.bitcast(BF16).ap(),
        h1=QDf[i].tensor.bitcast(BF16).ap(),
        l0=Ut[i].tensor.bitcast(BF16).ap(),
        l1=curA[i].tensor.bitcast(BF16).ap(),
    ) for i in range(NVH)]
    sq1 = sb("sq1", [128, T])
    sq2 = sb("sq2", [128, T])
    convacc_ = [sq1, sq2]
    bag = sq1
    sqsc = sq2[:, 0:128]
    rowbuf = sb("rowbuf", [1, T])
    negA = sb("negA", [32, 1])
    tmpA = sb("tmpA", [32, 1])
    dtb = sb("dtb", [32, 1])
    nw = sb("nw", [128, 1])
    ffs = sb("ffs", [128, 16])
    ffb = sb("ffb", [128, 16])

    P0 = nc.alloc_psum_tensor("P0", [128, 512], F32).ap()
    P1 = nc.alloc_psum_tensor("P1", [128, 512], F32).ap()
    POOLB = [nc.alloc_psum_tensor(f"PB{i}", [128, 512], F32).ap() for i in range(6)]

    b = Builder(nc)
    for s in ("wq0", "wq1", "wo0", "wo1", "mdma", "adma", "xdma", "sdma",
              "pin0", "pin1", "pout0", "pout1", "pe", "dv", "ac", "gp"):
        b.sem(s)

    pbank_free = [None, None]
    bank_state = [dict(free=[], busy=False) for _ in POOLB]
    bank_rr = [0]

    def acquire_bank(writer_eng="tensor"):
        for _ in range(len(POOLB)):
            i = bank_rr[0] % len(POOLB)
            bank_rr[0] += 1
            if not bank_state[i]["busy"]:
                break
        else:
            raise RuntimeError("psum pool exhausted: all banks busy")
        bank_state[i]["busy"] = True
        for (sname, v) in bank_state[i]["free"]:
            b.wait(writer_eng, sname, v)
        bank_state[i]["free"] = []
        return i, POOLB[i]

    def free_bank(i, sname, v):
        bank_state[i]["free"].append((sname, v))
        bank_state[i]["busy"] = False

    def pemark():
        return b.mark("tensor", "pe")

    def phase_barrier(engines=("tensor", "vector", "scalar")):
        snap = {"pe": b.cnt["pe"], "dv": b.cnt["dv"], "ac": b.cnt["ac"]}
        for e_ in engines:
            for s_, v_ in snap.items():
                if ENG_SEM[e_] != s_:
                    b.wait(e_, s_, v_)

    wq_stream = []
    for tc in range(NT):
        for ci in range(112):
            wq_stream.append((wq16h_d[ci], wq16l_d[ci]))
    wo_stream = []
    for tc in range(NT):
        for g in range(NG):
            for j in range(16):
                wo_stream.append((wo16h_d[g, j], wo16l_d[g, j]))

    wq_ptr, wq_used, wq_done, wq_freed = [0], [0], {}, [None, None]
    wo_ptr, wo_used, wo_done, wo_freed = [0], [0], {}, [None, None]

    def _dma_next(ptr, stream, freed, buf, width, done, sembase):
        i = ptr[0]
        if i >= len(stream):
            return
        slot = i % 2
        if freed[slot] is not None:
            b.wait("sync", freed[slot][0], freed[slot][1])
        src = stream[i]
        if isinstance(src, tuple):
            hbuf, lbuf = buf
            dh = hbuf[:, slot * width:(slot + 1) * width]
            dl = lbuf[:, slot * width:(slot + 1) * width]
            b.op("sync", lambda e, d=dh, s=src[0]: e.dma_start(out=d, in_=s))
            b.mark("sync", sembase + str(slot), 16)
            b.op("sync", lambda e, d=dl, s=src[1]: e.dma_start(out=d, in_=s))
            done[i] = (sembase + str(slot), b.mark("sync", sembase + str(slot), 16))
        else:
            dst = buf[:, slot * width:(slot + 1) * width]
            b.op("sync", lambda e, d=dst, s=src: e.dma_start(out=d, in_=s))
            done[i] = (sembase + str(slot), b.mark("sync", sembase + str(slot), 16))
        ptr[0] += 1

    def wq_consume():
        idx = wq_used[0]
        while wq_ptr[0] <= idx + 1 and wq_ptr[0] < len(wq_stream):
            _dma_next(wq_ptr, wq_stream, wq_freed, (wq16h_sb, wq16l_sb), NI * 128, wq_done, "wq")
        slot = idx % 2
        return (wq16h_sb[:, slot * NI * 128:(slot + 1) * NI * 128],
                wq16l_sb[:, slot * NI * 128:(slot + 1) * NI * 128]), wq_done[idx]

    def wq_release(sname, v):
        wq_freed[wq_used[0] % 2] = (sname, v)
        wq_used[0] += 1

    def wo_consume():
        idx = wo_used[0]
        while wo_ptr[0] <= idx + 1 and wo_ptr[0] < len(wo_stream):
            _dma_next(wo_ptr, wo_stream, wo_freed, (wo16h_sb, wo16l_sb), 512, wo_done, "wo")
        slot = idx % 2
        return (wo16h_sb[:, slot * 512:(slot + 1) * 512],
                wo16l_sb[:, slot * 512:(slot + 1) * 512]), wo_done[idx]

    def wo_release(sname, v):
        wo_freed[wo_used[0] % 2] = (sname, v)
        wo_used[0] += 1

    b.op("gpsimd", lambda e: e.memset(curA[0][:], -1.0), writes=[curA[0]])
    b.op("gpsimd", lambda e: e.affine_select(mS[:], curA[0][:], pattern=[[1, 128]], base=0,
                                             channel_multiplier=-1, compare_op=OP.is_gt, fill=0.0),
         reads=[curA[0]], writes=[mS])
    b.op("gpsimd", lambda e: e.memset(curA[0][:], 1.0), writes=[curA[0]])
    b.op("gpsimd", lambda e: e.affine_select(ident[:], curA[0][:], pattern=[[1, 128]], base=0,
                                             channel_multiplier=-1, compare_op=OP.is_equal, fill=0.0),
         reads=[curA[0]], writes=[ident])
    b.op("gpsimd", lambda e: e.memset(onesP[:], 1.0), writes=[onesP])
    b.op("gpsimd", lambda e: e.memset(ones_all[:], 1.0), writes=[ones_all])
    b.op("gpsimd", lambda e: e.memset(state[:], 0.0), writes=[state])
    b.op("gpsimd", lambda e: e.memset(carry[:], 0.0), writes=[carry])
    gp_init = b.mark("gpsimd")
    dv_init = 0

    for dst, src in [(cw_sb[:], cw_d), (tmpA[:], alog_d), (dtb[:], dtb_d),
                     (nw[:], nw_d), (ffs[:], fs_d), (ffb[:], fb_d)]:
        b.op("gpsimd", lambda e, d=dst, s=src: e.dma_start(out=d, in_=s))
        b.mark("gpsimd", "mdma", 16)
    ld_misc = b.cnt["mdma"]
    b.wait("scalar", "mdma", ld_misc)
    b.op("scalar", lambda e: e.activation(out=negA[:], in_=tmpA[:], func=A.Exp),
         reads=[tmpA], writes=[negA])
    ac_negA = b.mark("scalar")
    b.wait("vector", "ac", ac_negA)
    b.op("vector", lambda e: e.tensor_scalar(out=negA[:], in0=negA[:], scalar1=-1.0, scalar2=None,
                                             op0=OP.mult, op1=OP.bypass),
         reads=[negA], writes=[negA])
    dv_negA = b.mark("vector")
    b.wait("tensor", "gp", gp_init)
    b.wait("tensor", "dv", dv_init)
    b.wait("scalar", "gp", gp_init)

    prol_dv = [0, 0]
    pin_cnt = [0, 0]
    pout_cnt = [0, 0]

    def prol_batch(cols, width):
        n = len(cols)
        slot = prol_batch.it % 2
        prol_batch.it += 1
        xbase = slot * 4096
        for k, (s, _h, _l) in enumerate(cols):
            if k == 0 and prol_dv[slot]:
                b.wait("sync", "dv", prol_dv[slot])
            b.op("sync", lambda e, d=x_buf[:, xbase + k * width: xbase + (k + 1) * width],
                 s=s: e.dma_start(out=d, in_=s))
            pin_cnt[slot] += 16
            b.mark("sync", "pin" + str(slot), 16)
        b.wait("vector", "pin" + str(slot), pin_cnt[slot])
        b.wait("vector", "pout" + str(slot), pout_cnt[slot])
        tot = n * width
        xs = x_buf[:, xbase:xbase + tot]
        hs = attn16[:, slot * 4096:slot * 4096 + tot]
        ls = attn16[:, 8192 + slot * 4096:8192 + slot * 4096 + tot]
        b.op("vector", lambda e, hs=hs, xs=xs: e.tensor_copy(hs, xs),
             reads=[x_buf], writes=[attn])
        b.op("vector", lambda e, ls=ls, xs=xs, hs=hs: e.tensor_tensor(
            out=ls, in0=xs, in1=hs, op=OP.subtract),
            reads=[x_buf, attn], writes=[attn])
        prol_dv[slot] = b.mark("vector")
        b.wait("sync", "dv", prol_dv[slot])
        for k, (_s, h, l) in enumerate(cols):
            b.op("sync", lambda e, d=h, s=attn16[:, slot * 4096 + k * width:
                                                  slot * 4096 + (k + 1) * width]: e.dma_start(out=d, in_=s))
            b.mark("sync", "pout" + str(slot), 16)
            b.op("sync", lambda e, d=l, s=attn16[:, 8192 + slot * 4096 + k * width:
                                                  8192 + slot * 4096 + (k + 1) * width]: e.dma_start(out=d, in_=s))
            b.mark("sync", "pout" + str(slot), 16)
            pout_cnt[slot] += 32
    prol_batch.it = 0

    for c0 in range(0, 112, 2):
        cols = []
        for ci in range(c0, min(c0 + 2, 112)):
            srcap = wq_d[ci] if ci < 96 else wf_d[ci - 96]
            cols.append((srcap, wq16h_d[ci], wq16l_d[ci]))
        prol_batch(cols, 2048)
    for gi0 in range(0, NG * 16, 8):
        cols = []
        for gi in range(gi0, gi0 + 8):
            g_, j_ = gi // 16, gi % 16
            cols.append((wo_d[g_, j_], wo16h_d[g_, j_], wo16l_d[g_, j_]))
        prol_batch(cols, 512)
    xsl = x_buf[:, 0:NI * 64]
    b.wait("sync", "dv", prol_dv[0])
    b.wait("sync", "dv", prol_dv[1])
    b.op("sync", lambda e, d=xsl: e.dma_start(out=d, in_=wba_d))
    pin_cnt[0] += 16
    b.mark("sync", "pin0", 16)
    b.wait("vector", "pin0", pin_cnt[0])
    b.op("vector", lambda e, x=xsl: e.tensor_copy(wba16h[:], x), reads=[x_buf], writes=[wba16h])
    b.op("vector", lambda e, x=xsl: e.tensor_tensor(out=wba16l[:], in0=x, in1=wba16h[:],
                                                    op=OP.subtract),
         reads=[x_buf, wba16h], writes=[wba16l])
    wba_ready = b.mark("vector")
    b.wait("sync", "pout0", pout_cnt[0])
    b.wait("sync", "pout1", pout_cnt[1])
    b.wait("sync", "dv", wba_ready)
    prologue_done = b.cnt["dv"]

    def ln_stats(src_buf, tile_deps=None):
        bx_i, bx = acquire_bank()
        bq_i, bq = acquire_bank()
        sq_marks = [None, None]
        for i in range(NI):
            sq = sq1 if i % 2 == 0 else sq2
            if tile_deps is not None:
                for (sn, sv) in tile_deps[i]:
                    b.wait("scalar", sn, sv)
                    b.wait("tensor", sn, sv)
            if sq_marks[i % 2] is not None:
                b.wait("scalar", "pe", sq_marks[i % 2])
            b.op("scalar", lambda e, s=sq, i=i: e.activation(
                out=s[:], in_=src_buf[:, i * T:(i + 1) * T], func=A.Square),
                reads=[src_buf], writes=[sq])
            acm = b.mark("scalar")
            b.op("tensor", lambda e, i=i: e.matmul(
                bx[0:1, 0:T], onesP[:], src_buf[:, i * T:(i + 1) * T],
                start=(i == 0), stop=(i == NI - 1)))
            b.wait("tensor", "ac", acm)
            b.op("tensor", lambda e, s=sq, i=i: e.matmul(
                bq[0:1, 0:T], onesP[:], s[:], start=(i == 0), stop=(i == NI - 1)))
            sq_marks[i % 2] = pemark()
        pe_done = b.cnt["pe"]
        b.wait("scalar", "pe", pe_done)
        b.op("scalar", lambda e: e.copy(rowbuf[:], bx[0:1, 0:T]), writes=[rowbuf])
        m1 = b.mark("scalar")
        free_bank(bx_i, "ac", m1)
        bX_i, bX = acquire_bank()
        b.wait("tensor", "ac", m1)
        b.op("tensor", lambda e: e.matmul(bX[:, 0:T], ones_all[0:1, :], rowbuf[:],
                                          start=True, stop=True))
        pmX = pemark()
        b.wait("scalar", "pe", pmX)
        b.op("scalar", lambda e: e.copy(rowbuf[:], bq[0:1, 0:T]), writes=[rowbuf])
        m2 = b.mark("scalar")
        free_bank(bq_i, "ac", m2)
        bQ_i, bQ = acquire_bank()
        b.wait("tensor", "ac", m2)
        b.op("tensor", lambda e: e.matmul(bQ[:, 0:T], ones_all[0:1, :], rowbuf[:],
                                          start=True, stop=True))
        pmQ = pemark()
        b.wait("vector", "pe", pmQ)
        b.op("vector", lambda e: e.tensor_scalar(out=sq1[:], in0=bX[:, 0:T], scalar1=1.0 / H,
                                                 scalar2=None, op0=OP.mult, op1=OP.bypass),
             writes=[sq1])
        mX = b.mark("vector")
        free_bank(bX_i, "dv", mX)
        b.op("vector", lambda e: e.tensor_tensor(out=sq2[:], in0=sq1[:], in1=sq1[:], op=OP.mult),
             reads=[sq1], writes=[sq2])
        b.op("vector", lambda e: e.scalar_tensor_tensor(
            out=bQ[:, 0:T], in0=bQ[:, 0:T], scalar=1.0 / H, in1=sq2[:],
            op0=OP.mult, op1=OP.subtract),
            reads=[sq2, bQ], writes=[bQ])
        b.op("vector", lambda e: e.tensor_scalar(out=bQ[:, 0:T], in0=bQ[:, 0:T], scalar1=EPS_LN,
                                                 scalar2=None, op0=OP.add, op1=OP.bypass),
             reads=[bQ], writes=[bQ])
        mv = b.mark("vector")
        b.wait("scalar", "dv", mv)
        b.op("scalar", lambda e: e.activation(out=bQ[:, 0:T], in_=bQ[:, 0:T], func=A.Ln),
             reads=[bQ], writes=[bQ])
        mln3 = b.mark("scalar")
        b.wait("vector", "ac", mln3)
        b.op("vector", lambda e: e.tensor_scalar(out=bQ[:, 0:T], in0=bQ[:, 0:T], scalar1=-0.5,
                                                 scalar2=None, op0=OP.mult, op1=OP.bypass),
             reads=[bQ], writes=[bQ])
        mv2 = b.mark("vector")
        b.wait("scalar", "dv", mv2)
        b.op("scalar", lambda e: e.activation(out=bQ[:, 0:T], in_=bQ[:, 0:T], func=A.Exp),
             reads=[bQ], writes=[bQ])
        mac = b.mark("scalar")
        return bQ_i, bQ, mac

    def ln_apply(src_buf, dst_buf, stats, residual=False, split=None):
        ln_apply.tile_marks = []
        bQ_i, bQ, mac = stats
        b.wait("vector", "ac", mac)
        for i in range(NI):
            b.op("vector", lambda e, i=i: e.tensor_tensor(
                out=sq2[:], in0=src_buf[:, i * T:(i + 1) * T], in1=sq1[:], op=OP.subtract),
                reads=[src_buf, sq1], writes=[sq2])
            if split is not None:
                hi, lo = split
                b.op("vector", lambda e, i=i: e.tensor_tensor(
                    out=sq2[:], in0=sq2[:], in1=bQ[:, 0:T], op=OP.mult),
                    reads=[sq2], writes=[sq2])
                b.op("vector", lambda e, i=i, hi=hi: e.tensor_copy(
                    hi[:, i * T:(i + 1) * T], sq2[:]),
                    reads=[sq2], writes=[hi])
                b.op("vector", lambda e, i=i, hi=hi, lo=lo: e.tensor_tensor(
                    out=lo[:, i * T:(i + 1) * T], in0=sq2[:], in1=hi[:, i * T:(i + 1) * T],
                    op=OP.subtract),
                    reads=[sq2, hi], writes=[lo])
                ln_apply.tile_marks.append(b.mark("vector"))
            elif residual:
                b.op("vector", lambda e, i=i: e.tensor_tensor(
                    out=sq2[:], in0=sq2[:], in1=bQ[:, 0:T], op=OP.mult),
                    reads=[sq2], writes=[sq2])
                b.op("vector", lambda e, i=i: e.tensor_tensor(
                    out=dst_buf[:, i * T:(i + 1) * T], in0=dst_buf[:, i * T:(i + 1) * T],
                    in1=sq2[:], op=OP.add),
                    reads=[dst_buf, sq2], writes=[dst_buf])
                ln_apply.tile_marks.append(b.mark("vector"))
            else:
                b.op("vector", lambda e, i=i: e.tensor_tensor(
                    out=dst_buf[:, i * T:(i + 1) * T], in0=sq2[:], in1=bQ[:, 0:T], op=OP.mult),
                    reads=[sq2], writes=[dst_buf])
        dv1 = b.mark("vector")
        free_bank(bQ_i, "dv", dv1)
        return dv1


    x_store_mark = 0

    mprep_sl = [0, 0, 0, 0]
    for tc in range(NT):
        col0 = tc * T
        if tc == 0:
            b.wait("gpsimd", "dv", prologue_done)
        b.wait("gpsimd", "sdma", x_store_mark)
        for i in range(NI):
            b.op("gpsimd", lambda e, i=i, col0=col0: e.dma_start(
                out=x_buf[:, i * T:(i + 1) * T], in_=x_in[i * 128:(i + 1) * 128, col0:col0 + T]))
            b.mark("gpsimd", "xdma", 16)
        x_loaded = b.cnt["xdma"]
        for e_ in ("tensor", "scalar", "vector"):
            b.wait(e_, "xdma", x_loaded)

        st1 = ln_stats(x_buf)
        lnx_ready = ln_apply(x_buf, None, st1, split=(lnx16h, lnx16l))
        lnx_tm = list(ln_apply.tile_marks)

        bba_i, bba = acquire_bank()
        b.wait("tensor", "dv", wba_ready)
        for i in range(NI):
            b.wait("tensor", "dv", lnx_tm[i])
            wh = wba16h[:, i * 64:(i + 1) * 64]
            wl = wba16l[:, i * 64:(i + 1) * 64]
            mh = lnx16h[:, i * T:(i + 1) * T]
            ml = lnx16l[:, i * T:(i + 1) * T]
            b.op("tensor", lambda e, wh=wh, mh=mh, bba=bba, i=i: e.matmul(
                bba[0:64, 0:T], wh, mh, start=(i == 0), stop=False))
            b.op("tensor", lambda e, wh=wh, ml=ml, bba=bba: e.matmul(
                bba[0:64, 0:T], wh, ml, start=False, stop=False))
            b.op("tensor", lambda e, wl=wl, mh=mh, bba=bba, i=i: e.matmul(
                bba[0:64, 0:T], wl, mh, start=False, stop=(i == NI - 1)))
        ba_done = pemark()
        b.wait("scalar", "pe", ba_done)
        b.op("scalar", lambda e, bba=bba: e.activation(out=beta32[:], in_=bba[0:32, 0:T], func=A.Sigmoid),
             writes=[beta32])
        b.op("scalar", lambda e, bba=bba: e.copy(bag[32:64, 0:T], bba[32:64, 0:T]), writes=[bag])
        ba_ac = b.mark("scalar")
        free_bank(bba_i, "ac", ba_ac)
        b.wait("gpsimd", "ac", ba_ac)
        b.op("gpsimd", lambda e: e.dma_start(out=a32, in_=bag[32:64, 0:T]), reads=[bag])
        b.mark("gpsimd", "adma", 16)
        a_shift = b.cnt["adma"]
        b.wait("scalar", "adma", a_shift)
        b.op("scalar", lambda e: e.activation(out=sp32[:], in_=a32, func=A.Exp,
                                              bias=dtb[:], scale=1.0),
             reads=[gc32, dtb], writes=[sp32])
        b.op("scalar", lambda e: e.activation(out=sp32[:], in_=sp32[:], func=A.Ln,
                                              bias=1.0, scale=1.0),
             reads=[sp32], writes=[sp32])
        sp_ac = b.mark("scalar")
        b.wait("vector", "ac", sp_ac)
        b.wait("vector", "dv", dv_negA)
        b.op("vector", lambda e: e.tensor_scalar(out=g32[:], in0=sp32[:], scalar1=negA[:], scalar2=None,
                                                 op0=OP.mult, op1=OP.bypass),
             reads=[sp32, negA], writes=[g32])
        for cc in range(NCC):
            b.op("vector", lambda e, cc=cc: e.tensor_tensor_scan(
                out=gc32[:, cc * C:(cc + 1) * C], data0=g32[:, cc * C:(cc + 1) * C],
                data1=g32[:, cc * C:(cc + 1) * C], initial=0.0, op0=OP.add, op1=OP.bypass),
                reads=[g32], writes=[gc32])
        gates_dv = b.mark("vector")
        b.wait("tensor", "dv", gates_dv)
        for cc in range(NCC):
            for (srcb, dstb) in ((gc32, gct), (beta32, bett)):
                bt_i, bt = acquire_bank()
                b.op("tensor", lambda e, cc=cc, srcb=srcb, bt=bt: e.transpose(
                    bt[:, 0:32], srcb[0:32, cc * C:(cc + 1) * C], ident[0:32, 0:32]))
                pm = pemark()
                b.wait("scalar", "pe", pm)
                b.op("scalar", lambda e, cc=cc, dstb=dstb, bt=bt: e.copy(
                    dstb[:, cc * 32:(cc + 1) * 32], bt[:, 0:32]),
                    writes=[dstb])
                free_bank(bt_i, "ac", b.mark("scalar"))
        gates_ac = b.cnt["ac"]
        b.wait("vector", "ac", gates_ac)

        pending = []
        last_gw = None
        for g in range(NG):
            conv_marks = {}
            fill_iter = iter(pending)
            prep_msq4 = {}

            def emit_qkprep(cc, conv_marks=conv_marks, prep_msq4=prep_msq4):
                ccol = cc * C
                raw = {}
                for kh2 in range(2):
                    raw[kh2] = (qkvc[:, (4 * kh2) * T + ccol: (4 * kh2) * T + ccol + C],
                                qkvc[:, (4 * kh2 + 1) * T + ccol: (4 * kh2 + 1) * T + ccol + C])
                tbanks = {}
                for kh2 in range(2):
                    b.wait("tensor", "ac", conv_marks[("silu", 4 * kh2)])
                    b.wait("tensor", "ac", conv_marks[("silu", 4 * kh2 + 1)])
                    bq_i, bq = acquire_bank()
                    b.op("tensor", lambda e, a=raw[kh2][0], bq=bq: e.transpose(bq[:, 0:C], a, ident[:]))
                    pq = pemark()
                    bk_i, bk = acquire_bank()
                    b.op("tensor", lambda e, a=raw[kh2][1], bk=bk: e.transpose(bk[:, 0:C], a, ident[:]))
                    pk = pemark()
                    tbanks[kh2] = (bq_i, bq, pq, bk_i, bk, pk)
                for kh2 in range(2):
                    bq_i, bq, pq, bk_i, bk, pk = tbanks[kh2]
                    sqslice = sq2[:, kh2 * 128:(kh2 + 1) * 128]
                    b.wait("scalar", "pe", pq)
                    b.wait("scalar", "dv", mprep_sl[2 * kh2 + 1])
                    b.op("scalar", lambda e, bq=bq, kh2=kh2: e.copy(qt_sb[kh2][:], bq[:, 0:C]),
                         writes=[qt_sb[kh2]])
                    b.op("scalar", lambda e, bq=bq, s=sqslice, kh2=kh2: e.activation(
                        out=s, in_=bq[:, 0:C], func=A.Square, accum_out=rqk[:, 2 * kh2:2 * kh2 + 1]),
                        reads=[], writes=[sq2, rqk])
                    m1 = b.mark("scalar")
                    free_bank(bq_i, "ac", m1)
                    b.wait("scalar", "pe", pk)
                    b.op("scalar", lambda e, bk=bk, kh2=kh2: e.copy(kt_sb[kh2][:], bk[:, 0:C]),
                         writes=[kt_sb[kh2]])
                    b.op("scalar", lambda e, bk=bk, s=sqslice, kh2=kh2: e.activation(
                        out=s, in_=bk[:, 0:C], func=A.Square, accum_out=rqk[:, 2 * kh2 + 1:2 * kh2 + 2]),
                        reads=[], writes=[sq2, rqk])
                    m2 = b.mark("scalar")
                    free_bank(bk_i, "ac", m2)
                mqk = b.cnt["ac"]
                b.wait("vector", "ac", mqk)
                b.op("vector", lambda e: e.tensor_scalar(out=rqk[:], in0=rqk[:], scalar1=1.0 / DK,
                                                         scalar2=EPS_RMS, op0=OP.mult, op1=OP.add),
                     reads=[rqk], writes=[rqk])
                mrr = b.mark("vector")
                b.wait("scalar", "dv", mrr)
                b.op("scalar", lambda e: e.activation(out=rqk[:], in_=rqk[:], func=A.Ln),
                     reads=[rqk], writes=[rqk])
                mln = b.mark("scalar")
                b.wait("vector", "ac", mln)
                b.op("vector", lambda e: e.tensor_scalar(
                    out=rqk[:, 0:1], in0=rqk[:, 0:1], scalar1=-0.5, scalar2=-LN128C,
                    op0=OP.mult, op1=OP.add), reads=[rqk], writes=[rqk])
                b.op("vector", lambda e: e.tensor_scalar(
                    out=rqk[:, 1:2], in0=rqk[:, 1:2], scalar1=-0.5, scalar2=-LN128C / 2.0,
                    op0=OP.mult, op1=OP.add), reads=[rqk], writes=[rqk])
                b.op("vector", lambda e: e.tensor_scalar(
                    out=rqk[:, 2:3], in0=rqk[:, 2:3], scalar1=-0.5, scalar2=-LN128C,
                    op0=OP.mult, op1=OP.add), reads=[rqk], writes=[rqk])
                b.op("vector", lambda e: e.tensor_scalar(
                    out=rqk[:, 3:4], in0=rqk[:, 3:4], scalar1=-0.5, scalar2=-LN128C / 2.0,
                    op0=OP.mult, op1=OP.add), reads=[rqk], writes=[rqk])
                maff = b.mark("vector")
                b.wait("scalar", "dv", maff)
                b.op("scalar", lambda e: e.activation(out=rqk[:], in_=rqk[:], func=A.Exp),
                     reads=[rqk], writes=[rqk])
                prep_msq4[cc] = b.mark("scalar")

            for jj in range(8):
                if jj == 6:
                    emit_qkprep(0)
                wap, wv = wq_consume()
                pb = jj % 2
                bank = P0 if pb == 0 else P1
                b.wait("tensor", wv[0], wv[1])
                if pbank_free[pb] is not None:
                    b.wait("tensor", pbank_free[pb][0], pbank_free[pb][1])
                whap, wlap = wap
                for i in range(NI):
                    b.wait("tensor", "dv", lnx_tm[i])
                    wh = whap[:, i * 128:(i + 1) * 128]
                    wl = wlap[:, i * 128:(i + 1) * 128]
                    mh = lnx16h[:, i * T:(i + 1) * T]
                    ml = lnx16l[:, i * T:(i + 1) * T]
                    b.op("tensor", lambda e, wh=wh, mh=mh, bank=bank, i=i: e.matmul(
                        bank[:, 0:T], wh, mh, start=(i == 0), stop=False))
                    b.op("tensor", lambda e, wh=wh, ml=ml, bank=bank: e.matmul(
                        bank[:, 0:T], wh, ml, start=False, stop=False))
                    b.op("tensor", lambda e, wl=wl, mh=mh, bank=bank, i=i: e.matmul(
                        bank[:, 0:T], wl, mh, start=False, stop=(i == NI - 1)))
                mm_done = pemark()
                wq_release("pe", mm_done)
                slot = jj % 2
                rawap = u_buf[:, slot * 520: slot * 520 + 520]
                jj_g = 8 * g + jj
                b.wait("scalar", "pe", mm_done)
                if jj >= 2:
                    b.wait("scalar", conv_marks[("ceng", jj - 2)], conv_marks[("conv", jj - 2)])
                b.op("scalar", lambda e, rawap=rawap, bank=bank: e.copy(rawap[:, 3:515], bank[:, 0:T]),
                     writes=[u_buf])
                evac_m = b.mark("scalar")
                conv_marks[("evac", jj)] = evac_m
                pbank_free[pb] = ("ac", evac_m)
                ceng = "vector"
                cs = ENG_SEM[ceng]
                b.wait(ceng, "ac", evac_m)
                if jj >= 2:
                    b.wait(ceng, conv_marks[("ceng", jj - 2)], conv_marks[("conv", jj - 2)])
                acc = convacc_[0] if ceng == "vector" else convacc_[1]
                b.op(ceng, lambda e, rawap=rawap, jj_g=jj_g: e.tensor_copy(
                    rawap[:, 0:3], carry[:, jj_g * 3:jj_g * 3 + 3]),
                    reads=[carry], writes=[u_buf])
                b.op(ceng, lambda e, rawap=rawap, jj_g=jj_g, acc=acc: e.tensor_scalar(
                    out=acc[:], in0=rawap[:, 0:T], scalar1=cw_sb[:, jj_g * 4:jj_g * 4 + 1],
                    scalar2=None, op0=OP.mult, op1=OP.bypass),
                    reads=[u_buf, cw_sb], writes=[acc])
                for tau in range(1, 4):
                    b.op(ceng, lambda e, rawap=rawap, jj_g=jj_g, acc=acc, tau=tau: e.scalar_tensor_tensor(
                        out=acc[:], in0=rawap[:, tau:tau + T],
                        scalar=cw_sb[:, jj_g * 4 + tau:jj_g * 4 + tau + 1],
                        in1=acc[:], op0=OP.mult, op1=OP.add),
                        reads=[u_buf, cw_sb, acc], writes=[acc])
                b.op(ceng, lambda e, rawap=rawap, jj_g=jj_g: e.tensor_copy(
                    carry[:, jj_g * 3:jj_g * 3 + 3], rawap[:, 512:515]),
                    reads=[u_buf], writes=[carry])
                cm = b.mark(ceng, cs)
                conv_marks[("conv", jj)] = cm
                conv_marks[("ceng", jj)] = cs
                b.wait("scalar", cs, cm)
                b.op("scalar", lambda e, jj=jj, acc=acc: e.activation(
                    out=qkvc[:, jj * T:(jj + 1) * T], in_=acc[:], func=A.Silu),
                    reads=[acc], writes=[qkvc])
                conv_marks[("silu", jj)] = b.mark("scalar")
                nfill = 1 if jj < 2 else 3
                for _ in range(nfill):
                    f = next(fill_iter, None)
                    if f is not None:
                        f()
            for f in fill_iter:
                f()
            pending = []
            qkv_ready = b.cnt["ac"]

            st_upd_last = [0]
            st_upd_sl = [0, 0, 0, 0]
            prior_finish_pe = [0]
            prior_fin_sl = [0, 0, 0, 0]
            o_done = [0, 0, 0, 0]
            z_evac = [None, None, None, None]

            def emit_zcol(jz):
                wap, wv = wq_consume()
                pb = jz % 2
                bank = P0 if pb == 0 else P1
                b.wait("tensor", wv[0], wv[1])
                if jz < 2:
                    b.wait("tensor", "ac", qkv_ready)
                if pbank_free[pb] is not None:
                    b.wait("tensor", pbank_free[pb][0], pbank_free[pb][1])
                whap, wlap = wap
                for i in range(NI):
                    b.wait("tensor", "dv", lnx_tm[i])
                    wh = whap[:, i * 128:(i + 1) * 128]
                    wl = wlap[:, i * 128:(i + 1) * 128]
                    mh = lnx16h[:, i * T:(i + 1) * T]
                    ml = lnx16l[:, i * T:(i + 1) * T]
                    b.op("tensor", lambda e, wh=wh, mh=mh, bank=bank, i=i: e.matmul(
                        bank[:, 0:T], wh, mh, start=(i == 0), stop=False))
                    b.op("tensor", lambda e, wh=wh, ml=ml, bank=bank: e.matmul(
                        bank[:, 0:T], wh, ml, start=False, stop=False))
                    b.op("tensor", lambda e, wl=wl, mh=mh, bank=bank, i=i: e.matmul(
                        bank[:, 0:T], wl, mh, start=False, stop=(i == NI - 1)))
                mm_done = pemark()
                wq_release("pe", mm_done)
                b.wait("scalar", "pe", mm_done)
                if jz < 2:
                    for jprev in (6, 7):
                        b.wait("scalar", conv_marks[("ceng", jprev)], conv_marks[("conv", jprev)])
                b.op("scalar", lambda e, jz=jz, bank=bank: e.copy(
                    u_buf[:, jz * 520:jz * 520 + 512], bank[:, 0:T]),
                    writes=[u_buf])
                z_evac[jz] = b.mark("scalar")
                pbank_free[pb] = ("ac", z_evac[jz])

            for cc in range(NCC):
                ccol = cc * C
                msq4 = prep_msq4[cc]
                b.wait("vector", "ac", msq4)
                for kh2 in range(2):
                    b.op("vector", lambda e, kh2=kh2: e.tensor_scalar(
                        out=qt_sb[kh2][:], in0=qt_sb[kh2][:], scalar1=rqk[:, 2 * kh2:2 * kh2 + 1],
                        scalar2=None, op0=OP.mult, op1=OP.bypass),
                        reads=[qt_sb[kh2], rqk], writes=[qt_sb[kh2]])
                    b.op("vector", lambda e, kh2=kh2: e.tensor_scalar(
                        out=ktn[kh2][:], in0=kt_sb[kh2][:], scalar1=rqk[:, 2 * kh2 + 1:2 * kh2 + 2],
                        scalar2=None, op0=OP.mult, op1=OP.bypass),
                        reads=[kt_sb[kh2], rqk], writes=[ktn[kh2]])
                mnorm = b.mark("vector")
                b.wait("tensor", "dv", mnorm)
                backb = {}
                for kh2 in range(2):
                    bq2_i, bq2 = acquire_bank()
                    b.op("tensor", lambda e, bq2=bq2, kh2=kh2: e.transpose(
                        bq2[:, 0:C], qt_sb[kh2][:], ident[:]))
                    pq2 = pemark()
                    bk2_i, bk2 = acquire_bank()
                    b.op("tensor", lambda e, bk2=bk2, kh2=kh2: e.transpose(
                        bk2[:, 0:C], ktn[kh2][:], ident[:]))
                    pk2 = pemark()
                    backb[kh2] = (bq2_i, bq2, pq2, bk2_i, bk2, pk2)
                for kh2 in range(2):
                    bq2_i, bq2, pq2, bk2_i, bk2, pk2 = backb[kh2]
                    b.wait("scalar", "pe", pq2)
                    b.op("scalar", lambda e, bq2=bq2, kh2=kh2: e.copy(Qf[kh2][:], bq2[:, 0:C]),
                         writes=[Qf[kh2]])
                    mq = b.mark("scalar")
                    free_bank(bq2_i, "ac", mq)
                    b.wait("scalar", "pe", pk2)
                    b.op("scalar", lambda e, bk2=bk2, kh2=kh2: e.copy(Kf[kh2][:], bk2[:, 0:C]),
                         writes=[Kf[kh2]])
                    mk = b.mark("scalar")
                    free_bank(bk2_i, "ac", mk)
                mqkf = b.cnt["ac"]
                SLK = [(0, 0), (1, 0), (2, 1), (3, 1)]
                gcols = {sl: gct[:, cc * 32 + 4 * g + sl:cc * 32 + 4 * g + sl + 1] for sl in range(4)}
                bcols = {sl: bett[:, cc * 32 + 4 * g + sl:cc * 32 + 4 * g + sl + 1] for sl in range(4)}
                b.wait("vector", "ac", gates_ac)
                b.wait("vector", "dv", gates_dv)
                for sl, kh2 in SLK:
                    b.op("vector", lambda e, sl=sl, gc_=gcols[sl]: e.tensor_scalar(
                        out=e_t[sl][:], in0=ones_all[:], scalar1=gc_, scalar2=None,
                        op0=OP.mult, op1=OP.bypass),
                        reads=[ones_all, gct], writes=[e_t[sl]])
                mgtb = b.mark("vector")
                for sl, kh2 in SLK:
                    b.op("vector", lambda e, sl=sl, bc_=bcols[sl]: e.tensor_scalar(
                        out=curT[sl][:], in0=ones_all[:], scalar1=bc_, scalar2=None,
                        op0=OP.mult, op1=OP.bypass),
                        reads=[ones_all, bett], writes=[curT[sl]])
                mbtb = b.mark("vector")
                b.wait("tensor", "dv", mgtb)
                bcb = {}
                for sl, kh2 in SLK:
                    bBC_i, bBC = acquire_bank()
                    b.op("tensor", lambda e, sl=sl, bBC=bBC: e.transpose(
                        bBC[:, 0:C], e_t[sl][:], ident[:]))
                    bcb[sl] = (bBC_i, bBC, pemark())
                for sl, kh2 in SLK:
                    bBC_i, bBC, pm = bcb[sl]
                    b.wait("vector", "pe", pm)
                    b.op("vector", lambda e, sl=sl, bBC=bBC, gc_=gcols[sl]: e.tensor_scalar(
                        out=e_t[sl][:], in0=bBC[:, 0:C], scalar1=gc_, scalar2=0.0,
                        op0=OP.subtract, op1=OP.min),
                        reads=[gct], writes=[e_t[sl]])
                    me = b.mark("vector")
                    b.wait("scalar", "dv", me)
                    b.op("scalar", lambda e, sl=sl: e.activation(out=Dfull[sl][:], in_=e_t[sl][:],
                                                                 func=A.Exp),
                         reads=[e_t[sl]], writes=[Dfull[sl]])
                    b.op("scalar", lambda e, sl=sl, bBC=bBC: e.activation(out=Dbc[sl][:],
                                                                          in_=bBC[:, 0:C], func=A.Exp),
                         writes=[Dbc[sl]])
                    mexp = b.mark("scalar")
                    free_bank(bBC_i, "ac", mexp)
                    bcb[sl] = mexp
                for sl, kh2 in SLK:
                    b.wait("gpsimd", "ac", bcb[sl])
                    b.op("gpsimd", lambda e, sl=sl: e.affine_select(
                        Dincl[sl][:], Dfull[sl][:], pattern=[[1, 128]], base=0,
                        channel_multiplier=-1, compare_op=OP.is_ge, fill=0.0),
                        reads=[Dfull[sl]], writes=[Dincl[sl]])
                mdincl = b.mark("gpsimd")
                b.wait("tensor", "dv", mbtb)
                bbb = {}
                for sl, kh2 in SLK:
                    bBB_i, bBB = acquire_bank()
                    b.op("tensor", lambda e, sl=sl, bBB=bBB: e.transpose(
                        bBB[:, 0:C], curT[sl][:], ident[:]))
                    bbb[sl] = (bBB_i, bBB, pemark())
                for sl, kh2 in SLK:
                    bBB_i, bBB, pmb = bbb[sl]
                    b.wait("vector", "pe", prior_fin_sl[sl])
                    vraw = qkvc[:, (4 * kh2 + 2 + (sl % 2)) * T + ccol:
                                (4 * kh2 + 2 + (sl % 2)) * T + ccol + C]
                    b.wait("vector", "pe", pmb)
                    b.wait("vector", "ac", bcb[sl])
                    b.wait("vector", "ac", conv_marks[("silu", 4 * kh2 + 2 + (sl % 2))])
                    b.op("vector", lambda e, sl=sl, bBB=bBB: e.tensor_tensor(
                        out=curB[sl][:], in0=Dfull[sl][:], in1=bBB[:, 0:C], op=OP.mult),
                        reads=[Dfull[sl]], writes=[curB[sl]])
                    b.op("vector", lambda e, sl=sl, bBB=bBB: e.tensor_tensor(
                        out=KDf[sl][:], in0=Dbc[sl][:], in1=bBB[:, 0:C], op=OP.mult),
                        reads=[Dbc[sl]], writes=[KDf[sl]])
                    b.op("vector", lambda e, sl=sl, bBB=bBB, vraw=vraw: e.tensor_tensor(
                        out=Bfv[sl][:], in0=vraw, in1=bBB[:, 0:C], op=OP.mult),
                        reads=[qkvc], writes=[Bfv[sl]])
                    mvv = b.mark("vector")
                    free_bank(bBB_i, "dv", mvv)
                b.wait("tensor", "ac", mqkf)
                sc = {}
                for kh2 in range(2):
                    bG_i, bG = acquire_bank()
                    b.op("tensor", lambda e, bG=bG, kh2=kh2: e.matmul(
                        bG[:, 0:C], Kf[kh2][:], Kf[kh2][:], start=True, stop=True))
                    bP_i, bP = acquire_bank()
                    b.op("tensor", lambda e, bP=bP, kh2=kh2: e.matmul(
                        bP[:, 0:C], Kf[kh2][:], Qf[kh2][:], start=True, stop=True))
                    sc[kh2] = (bG_i, bG, bP_i, bP, pemark())
                b.wait("vector", "gp", mdincl)
                mprep_sl = [0, 0, 0, 0]
                for sl, kh2 in SLK:
                    bG_i, bG, bP_i, bP, pm_sc = sc[kh2]
                    b.wait("vector", "pe", pm_sc)
                    b.op("vector", lambda e, sl=sl, bP=bP: e.tensor_tensor(
                        out=Mt[sl][:], in0=bP[:, 0:C], in1=Dincl[sl][:], op=OP.mult),
                        reads=[Dincl[sl]], writes=[Mt[sl]])
                    b.op("vector", lambda e, sl=sl, bG=bG: e.tensor_tensor(
                        out=curT[sl][:], in0=bG[:, 0:C], in1=curB[sl][:], op=OP.mult),
                        reads=[curB[sl], curT[sl]], writes=[curT[sl]])
                    b.op("vector", lambda e, sl=sl: e.tensor_tensor(
                        out=curA[sl][:], in0=curT[sl][:], in1=mS[:], op=OP.mult),
                        reads=[curT[sl], mS], writes=[curA[sl]])
                    b.op("vector", lambda e, sl=sl, kh2=kh2: e.tensor_tensor(
                        out=KDf[sl][:], in0=Kf[kh2][:], in1=KDf[sl][:], op=OP.mult),
                        reads=[Kf[kh2], KDf[sl]], writes=[KDf[sl]])
                    b.op("vector", lambda e, sl=sl, kh2=kh2: e.tensor_tensor(
                        out=QDf[sl][:], in0=Qf[kh2][:], in1=Dbc[sl][:], op=OP.mult),
                        reads=[Qf[kh2], Dbc[sl]], writes=[QDf[sl]])
                    mprep_sl[sl] = b.mark("vector")
                mprep = mprep_sl[3]
                for kh2 in range(2):
                    bG_i, bG, bP_i, bP, _ = sc[kh2]
                    free_bank(bG_i, "dv", mprep_sl[2 * kh2 + 1])
                    free_bank(bP_i, "dv", mprep_sl[2 * kh2 + 1])
                w0b = {}
                for sl, kh2 in SLK:
                    vh = 4 * g + sl
                    b.wait("tensor", "dv", mprep_sl[sl])
                    b.wait("tensor", "dv", st_upd_sl[sl])
                    bW_i, bW = acquire_bank()
                    b.op("tensor", lambda e, bW=bW, vh=vh, sl=sl: e.matmul(
                        bW[:, 0:C], state[:, vh * 128:(vh + 1) * 128], KDf[sl][:],
                        start=True, stop=True))
                    w0b[sl] = (bW_i, bW, pemark())
                mv2_sl = [0, 0, 0, 0]
                for sl, kh2 in SLK:
                    bW_i, bW, pmw = w0b[sl]
                    b.wait("vector", "pe", pmw)
                    b.op("vector", lambda e, sl=sl, bW=bW: e.tensor_tensor(
                        out=Bfv[sl][:], in0=Bfv[sl][:], in1=bW[:, 0:C], op=OP.subtract),
                        reads=[Bfv[sl]], writes=[Bfv[sl]])
                    mv2_sl[sl] = b.mark("vector")
                    free_bank(bW_i, "dv", mv2_sl[sl])
                ut_evac = [0, 0, 0, 0]
                btb_ = {}
                for sl, kh2 in SLK:
                    b.wait("tensor", "dv", mv2_sl[sl])
                    bT_i, bT = acquire_bank()
                    b.op("tensor", lambda e, bT=bT, sl=sl: e.transpose(bT[:, 0:C], Bfv[sl][:],
                                                                       ident[:]))
                    btb_[sl] = (bT_i, bT, pemark())
                for sl, kh2 in SLK:
                    bT_i, bT, pmt = btb_[sl]
                    b.wait("scalar", "pe", pmt)
                    b.op("scalar", lambda e, sl=sl, bT=bT: e.copy(Ut[sl][:], bT[:, 0:C]),
                         writes=[Ut[sl]])
                    ut_evac[sl] = b.mark("scalar")
                    free_bank(bT_i, "ac", ut_evac[sl])
                for sl, kh2 in SLK:
                    b.op("vector", lambda e, sl=sl, kh2=kh2: e.tensor_scalar(
                        out=Kddt[sl][:], in0=ktn[kh2][:], scalar1=Dincl[sl][:, C - 1:C],
                        scalar2=None, op0=OP.mult, op1=OP.bypass),
                        reads=[ktn[kh2], Dincl[sl]], writes=[Kddt[sl]])
                b.mark("vector")

                cur = [curA[i] for i in range(4)]
                nxt = [curB[i] for i in range(4)]
                uadd = [0, 0, 0, 0]
                sqcp = [0, 0, 0, 0]
                for L in range(7):
                    app_banks = []
                    for sl in range(4):
                        bA_i, bA = acquire_bank()
                        if L > 0:
                            b.wait("tensor", "dv", uadd[sl])
                            b.wait("tensor", "ac", sqcp[sl])
                        else:
                            b.wait("tensor", "dv", mprep_sl[sl])
                            b.wait("tensor", "ac", ut_evac[sl])
                        b.op("tensor", lambda e, bA=bA, sl=sl, cur=cur: e.matmul(
                            bA[:, 0:C], cur[sl][:], Ut[sl][:], start=True, stop=True))
                        app_banks.append((bA_i, bA, pemark()))
                    for sl in range(4):
                        bA_i, bA, pm = app_banks[sl]
                        b.wait("vector", "pe", pm)
                        b.op("vector", lambda e, sl=sl, bA=bA: e.tensor_tensor(
                            out=Ut[sl][:], in0=Ut[sl][:], in1=bA[:, 0:C], op=OP.add),
                            reads=[Ut[sl]], writes=[Ut[sl]])
                        uadd[sl] = b.mark("vector")
                        free_bank(bA_i, "dv", uadd[sl])
                    if L < 6:
                        tr_banks = []
                        for sl in range(4):
                            bR_i, bR = acquire_bank()
                            b.op("tensor", lambda e, bR=bR, sl=sl, cur=cur: e.transpose(
                                bR[:, 0:C], cur[sl][:], ident[:]))
                            tr_banks.append((bR_i, bR, pemark()))
                        tr_marks = []
                        for sl in range(4):
                            bR_i, bR, pm = tr_banks[sl]
                            b.wait("scalar", "pe", pm)
                            b.op("scalar", lambda e, sl=sl, bR=bR: e.copy(curT[sl][:], bR[:, 0:C]),
                                 writes=[curT[sl]])
                            mm_ = b.mark("scalar")
                            tr_marks.append(mm_)
                            free_bank(bR_i, "ac", mm_)
                        sq_banks = []
                        for sl in range(4):
                            bS_i, bS = acquire_bank()
                            b.wait("tensor", "ac", tr_marks[sl])
                            b.op("tensor", lambda e, sl=sl, bS=bS, cur=cur: e.matmul(
                                bS[:, 0:C], curT[sl][:], cur[sl][:], start=True, stop=True))
                            sq_banks.append((bS_i, bS, pemark()))
                        for sl in range(4):
                            bS_i, bS, pm = sq_banks[sl]
                            b.wait("scalar", "pe", pm)
                            b.op("scalar", lambda e, sl=sl, bS=bS, nxt=nxt: e.copy(nxt[sl][:],
                                                                                   bS[:, 0:C]),
                                 writes=[nxt[sl]])
                            sqcp[sl] = b.mark("scalar")
                            free_bank(bS_i, "ac", sqcp[sl])
                        cur, nxt = nxt, cur
                for half in range(2):
                    fb = {}
                    for sl in (2 * half, 2 * half + 1):
                        vh = 4 * g + sl
                        b.wait("tensor", "dv", uadd[sl])
                        bW_i, bW = acquire_bank()
                        b.op("tensor", lambda e, sl=sl, bW=bW: e.matmul(
                            bW[:, 0:C], Kddt[sl][:], Ut[sl][:], start=True, stop=True))
                        pm_ds = pemark()
                        bO_i, bO = acquire_bank()
                        b.op("tensor", lambda e, sl=sl, bO=bO: e.matmul(
                            bO[:, 0:C], Ut[sl][:], Mt[sl][:], start=True, stop=False))
                        b.op("tensor", lambda e, sl=sl, vh=vh, bO=bO: e.matmul(
                            bO[:, 0:C], state[:, vh * 128:(vh + 1) * 128], QDf[sl][:],
                            start=False, stop=True))
                        fb[sl] = (bW_i, bW, bO_i, bO, pemark())
                        prior_finish_pe[0] = fb[sl][4]
                        prior_fin_sl[sl] = fb[sl][4]
                    for sl in (2 * half, 2 * half + 1):
                        vh = 4 * g + sl
                        bW_i, bW, bO_i, bO, pm_o = fb[sl]
                        b.wait("vector", "pe", pm_o)
                        b.op("vector", lambda e, sl=sl, vh=vh, bW=bW: e.scalar_tensor_tensor(
                            out=state[:, vh * 128:(vh + 1) * 128],
                            in0=state[:, vh * 128:(vh + 1) * 128],
                            scalar=Dbc[sl][:, C - 1:C], in1=bW[:, 0:C], op0=OP.mult, op1=OP.add),
                            reads=[state, Dbc[sl]], writes=[state])
                        smu = b.mark("vector")
                        st_upd_last[0] = smu
                        st_upd_sl[sl] = smu
                        free_bank(bW_i, "dv", smu)
                        b.wait("scalar", "pe", pm_o)
                        b.op("scalar", lambda e, sl=sl, cc=cc, bO=bO: e.copy(
                            o_buf[:, sl * T + cc * C: sl * T + cc * C + C], bO[:, 0:C]),
                            writes=[o_buf])
                        o_done[sl] = b.mark("scalar")
                        free_bank(bO_i, "ac", o_done[sl])
                if cc + 1 < NCC:
                    emit_qkprep(cc + 1)
                emit_zcol(cc)
            scan_ac = b.cnt["ac"]
            scan_dv = b.cnt["dv"]

            gw = dict(gate_dv=[], rowbuf_pe=[0], wout_evac=[], wout_pe=[])

            def make_gate(half, g=g, z_evac=z_evac, scan_ac=scan_ac, gw=gw, o_done=tuple(o_done)):
                gate_dv = gw["gate_dv"]
                rowbuf_pe = gw["rowbuf_pe"]
                for _one in [half]:
                        sls = (2 * half, 2 * half + 1)
                        rowb = {}
                        for sl in sls:
                            osl = o_buf[:, sl * T:(sl + 1) * T]
                            sqX = sq1 if sl % 2 == 0 else sq2
                            b.wait("scalar", "ac", o_done[sl])
                            b.op("scalar", lambda e, osl=osl, sqX=sqX: e.activation(
                                out=sqX[:], in_=osl, func=A.Square),
                                reads=[o_buf], writes=[sqX])
                            msq = b.mark("scalar")
                            brow_i, brow = acquire_bank()
                            b.wait("tensor", "ac", msq)
                            b.op("tensor", lambda e, brow=brow, sqX=sqX: e.matmul(
                                brow[0:1, 0:T], onesP[:], sqX[:], start=True, stop=True))
                            rowb[sl] = (brow_i, brow, pemark())
                        bcb2 = {}
                        for sl in sls:
                            brow_i, brow, pm_row = rowb[sl]
                            b.wait("scalar", "pe", pm_row)
                            b.wait("scalar", "pe", rowbuf_pe[0])
                            b.op("scalar", lambda e, brow=brow: e.copy(rowbuf[:], brow[0:1, 0:T]),
                                 writes=[rowbuf])
                            mrb = b.mark("scalar")
                            free_bank(brow_i, "ac", mrb)
                            bb_i, bb = acquire_bank()
                            b.wait("tensor", "ac", mrb)
                            b.op("tensor", lambda e, bb=bb: e.matmul(bb[:, 0:T], ones_all[0:1, :], rowbuf[:],
                                                                     start=True, stop=True))
                            bcb2[sl] = (bb_i, bb, pemark())
                            rowbuf_pe[0] = bcb2[sl][2]
                        for sl in sls:
                            bb_i, bb, pm_bb = bcb2[sl]
                            b.wait("vector", "pe", pm_bb)
                            b.op("vector", lambda e, bb=bb: e.tensor_scalar(
                                out=bb[:, 0:T], in0=bb[:, 0:T], scalar1=1.0 / DV, scalar2=EPS_RMS,
                                op0=OP.mult, op1=OP.add), reads=[bb], writes=[bb])
                        mrec = b.mark("vector")
                        b.wait("scalar", "dv", mrec)
                        for sl in sls:
                            bb_i, bb, _ = bcb2[sl]
                            b.op("scalar", lambda e, bb=bb: e.activation(out=bb[:, 0:T], in_=bb[:, 0:T],
                                                                         func=A.Ln), reads=[bb], writes=[bb])
                        mln2 = b.mark("scalar")
                        b.wait("vector", "ac", mln2)
                        for sl in sls:
                            bb_i, bb, _ = bcb2[sl]
                            b.op("vector", lambda e, bb=bb: e.tensor_scalar(
                                out=bb[:, 0:T], in0=bb[:, 0:T], scalar1=-0.5, scalar2=None,
                                op0=OP.mult, op1=OP.bypass), reads=[bb], writes=[bb])
                        maf2 = b.mark("vector")
                        b.wait("scalar", "dv", maf2)
                        for sl in sls:
                            bb_i, bb, _ = bcb2[sl]
                            zsl = u_buf[:, sl * 520:sl * 520 + 512]
                            b.op("scalar", lambda e, bb=bb: e.activation(out=bb[:, 0:T], in_=bb[:, 0:T],
                                                                         func=A.Exp), reads=[bb], writes=[bb])
                            b.wait("scalar", "ac", z_evac[sl])
                            b.op("scalar", lambda e, zsl=zsl: e.activation(out=zsl, in_=zsl, func=A.Silu),
                                 reads=[u_buf], writes=[u_buf])
                        msil = b.mark("scalar")
                        b.wait("vector", "ac", msil)
                        b.wait("vector", "mdma", ld_misc)
                        for sl in sls:
                            bb_i, bb, _ = bcb2[sl]
                            osl = o_buf[:, sl * T:(sl + 1) * T]
                            zsl = u_buf[:, sl * 520:sl * 520 + 512]
                            b.op("vector", lambda e, osl=osl, bb=bb: e.scalar_tensor_tensor(
                                out=osl, in0=osl, scalar=nw[:], in1=bb[:, 0:T], op0=OP.mult, op1=OP.mult),
                                reads=[o_buf, nw], writes=[o_buf])
                            mg1 = b.mark("vector")
                            free_bank(bb_i, "dv", mg1)
                            b.op("vector", lambda e, osl=osl, zsl=zsl: e.tensor_tensor(
                                out=osl, in0=osl, in1=zsl, op=OP.mult),
                                reads=[o_buf, u_buf], writes=[o_buf])
                            gv = g16[sl]
                            b.op("vector", lambda e, osl=osl, d=gv["h0"]: e.tensor_copy(
                                d[:, 0:256], osl[:, 0:256]), reads=[o_buf], writes=[Mt[sl]])
                            b.op("vector", lambda e, osl=osl, d=gv["h1"]: e.tensor_copy(
                                d[:, 0:256], osl[:, 256:512]), reads=[o_buf], writes=[QDf[sl]])
                            b.op("vector", lambda e, osl=osl, h=gv["h0"], d=gv["l0"]: e.tensor_tensor(
                                out=d[:, 0:256], in0=osl[:, 0:256], in1=h[:, 0:256], op=OP.subtract),
                                reads=[o_buf, Mt[sl]], writes=[Ut[sl]])
                            b.op("vector", lambda e, osl=osl, h=gv["h1"], d=gv["l1"]: e.tensor_tensor(
                                out=d[:, 0:256], in0=osl[:, 256:512], in1=h[:, 0:256], op=OP.subtract),
                                reads=[o_buf, QDf[sl]], writes=[curA[sl]])
                            gate_dv.append(b.mark("vector"))


            def make_wout(j, g=g, gw=gw):
                gate_dv = gw["gate_dv"]
                wout_evac = gw["wout_evac"]
                wout_pe = gw["wout_pe"]
                for _one in [j]:
                        wap, wv = wo_consume()
                        pb = j % 2
                        bank = P0 if pb == 0 else P1
                        b.wait("tensor", wv[0], wv[1])
                        b.wait("tensor", "dv", gate_dv[-1])
                        if pbank_free[pb] is not None:
                            b.wait("tensor", pbank_free[pb][0], pbank_free[pb][1])
                        whap, wlap = wap
                        for half in range(2):
                            nmm = 0
                            for i in range(4):
                                wh = whap[:, i * 128:(i + 1) * 128]
                                wl = wlap[:, i * 128:(i + 1) * 128]
                                mh = (g16[i]["h0"] if half == 0 else g16[i]["h1"])[:, 0:256]
                                ml = (g16[i]["l0"] if half == 0 else g16[i]["l1"])[:, 0:256]
                                hsl = slice(half * 256, half * 256 + 256)
                                b.op("tensor", lambda e, wh=wh, mh=mh, bank=bank, hsl=hsl, first=(nmm == 0):
                                     e.matmul(bank[:, hsl], wh, mh, start=first, stop=False))
                                nmm += 1
                                b.op("tensor", lambda e, wh=wh, ml=ml, bank=bank, hsl=hsl: e.matmul(
                                    bank[:, hsl], wh, ml, start=False, stop=False))
                                nmm += 1
                                b.op("tensor", lambda e, wl=wl, mh=mh, bank=bank, hsl=hsl,
                                     last=(i == 3): e.matmul(bank[:, hsl], wl, mh, start=False, stop=last))
                                nmm += 1
                        mm_done = pemark()
                        wout_pe.append(mm_done)
                        wo_release("pe", mm_done)
                        if g == 0:
                            b.wait("scalar", "pe", mm_done)
                            b.op("scalar", lambda e, j=j, bank=bank: e.copy(
                                attn[:, j * T:(j + 1) * T], bank[:, 0:T]), writes=[attn])
                            mk = ("ac", b.mark("scalar"))
                        else:
                            b.wait("vector", "pe", mm_done)
                            b.op("vector", lambda e, j=j, bank=bank: e.tensor_tensor(
                                out=attn[:, j * T:(j + 1) * T], in0=attn[:, j * T:(j + 1) * T],
                                in1=bank[:, 0:T], op=OP.add),
                                reads=[attn], writes=[attn])
                            mk = ("dv", b.mark("vector"))
                        wout_evac.append(mk)
                        pbank_free[pb] = mk


            make_gate(0)
            make_gate(1)
            pending = [(lambda j=j: make_wout(j)) for j in range(16)]
            last_gw = gw
        for f in pending:
            f()
        pending = []
        deps2 = [[last_gw["wout_evac"][j], ("pe", last_gw["wout_pe"][j])] for j in range(16)]
        st2 = ln_stats(attn, tile_deps=deps2)
        ln_apply(attn, x_buf, st2, residual=True)
        phase_barrier()
        st3 = ln_stats(x_buf)
        lnx_ready = ln_apply(x_buf, None, st3, split=(lnx16h, lnx16l))
        lnx_tm = list(ln_apply.tile_marks)
        ffn_evac = []
        for j in range(16):
            wap, wv = wq_consume()
            pb = j % 2
            bank = P0 if pb == 0 else P1
            b.wait("tensor", wv[0], wv[1])
            if pbank_free[pb] is not None:
                b.wait("tensor", pbank_free[pb][0], pbank_free[pb][1])
            whap, wlap = wap
            for i in range(NI):
                b.wait("tensor", "dv", lnx_tm[i])
                wh = whap[:, i * 128:(i + 1) * 128]
                wl = wlap[:, i * 128:(i + 1) * 128]
                mh = lnx16h[:, i * T:(i + 1) * T]
                ml = lnx16l[:, i * T:(i + 1) * T]
                b.op("tensor", lambda e, wh=wh, mh=mh, bank=bank, i=i: e.matmul(
                    bank[:, 0:T], wh, mh, start=(i == 0), stop=False))
                b.op("tensor", lambda e, wh=wh, ml=ml, bank=bank: e.matmul(
                    bank[:, 0:T], wh, ml, start=False, stop=False))
                b.op("tensor", lambda e, wl=wl, mh=mh, bank=bank, i=i: e.matmul(
                    bank[:, 0:T], wl, mh, start=False, stop=(i == NI - 1)))
            mm_done = pemark()
            wq_release("pe", mm_done)
            b.wait("scalar", "pe", mm_done)
            b.op("scalar", lambda e, j=j, bank=bank: e.activation(
                out=attn[:, j * T:(j + 1) * T], in_=bank[:, 0:T], func=A.Identity,
                scale=ffs[:, j:j + 1], bias=ffb[:, j:j + 1]),
                reads=[ffs, ffb], writes=[attn])
            ffn_evac.append(b.mark("scalar"))
            pbank_free[pb] = ("ac", ffn_evac[-1])
        deps4 = [[("ac", ffn_evac[j])] for j in range(16)]
        st4 = ln_stats(attn, tile_deps=deps4)
        res4 = ln_apply(attn, x_buf, st4, residual=True)
        res4_tm = list(ln_apply.tile_marks)
        for i in range(NI):
            b.wait("gpsimd", "dv", res4_tm[i])
            b.op("gpsimd", lambda e, i=i, col0=col0: e.dma_start(
                out=out_d[i * 128:(i + 1) * 128, col0:col0 + T], in_=x_buf[:, i * T:(i + 1) * T]))
            b.mark("gpsimd", "sdma", 16)
        x_store_mark = b.cnt["sdma"]

    b.wait("gpsimd", "sdma", b.cnt["sdma"])

    with nc.Block() as block:
        @block.sync
        def _(e):
            b.replay("sync", e)

        @block.tensor
        def _(e):
            b.replay("tensor", e)

        @block.vector
        def _(e):
            b.replay("vector", e)

        @block.scalar
        def _(e):
            b.replay("scalar", e)

        @block.gpsimd
        def _(e):
            b.replay("gpsimd", e)

    return nc


def host_prep_weights(W_qkvz, W_ba, conv_w, dt_bias, A_log, norm_w, W_out,
                      ffn_w, ffn_scale, ffn_bias):
    order = []
    for g in range(8):
        for kh2 in range(2):
            kh = 2 * g + kh2
            order += [6 * kh + 0, 6 * kh + 1, 6 * kh + 2, 6 * kh + 3]
        order += [6 * (2 * g) + 4, 6 * (2 * g) + 5, 6 * (2 * g + 1) + 4, 6 * (2 * g + 1) + 5]
    wq4 = W_qkvz.reshape(16, 128, 96, 128).transpose(2, 1, 0, 3)
    wq_dv = np.ascontiguousarray(wq4[order]).astype(np.float32)

    wo4 = W_out.reshape(8, 4, 128, 16, 128)
    wo_dv = np.ascontiguousarray(wo4.transpose(0, 3, 2, 1, 4)).astype(np.float32)

    wf_dv = np.ascontiguousarray(
        ffn_w.T.reshape(16, 128, 16, 128).transpose(2, 1, 0, 3)).astype(np.float32)

    perm = []
    for vh in range(32):
        perm.append((vh // 2) * 4 + vh % 2)
    for vh in range(32):
        perm.append((vh // 2) * 4 + 2 + vh % 2)
    wba_dv = np.ascontiguousarray(
        W_ba[:, perm].reshape(16, 128, 64).transpose(1, 0, 2)).astype(np.float32)

    cw_order = []
    for g in range(8):
        cw_order += [2 * g, 16 + 2 * g, 32 + 4 * g, 33 + 4 * g,
                     2 * g + 1, 16 + 2 * g + 1, 34 + 4 * g, 35 + 4 * g]
    cw_dv = np.ascontiguousarray(
        conv_w.reshape(64, 128, 4)[cw_order].transpose(1, 0, 2)).astype(np.float32)

    return dict(
        wq_d=wq_dv, wo_d=wo_dv, wf_d=wf_dv, wba_d=wba_dv, cw_d=cw_dv,
        alog_d=np.ascontiguousarray(A_log.reshape(32, 1)).astype(np.float32),
        dtb_d=np.ascontiguousarray(dt_bias.reshape(32, 1)).astype(np.float32),
        nw_d=np.ascontiguousarray(norm_w.reshape(128, 1)).astype(np.float32),
        fs_d=np.ascontiguousarray(ffn_scale.reshape(16, 128).T).astype(np.float32),
        fb_d=np.ascontiguousarray(ffn_bias.reshape(16, 128).T).astype(np.float32),
    )


from concourse.bass_utils import run_bass_kernel_spmd

_NC_CACHE = {}


def kernel(L0_attn_x, L0_attn_W_qkvz, L0_attn_W_ba, L0_attn_conv_w, L0_attn_dt_bias,
           L0_attn_A_log, L0_attn_norm_w, L0_attn_W_out, L0_ffn_weight,
           L0_ffn_scale_weight, L0_ffn_bias):
    wd = host_prep_weights(
        np.asarray(L0_attn_W_qkvz, dtype=np.float32),
        np.asarray(L0_attn_W_ba, dtype=np.float32),
        np.asarray(L0_attn_conv_w, dtype=np.float32),
        np.asarray(L0_attn_dt_bias, dtype=np.float32),
        np.asarray(L0_attn_A_log, dtype=np.float32),
        np.asarray(L0_attn_norm_w, dtype=np.float32),
        np.asarray(L0_attn_W_out, dtype=np.float32),
        np.asarray(L0_ffn_weight, dtype=np.float32),
        np.asarray(L0_ffn_scale_weight, dtype=np.float32),
        np.asarray(L0_ffn_bias, dtype=np.float32))
    x = np.asarray(L0_attn_x, dtype=np.float32)
    n_cores = 8
    in_maps = []
    for i in range(n_cores):
        m = dict(wd)
        m["x_in"] = np.ascontiguousarray(x[i].T)
        in_maps.append(m)
    nc = build(NT=8)
    res = run_bass_kernel_spmd(nc, in_maps, core_ids=list(range(n_cores)))
    out = np.stack([np.asarray(r["out_d"]).T for r in res.results], axis=0)
    return np.ascontiguousarray(out.astype(np.float32))


# version 19 (final)
# speedup vs baseline: 1.3478x; 1.0897x over previous
import numpy as np
import concourse.bass as bass
import concourse.mybir as mybir

F32 = mybir.dt.float32
BF16 = mybir.dt.bfloat16
A = mybir.ActivationFunctionType
OP = mybir.AluOpType

H = 2048
SEQ = 4096
T = 512
C = 128
NCC = T // C
NK, NV, DK, DV = 16, 32, 128, 128
NG = 8
NI = H // 128
EPS_LN = 1e-5
EPS_RMS = 1e-6
LN128C = 4.852030263919617

ENGINES = ("tensor", "vector", "scalar", "gpsimd", "sync")
ENG_SEM = {"tensor": "pe", "vector": "dv", "scalar": "ac", "gpsimd": "gp"}


class Builder:
    def __init__(self, nc):
        self.nc = nc
        self.prog = {e: [] for e in ENGINES}
        self.sems = {}
        self.cnt = {}
        self.dirty_w = {e: set() for e in ENGINES}
        self.dirty_r = {e: set() for e in ENGINES}
        self.waited = {e: {} for e in ENGINES}

    def sem(self, name):
        self.sems[name] = self.nc.alloc_semaphore(name)
        self.cnt[name] = 0

    @staticmethod
    def _nm(x):
        if x is None:
            return None
        t = getattr(x, "tensor", x)
        return getattr(t, "name", str(t))

    def op(self, eng, fn, reads=(), writes=()):
        r = {self._nm(a) for a in reads} - {None}
        w = {self._nm(a) for a in writes} - {None}
        if (r & self.dirty_w[eng]) or (w & (self.dirty_w[eng] | self.dirty_r[eng])):
            self.prog[eng].append(dict(fn=lambda e: e.drain(), incs=[]))
            self.dirty_w[eng] = set()
            self.dirty_r[eng] = set()
        self.dirty_r[eng] |= r
        self.dirty_w[eng] |= w
        self.prog[eng].append(dict(fn=fn, incs=[]))

    def mark(self, eng, sname=None, n=1):
        sname = sname or ENG_SEM[eng]
        item = self.prog[eng][-1]
        for (s0, _n0) in item["incs"]:
            if s0 == sname:
                return self.cnt[sname]
        assert not item["incs"], "one sem update per instruction"
        item["incs"].append((sname, n))
        self.cnt[sname] += n
        return self.cnt[sname]

    def wait(self, eng, sname, value):
        if value is None or value <= 0:
            return
        if self.waited[eng].get(sname, -1) >= value:
            return
        self.waited[eng][sname] = value
        h = self.sems[sname]
        self.prog[eng].append(dict(fn=lambda e, h=h, v=value: e.wait_ge(h, v), incs=[]))

    def replay(self, eng, e):
        for item in self.prog[eng]:
            inst = item["fn"](e)
            for (sname, n) in item["incs"]:
                inst.then_inc(self.sems[sname], n)


def build(NT=8):
    nc = bass.Bass("TRN2", target_bir_lowering=False)

    x_in = nc.dram_tensor("x_in", [H, SEQ], F32, kind="ExternalInput").ap()
    wq_d = nc.dram_tensor("wq_d", [96, 128, NI, 128], F32, kind="ExternalInput").ap()
    wo_d = nc.dram_tensor("wo_d", [NG, 16, 128, 4, 128], F32, kind="ExternalInput").ap()
    wf_d = nc.dram_tensor("wf_d", [16, 128, NI, 128], F32, kind="ExternalInput").ap()
    wba_d = nc.dram_tensor("wba_d", [128, NI, 64], F32, kind="ExternalInput").ap()
    cw_d = nc.dram_tensor("cw_d", [128, 64, 4], F32, kind="ExternalInput").ap()
    alog_d = nc.dram_tensor("alog_d", [32, 1], F32, kind="ExternalInput").ap()
    dtb_d = nc.dram_tensor("dtb_d", [32, 1], F32, kind="ExternalInput").ap()
    nw_d = nc.dram_tensor("nw_d", [128, 1], F32, kind="ExternalInput").ap()
    fs_d = nc.dram_tensor("fs_d", [128, 16], F32, kind="ExternalInput").ap()
    fb_d = nc.dram_tensor("fb_d", [128, 16], F32, kind="ExternalInput").ap()
    out_d = nc.dram_tensor("out_d", [H, SEQ], F32, kind="ExternalOutput").ap()
    wq16h_d = nc.dram_tensor("wq16h_d", [112, 128, NI, 128], BF16, kind="Internal").ap()
    wq16l_d = nc.dram_tensor("wq16l_d", [112, 128, NI, 128], BF16, kind="Internal").ap()
    wo16h_d = nc.dram_tensor("wo16h_d", [NG, 16, 128, 4, 128], BF16, kind="Internal").ap()
    wo16l_d = nc.dram_tensor("wo16l_d", [NG, 16, 128, 4, 128], BF16, kind="Internal").ap()

    sb = lambda n, s: nc.alloc_sbuf_tensor(n, s, F32).ap()
    x_buf = sb("x_buf", [128, NI * T])
    lnx16h = nc.alloc_sbuf_tensor("lnx16h", [128, NI * T], BF16).ap()
    lnx16l = nc.alloc_sbuf_tensor("lnx16l", [128, NI * T], BF16).ap()
    qkvc = sb("qkvc", [128, 8 * T])
    attn_h = nc.alloc_sbuf_tensor("attn", [128, NI * T], F32)
    attn = attn_h.ap()
    attn16 = attn_h.bitcast(BF16).ap()
    o_buf = sb("o_buf", [128, 4 * T])
    u_buf_h = nc.alloc_sbuf_tensor("u_buf", [128, 4 * 520], F32)
    u_buf = u_buf_h.ap()
    u16 = u_buf_h.bitcast(BF16).ap()
    state = sb("state", [128, 32 * 128])
    carry = sb("carry", [128, 64 * 3])
    wq16h_sb = nc.alloc_sbuf_tensor("wq16h_sb", [128, 2 * NI * 128], BF16).ap()
    wq16l_sb = nc.alloc_sbuf_tensor("wq16l_sb", [128, 2 * NI * 128], BF16).ap()
    wo16h_sb = nc.alloc_sbuf_tensor("wo16h_sb", [128, 2 * 4 * 128], BF16).ap()
    wo16l_sb = nc.alloc_sbuf_tensor("wo16l_sb", [128, 2 * 4 * 128], BF16).ap()
    wba16h = nc.alloc_sbuf_tensor("wba16h", [128, NI * 64], BF16).ap()
    wba16l = nc.alloc_sbuf_tensor("wba16l", [128, NI * 64], BF16).ap()
    cw_sb = sb("cw_sb", [128, 64 * 4])
    mS = sb("mS", [128, 128])
    ident = sb("ident", [128, 128])
    onesP = sb("onesP", [128, 1])
    ones_all = sb("ones_all", [128, 128])
    beta32 = sb("beta32", [32, T])
    sp32 = sb("sp32", [32, T])
    g32 = sp32
    gc32 = sb("gc32", [32, T])
    a32 = gc32[0:32, 0:T]
    gct = sb("gct", [128, NCC * 32])
    bett = sb("bett", [128, NCC * 32])
    qt_sb = [sb("qt_sb0", [128, 128]), sb("qt_sb1", [128, 128])]
    kt_sb = [sb("kt_sb0", [128, 128]), sb("kt_sb1", [128, 128])]
    ktn = [sb("ktn0", [128, 128]), sb("ktn1", [128, 128])]
    Qf = qt_sb
    Kf = kt_sb
    rqk = sb("rqk", [128, 4])
    NVH = 4
    e_t = [sb(f"e_t{i}", [128, 128]) for i in range(NVH)]
    Dfull = [sb(f"Dfull{i}", [128, 128]) for i in range(NVH)]
    Dincl = [sb(f"Dincl{i}", [128, 128]) for i in range(NVH)]
    Dbc = [sb(f"Dbc{i}", [128, 128]) for i in range(NVH)]
    Mt = [sb(f"Mt{i}", [128, 128]) for i in range(NVH)]
    curA = [sb(f"curA{i}", [128, 128]) for i in range(NVH)]
    curB = [sb(f"curB{i}", [128, 128]) for i in range(NVH)]
    curT = [sb(f"curT{i}", [128, 128]) for i in range(NVH)]
    QDf = [sb(f"QDf{i}", [128, 128]) for i in range(NVH)]
    Ut = [sb(f"Ut{i}", [128, 128]) for i in range(NVH)]
    Bfv = e_t
    Kddt = Dfull
    KDf = Ut
    g16 = [dict(
        h0=Mt[i].tensor.bitcast(BF16).ap(),
        h1=QDf[i].tensor.bitcast(BF16).ap(),
        l0=Ut[i].tensor.bitcast(BF16).ap(),
        l1=curA[i].tensor.bitcast(BF16).ap(),
    ) for i in range(NVH)]
    sq1 = sb("sq1", [128, T])
    sq2 = sb("sq2", [128, T])
    convacc_ = [sq1, sq2]
    bag = sq1
    sqsc = sq2[:, 0:128]
    rowbuf = sb("rowbuf", [1, T])
    negA = sb("negA", [32, 1])
    tmpA = sb("tmpA", [32, 1])
    dtb = sb("dtb", [32, 1])
    nw = sb("nw", [128, 1])
    ffs = sb("ffs", [128, 16])
    ffb = sb("ffb", [128, 16])

    P0 = nc.alloc_psum_tensor("P0", [128, 512], F32).ap()
    P1 = nc.alloc_psum_tensor("P1", [128, 512], F32).ap()
    POOLB = [nc.alloc_psum_tensor(f"PB{i}", [128, 512], F32).ap() for i in range(6)]

    b = Builder(nc)
    for s in ("wq0", "wq1", "wo0", "wo1", "mdma", "adma", "xdma", "sdma",
              "pin0", "pin1", "pout0", "pout1", "pe", "dv", "ac", "gp"):
        b.sem(s)

    pbank_free = [None, None]
    bank_state = [dict(free=[], busy=False) for _ in POOLB]
    bank_rr = [0]

    def acquire_bank(writer_eng="tensor"):
        for _ in range(len(POOLB)):
            i = bank_rr[0] % len(POOLB)
            bank_rr[0] += 1
            if not bank_state[i]["busy"]:
                break
        else:
            raise RuntimeError("psum pool exhausted: all banks busy")
        bank_state[i]["busy"] = True
        for (sname, v) in bank_state[i]["free"]:
            b.wait(writer_eng, sname, v)
        bank_state[i]["free"] = []
        return i, POOLB[i]

    def free_bank(i, sname, v):
        bank_state[i]["free"].append((sname, v))
        bank_state[i]["busy"] = False

    def pemark():
        return b.mark("tensor", "pe")

    def phase_barrier(engines=("tensor", "vector", "scalar")):
        snap = {"pe": b.cnt["pe"], "dv": b.cnt["dv"], "ac": b.cnt["ac"]}
        for e_ in engines:
            for s_, v_ in snap.items():
                if ENG_SEM[e_] != s_:
                    b.wait(e_, s_, v_)

    wq_stream = []
    for tc in range(NT):
        for ci in range(112):
            wq_stream.append((wq16h_d[ci], wq16l_d[ci]))
    wo_stream = []
    for tc in range(NT):
        for g in range(NG):
            for j in range(16):
                wo_stream.append((wo16h_d[g, j], wo16l_d[g, j]))

    wq_ptr, wq_used, wq_done, wq_freed = [0], [0], {}, [None, None]
    wo_ptr, wo_used, wo_done, wo_freed = [0], [0], {}, [None, None]

    def _dma_next(ptr, stream, freed, buf, width, done, sembase):
        i = ptr[0]
        if i >= len(stream):
            return
        slot = i % 2
        if freed[slot] is not None:
            b.wait("sync", freed[slot][0], freed[slot][1])
        src = stream[i]
        if isinstance(src, tuple):
            hbuf, lbuf = buf
            dh = hbuf[:, slot * width:(slot + 1) * width]
            dl = lbuf[:, slot * width:(slot + 1) * width]
            b.op("sync", lambda e, d=dh, s=src[0]: e.dma_start(out=d, in_=s))
            b.mark("sync", sembase + str(slot), 16)
            b.op("sync", lambda e, d=dl, s=src[1]: e.dma_start(out=d, in_=s))
            done[i] = (sembase + str(slot), b.mark("sync", sembase + str(slot), 16))
        else:
            dst = buf[:, slot * width:(slot + 1) * width]
            b.op("sync", lambda e, d=dst, s=src: e.dma_start(out=d, in_=s))
            done[i] = (sembase + str(slot), b.mark("sync", sembase + str(slot), 16))
        ptr[0] += 1

    def wq_consume():
        idx = wq_used[0]
        while wq_ptr[0] <= idx + 1 and wq_ptr[0] < len(wq_stream):
            _dma_next(wq_ptr, wq_stream, wq_freed, (wq16h_sb, wq16l_sb), NI * 128, wq_done, "wq")
        slot = idx % 2
        return (wq16h_sb[:, slot * NI * 128:(slot + 1) * NI * 128],
                wq16l_sb[:, slot * NI * 128:(slot + 1) * NI * 128]), wq_done[idx]

    def wq_release(sname, v):
        wq_freed[wq_used[0] % 2] = (sname, v)
        wq_used[0] += 1

    def wo_consume():
        idx = wo_used[0]
        while wo_ptr[0] <= idx + 1 and wo_ptr[0] < len(wo_stream):
            _dma_next(wo_ptr, wo_stream, wo_freed, (wo16h_sb, wo16l_sb), 512, wo_done, "wo")
        slot = idx % 2
        return (wo16h_sb[:, slot * 512:(slot + 1) * 512],
                wo16l_sb[:, slot * 512:(slot + 1) * 512]), wo_done[idx]

    def wo_release(sname, v):
        wo_freed[wo_used[0] % 2] = (sname, v)
        wo_used[0] += 1

    b.op("gpsimd", lambda e: e.memset(curA[0][:], -1.0), writes=[curA[0]])
    b.op("gpsimd", lambda e: e.affine_select(mS[:], curA[0][:], pattern=[[1, 128]], base=0,
                                             channel_multiplier=-1, compare_op=OP.is_gt, fill=0.0),
         reads=[curA[0]], writes=[mS])
    b.op("gpsimd", lambda e: e.memset(curA[0][:], 1.0), writes=[curA[0]])
    b.op("gpsimd", lambda e: e.affine_select(ident[:], curA[0][:], pattern=[[1, 128]], base=0,
                                             channel_multiplier=-1, compare_op=OP.is_equal, fill=0.0),
         reads=[curA[0]], writes=[ident])
    b.op("gpsimd", lambda e: e.memset(onesP[:], 1.0), writes=[onesP])
    b.op("gpsimd", lambda e: e.memset(ones_all[:], 1.0), writes=[ones_all])
    b.op("gpsimd", lambda e: e.memset(state[:], 0.0), writes=[state])
    b.op("gpsimd", lambda e: e.memset(carry[:], 0.0), writes=[carry])
    gp_init = b.mark("gpsimd")
    dv_init = 0

    for dst, src in [(cw_sb[:], cw_d), (tmpA[:], alog_d), (dtb[:], dtb_d),
                     (nw[:], nw_d), (ffs[:], fs_d), (ffb[:], fb_d)]:
        b.op("gpsimd", lambda e, d=dst, s=src: e.dma_start(out=d, in_=s))
        b.mark("gpsimd", "mdma", 16)
    ld_misc = b.cnt["mdma"]
    b.wait("scalar", "mdma", ld_misc)
    b.op("scalar", lambda e: e.activation(out=negA[:], in_=tmpA[:], func=A.Exp),
         reads=[tmpA], writes=[negA])
    ac_negA = b.mark("scalar")
    b.wait("vector", "ac", ac_negA)
    b.op("vector", lambda e: e.tensor_scalar(out=negA[:], in0=negA[:], scalar1=-1.0, scalar2=None,
                                             op0=OP.mult, op1=OP.bypass),
         reads=[negA], writes=[negA])
    dv_negA = b.mark("vector")
    b.wait("tensor", "gp", gp_init)
    b.wait("tensor", "dv", dv_init)
    b.wait("scalar", "gp", gp_init)

    prol_dv = [0, 0]
    pin_cnt = [0, 0]
    pout_cnt = [0, 0]

    def prol_batch(cols, width):
        n = len(cols)
        slot = prol_batch.it % 2
        prol_batch.it += 1
        xbase = slot * 4096
        for k, (s, _h, _l) in enumerate(cols):
            if k == 0 and prol_dv[slot]:
                b.wait("sync", "dv", prol_dv[slot])
            b.op("sync", lambda e, d=x_buf[:, xbase + k * width: xbase + (k + 1) * width],
                 s=s: e.dma_start(out=d, in_=s))
            pin_cnt[slot] += 16
            b.mark("sync", "pin" + str(slot), 16)
        b.wait("vector", "pin" + str(slot), pin_cnt[slot])
        b.wait("vector", "pout" + str(slot), pout_cnt[slot])
        tot = n * width
        xs = x_buf[:, xbase:xbase + tot]
        hs = attn16[:, slot * 4096:slot * 4096 + tot]
        ls = attn16[:, 8192 + slot * 4096:8192 + slot * 4096 + tot]
        b.op("vector", lambda e, hs=hs, xs=xs: e.tensor_copy(hs, xs),
             reads=[x_buf], writes=[attn])
        b.op("vector", lambda e, ls=ls, xs=xs, hs=hs: e.tensor_tensor(
            out=ls, in0=xs, in1=hs, op=OP.subtract),
            reads=[x_buf, attn], writes=[attn])
        prol_dv[slot] = b.mark("vector")
        b.wait("sync", "dv", prol_dv[slot])
        for k, (_s, h, l) in enumerate(cols):
            b.op("sync", lambda e, d=h, s=attn16[:, slot * 4096 + k * width:
                                                  slot * 4096 + (k + 1) * width]: e.dma_start(out=d, in_=s))
            b.mark("sync", "pout" + str(slot), 16)
            b.op("sync", lambda e, d=l, s=attn16[:, 8192 + slot * 4096 + k * width:
                                                  8192 + slot * 4096 + (k + 1) * width]: e.dma_start(out=d, in_=s))
            b.mark("sync", "pout" + str(slot), 16)
            pout_cnt[slot] += 32
    prol_batch.it = 0

    for c0 in range(0, 112, 2):
        cols = []
        for ci in range(c0, min(c0 + 2, 112)):
            srcap = wq_d[ci] if ci < 96 else wf_d[ci - 96]
            cols.append((srcap, wq16h_d[ci], wq16l_d[ci]))
        prol_batch(cols, 2048)
    for gi0 in range(0, NG * 16, 8):
        cols = []
        for gi in range(gi0, gi0 + 8):
            g_, j_ = gi // 16, gi % 16
            cols.append((wo_d[g_, j_], wo16h_d[g_, j_], wo16l_d[g_, j_]))
        prol_batch(cols, 512)
    xsl = x_buf[:, 0:NI * 64]
    b.wait("sync", "dv", prol_dv[0])
    b.wait("sync", "dv", prol_dv[1])
    b.op("sync", lambda e, d=xsl: e.dma_start(out=d, in_=wba_d))
    pin_cnt[0] += 16
    b.mark("sync", "pin0", 16)
    b.wait("vector", "pin0", pin_cnt[0])
    b.op("vector", lambda e, x=xsl: e.tensor_copy(wba16h[:], x), reads=[x_buf], writes=[wba16h])
    b.op("vector", lambda e, x=xsl: e.tensor_tensor(out=wba16l[:], in0=x, in1=wba16h[:],
                                                    op=OP.subtract),
         reads=[x_buf, wba16h], writes=[wba16l])
    wba_ready = b.mark("vector")
    b.wait("sync", "pout0", pout_cnt[0])
    b.wait("sync", "pout1", pout_cnt[1])
    b.wait("sync", "dv", wba_ready)
    prologue_done = b.cnt["dv"]

    def ln_stats(src_buf, tile_deps=None):
        bx_i, bx = acquire_bank()
        bq_i, bq = acquire_bank()
        sq_marks = [None, None]
        for i in range(NI):
            sq = sq1 if i % 2 == 0 else sq2
            if tile_deps is not None:
                for (sn, sv) in tile_deps[i]:
                    b.wait("scalar", sn, sv)
                    b.wait("tensor", sn, sv)
            if sq_marks[i % 2] is not None:
                b.wait("scalar", "pe", sq_marks[i % 2])
            b.op("scalar", lambda e, s=sq, i=i: e.activation(
                out=s[:], in_=src_buf[:, i * T:(i + 1) * T], func=A.Square),
                reads=[src_buf], writes=[sq])
            acm = b.mark("scalar")
            b.op("tensor", lambda e, i=i: e.matmul(
                bx[0:1, 0:T], onesP[:], src_buf[:, i * T:(i + 1) * T],
                start=(i == 0), stop=(i == NI - 1)))
            b.wait("tensor", "ac", acm)
            b.op("tensor", lambda e, s=sq, i=i: e.matmul(
                bq[0:1, 0:T], onesP[:], s[:], start=(i == 0), stop=(i == NI - 1)))
            sq_marks[i % 2] = pemark()
        pe_done = b.cnt["pe"]
        b.wait("scalar", "pe", pe_done)
        b.op("scalar", lambda e: e.copy(rowbuf[:], bx[0:1, 0:T]), writes=[rowbuf])
        m1 = b.mark("scalar")
        free_bank(bx_i, "ac", m1)
        bX_i, bX = acquire_bank()
        b.wait("tensor", "ac", m1)
        b.op("tensor", lambda e: e.matmul(bX[:, 0:T], ones_all[0:1, :], rowbuf[:],
                                          start=True, stop=True))
        pmX = pemark()
        b.wait("scalar", "pe", pmX)
        b.op("scalar", lambda e: e.copy(rowbuf[:], bq[0:1, 0:T]), writes=[rowbuf])
        m2 = b.mark("scalar")
        free_bank(bq_i, "ac", m2)
        bQ_i, bQ = acquire_bank()
        b.wait("tensor", "ac", m2)
        b.op("tensor", lambda e: e.matmul(bQ[:, 0:T], ones_all[0:1, :], rowbuf[:],
                                          start=True, stop=True))
        pmQ = pemark()
        b.wait("vector", "pe", pmQ)
        b.op("vector", lambda e: e.tensor_scalar(out=sq1[:], in0=bX[:, 0:T], scalar1=1.0 / H,
                                                 scalar2=None, op0=OP.mult, op1=OP.bypass),
             writes=[sq1])
        mX = b.mark("vector")
        free_bank(bX_i, "dv", mX)
        b.op("vector", lambda e: e.tensor_tensor(out=sq2[:], in0=sq1[:], in1=sq1[:], op=OP.mult),
             reads=[sq1], writes=[sq2])
        b.op("vector", lambda e: e.scalar_tensor_tensor(
            out=bQ[:, 0:T], in0=bQ[:, 0:T], scalar=1.0 / H, in1=sq2[:],
            op0=OP.mult, op1=OP.subtract),
            reads=[sq2, bQ], writes=[bQ])
        b.op("vector", lambda e: e.tensor_scalar(out=bQ[:, 0:T], in0=bQ[:, 0:T], scalar1=EPS_LN,
                                                 scalar2=None, op0=OP.add, op1=OP.bypass),
             reads=[bQ], writes=[bQ])
        mv = b.mark("vector")
        b.wait("scalar", "dv", mv)
        b.op("scalar", lambda e: e.activation(out=bQ[:, 0:T], in_=bQ[:, 0:T], func=A.Ln),
             reads=[bQ], writes=[bQ])
        mln3 = b.mark("scalar")
        b.wait("vector", "ac", mln3)
        b.op("vector", lambda e: e.tensor_scalar(out=bQ[:, 0:T], in0=bQ[:, 0:T], scalar1=-0.5,
                                                 scalar2=None, op0=OP.mult, op1=OP.bypass),
             reads=[bQ], writes=[bQ])
        mv2 = b.mark("vector")
        b.wait("scalar", "dv", mv2)
        b.op("scalar", lambda e: e.activation(out=bQ[:, 0:T], in_=bQ[:, 0:T], func=A.Exp),
             reads=[bQ], writes=[bQ])
        mac = b.mark("scalar")
        return bQ_i, bQ, mac

    def ln_apply(src_buf, dst_buf, stats, residual=False, split=None):
        ln_apply.tile_marks = []
        bQ_i, bQ, mac = stats
        b.wait("vector", "ac", mac)
        for i in range(NI):
            b.op("vector", lambda e, i=i: e.tensor_tensor(
                out=sq2[:], in0=src_buf[:, i * T:(i + 1) * T], in1=sq1[:], op=OP.subtract),
                reads=[src_buf, sq1], writes=[sq2])
            if split is not None:
                hi, lo = split
                b.op("vector", lambda e, i=i: e.tensor_tensor(
                    out=sq2[:], in0=sq2[:], in1=bQ[:, 0:T], op=OP.mult),
                    reads=[sq2], writes=[sq2])
                b.op("vector", lambda e, i=i, hi=hi: e.tensor_copy(
                    hi[:, i * T:(i + 1) * T], sq2[:]),
                    reads=[sq2], writes=[hi])
                b.op("vector", lambda e, i=i, hi=hi, lo=lo: e.tensor_tensor(
                    out=lo[:, i * T:(i + 1) * T], in0=sq2[:], in1=hi[:, i * T:(i + 1) * T],
                    op=OP.subtract),
                    reads=[sq2, hi], writes=[lo])
                ln_apply.tile_marks.append(b.mark("vector"))
            elif residual:
                b.op("vector", lambda e, i=i: e.tensor_tensor(
                    out=sq2[:], in0=sq2[:], in1=bQ[:, 0:T], op=OP.mult),
                    reads=[sq2], writes=[sq2])
                b.op("vector", lambda e, i=i: e.tensor_tensor(
                    out=dst_buf[:, i * T:(i + 1) * T], in0=dst_buf[:, i * T:(i + 1) * T],
                    in1=sq2[:], op=OP.add),
                    reads=[dst_buf, sq2], writes=[dst_buf])
                ln_apply.tile_marks.append(b.mark("vector"))
            else:
                b.op("vector", lambda e, i=i: e.tensor_tensor(
                    out=dst_buf[:, i * T:(i + 1) * T], in0=sq2[:], in1=bQ[:, 0:T], op=OP.mult),
                    reads=[sq2], writes=[dst_buf])
        dv1 = b.mark("vector")
        free_bank(bQ_i, "dv", dv1)
        return dv1


    x_store_mark = 0

    mprep_sl = [0, 0, 0, 0]
    for tc in range(NT):
        col0 = tc * T
        if tc == 0:
            b.wait("gpsimd", "dv", prologue_done)
        b.wait("gpsimd", "sdma", x_store_mark)
        for i in range(NI):
            b.op("gpsimd", lambda e, i=i, col0=col0: e.dma_start(
                out=x_buf[:, i * T:(i + 1) * T], in_=x_in[i * 128:(i + 1) * 128, col0:col0 + T]))
            b.mark("gpsimd", "xdma", 16)
        x_loaded = b.cnt["xdma"]
        for e_ in ("tensor", "scalar", "vector"):
            b.wait(e_, "xdma", x_loaded)

        st1 = ln_stats(x_buf)
        lnx_ready = ln_apply(x_buf, None, st1, split=(lnx16h, lnx16l))
        lnx_tm = list(ln_apply.tile_marks)

        bba_i, bba = acquire_bank()
        b.wait("tensor", "dv", wba_ready)
        for i in range(NI):
            b.wait("tensor", "dv", lnx_tm[i])
            wh = wba16h[:, i * 64:(i + 1) * 64]
            wl = wba16l[:, i * 64:(i + 1) * 64]
            mh = lnx16h[:, i * T:(i + 1) * T]
            ml = lnx16l[:, i * T:(i + 1) * T]
            b.op("tensor", lambda e, wh=wh, mh=mh, bba=bba, i=i: e.matmul(
                bba[0:64, 0:T], wh, mh, start=(i == 0), stop=False))
            b.op("tensor", lambda e, wh=wh, ml=ml, bba=bba: e.matmul(
                bba[0:64, 0:T], wh, ml, start=False, stop=False))
            b.op("tensor", lambda e, wl=wl, mh=mh, bba=bba, i=i: e.matmul(
                bba[0:64, 0:T], wl, mh, start=False, stop=(i == NI - 1)))
        ba_done = pemark()
        b.wait("scalar", "pe", ba_done)
        b.op("scalar", lambda e, bba=bba: e.activation(out=beta32[:], in_=bba[0:32, 0:T], func=A.Sigmoid),
             writes=[beta32])
        b.op("scalar", lambda e, bba=bba: e.copy(bag[32:64, 0:T], bba[32:64, 0:T]), writes=[bag])
        ba_ac = b.mark("scalar")
        free_bank(bba_i, "ac", ba_ac)
        b.wait("gpsimd", "ac", ba_ac)
        b.op("gpsimd", lambda e: e.dma_start(out=a32, in_=bag[32:64, 0:T]), reads=[bag])
        b.mark("gpsimd", "adma", 16)
        a_shift = b.cnt["adma"]
        b.wait("scalar", "adma", a_shift)
        b.op("scalar", lambda e: e.activation(out=sp32[:], in_=a32, func=A.Exp,
                                              bias=dtb[:], scale=1.0),
             reads=[gc32, dtb], writes=[sp32])
        b.op("scalar", lambda e: e.activation(out=sp32[:], in_=sp32[:], func=A.Ln,
                                              bias=1.0, scale=1.0),
             reads=[sp32], writes=[sp32])
        sp_ac = b.mark("scalar")
        b.wait("vector", "ac", sp_ac)
        b.wait("vector", "dv", dv_negA)
        b.op("vector", lambda e: e.tensor_scalar(out=g32[:], in0=sp32[:], scalar1=negA[:], scalar2=None,
                                                 op0=OP.mult, op1=OP.bypass),
             reads=[sp32, negA], writes=[g32])
        for cc in range(NCC):
            b.op("vector", lambda e, cc=cc: e.tensor_tensor_scan(
                out=gc32[:, cc * C:(cc + 1) * C], data0=g32[:, cc * C:(cc + 1) * C],
                data1=g32[:, cc * C:(cc + 1) * C], initial=0.0, op0=OP.add, op1=OP.bypass),
                reads=[g32], writes=[gc32])
        gates_dv = b.mark("vector")
        b.wait("tensor", "dv", gates_dv)
        for cc in range(NCC):
            for (srcb, dstb) in ((gc32, gct), (beta32, bett)):
                bt_i, bt = acquire_bank()
                b.op("tensor", lambda e, cc=cc, srcb=srcb, bt=bt: e.transpose(
                    bt[:, 0:32], srcb[0:32, cc * C:(cc + 1) * C], ident[0:32, 0:32]))
                pm = pemark()
                b.wait("scalar", "pe", pm)
                b.op("scalar", lambda e, cc=cc, dstb=dstb, bt=bt: e.copy(
                    dstb[:, cc * 32:(cc + 1) * 32], bt[:, 0:32]),
                    writes=[dstb])
                free_bank(bt_i, "ac", b.mark("scalar"))
        gates_ac = b.cnt["ac"]
        b.wait("vector", "ac", gates_ac)

        pending = []
        last_gw = None
        for g in range(NG):
            conv_marks = {}
            fill_iter = iter(pending)
            prep_msq4 = {}

            def emit_qkprep(cc, conv_marks=conv_marks, prep_msq4=prep_msq4):
                ccol = cc * C
                raw = {}
                for kh2 in range(2):
                    raw[kh2] = (qkvc[:, (4 * kh2) * T + ccol: (4 * kh2) * T + ccol + C],
                                qkvc[:, (4 * kh2 + 1) * T + ccol: (4 * kh2 + 1) * T + ccol + C])
                tbanks = {}
                for kh2 in range(2):
                    b.wait("tensor", "ac", conv_marks[("silu", 4 * kh2)])
                    b.wait("tensor", "ac", conv_marks[("silu", 4 * kh2 + 1)])
                    bq_i, bq = acquire_bank()
                    b.op("tensor", lambda e, a=raw[kh2][0], bq=bq: e.transpose(bq[:, 0:C], a, ident[:]))
                    pq = pemark()
                    bk_i, bk = acquire_bank()
                    b.op("tensor", lambda e, a=raw[kh2][1], bk=bk: e.transpose(bk[:, 0:C], a, ident[:]))
                    pk = pemark()
                    tbanks[kh2] = (bq_i, bq, pq, bk_i, bk, pk)
                for kh2 in range(2):
                    bq_i, bq, pq, bk_i, bk, pk = tbanks[kh2]
                    sqslice = sq2[:, kh2 * 128:(kh2 + 1) * 128]
                    b.wait("scalar", "pe", pq)
                    b.wait("scalar", "dv", mprep_sl[2 * kh2 + 1])
                    b.op("scalar", lambda e, bq=bq, kh2=kh2: e.copy(qt_sb[kh2][:], bq[:, 0:C]),
                         writes=[qt_sb[kh2]])
                    b.op("scalar", lambda e, bq=bq, s=sqslice, kh2=kh2: e.activation(
                        out=s, in_=bq[:, 0:C], func=A.Square, accum_out=rqk[:, 2 * kh2:2 * kh2 + 1]),
                        reads=[], writes=[sq2, rqk])
                    m1 = b.mark("scalar")
                    free_bank(bq_i, "ac", m1)
                    b.wait("scalar", "pe", pk)
                    b.op("scalar", lambda e, bk=bk, kh2=kh2: e.copy(kt_sb[kh2][:], bk[:, 0:C]),
                         writes=[kt_sb[kh2]])
                    b.op("scalar", lambda e, bk=bk, s=sqslice, kh2=kh2: e.activation(
                        out=s, in_=bk[:, 0:C], func=A.Square, accum_out=rqk[:, 2 * kh2 + 1:2 * kh2 + 2]),
                        reads=[], writes=[sq2, rqk])
                    m2 = b.mark("scalar")
                    free_bank(bk_i, "ac", m2)
                mqk = b.cnt["ac"]
                b.wait("vector", "ac", mqk)
                b.op("vector", lambda e: e.tensor_scalar(out=rqk[:], in0=rqk[:], scalar1=1.0 / DK,
                                                         scalar2=EPS_RMS, op0=OP.mult, op1=OP.add),
                     reads=[rqk], writes=[rqk])
                mrr = b.mark("vector")
                b.wait("scalar", "dv", mrr)
                b.op("scalar", lambda e: e.activation(out=rqk[:], in_=rqk[:], func=A.Ln),
                     reads=[rqk], writes=[rqk])
                mln = b.mark("scalar")
                b.wait("vector", "ac", mln)
                b.op("vector", lambda e: e.tensor_scalar(
                    out=rqk[:, 0:1], in0=rqk[:, 0:1], scalar1=-0.5, scalar2=-LN128C,
                    op0=OP.mult, op1=OP.add), reads=[rqk], writes=[rqk])
                b.op("vector", lambda e: e.tensor_scalar(
                    out=rqk[:, 1:2], in0=rqk[:, 1:2], scalar1=-0.5, scalar2=-LN128C / 2.0,
                    op0=OP.mult, op1=OP.add), reads=[rqk], writes=[rqk])
                b.op("vector", lambda e: e.tensor_scalar(
                    out=rqk[:, 2:3], in0=rqk[:, 2:3], scalar1=-0.5, scalar2=-LN128C,
                    op0=OP.mult, op1=OP.add), reads=[rqk], writes=[rqk])
                b.op("vector", lambda e: e.tensor_scalar(
                    out=rqk[:, 3:4], in0=rqk[:, 3:4], scalar1=-0.5, scalar2=-LN128C / 2.0,
                    op0=OP.mult, op1=OP.add), reads=[rqk], writes=[rqk])
                maff = b.mark("vector")
                b.wait("scalar", "dv", maff)
                b.op("scalar", lambda e: e.activation(out=rqk[:], in_=rqk[:], func=A.Exp),
                     reads=[rqk], writes=[rqk])
                prep_msq4[cc] = b.mark("scalar")

            for jj in range(8):
                if jj == 6:
                    emit_qkprep(0)
                wap, wv = wq_consume()
                pb = jj % 2
                bank = P0 if pb == 0 else P1
                b.wait("tensor", wv[0], wv[1])
                if pbank_free[pb] is not None:
                    b.wait("tensor", pbank_free[pb][0], pbank_free[pb][1])
                whap, wlap = wap
                for i in range(NI):
                    b.wait("tensor", "dv", lnx_tm[i])
                    wh = whap[:, i * 128:(i + 1) * 128]
                    wl = wlap[:, i * 128:(i + 1) * 128]
                    mh = lnx16h[:, i * T:(i + 1) * T]
                    ml = lnx16l[:, i * T:(i + 1) * T]
                    b.op("tensor", lambda e, wh=wh, mh=mh, bank=bank, i=i: e.matmul(
                        bank[:, 0:T], wh, mh, start=(i == 0), stop=False))
                    b.op("tensor", lambda e, wh=wh, ml=ml, bank=bank: e.matmul(
                        bank[:, 0:T], wh, ml, start=False, stop=False))
                    b.op("tensor", lambda e, wl=wl, mh=mh, bank=bank, i=i: e.matmul(
                        bank[:, 0:T], wl, mh, start=False, stop=(i == NI - 1)))
                mm_done = pemark()
                wq_release("pe", mm_done)
                slot = jj % 2
                rawap = u_buf[:, slot * 520: slot * 520 + 520]
                jj_g = 8 * g + jj
                b.wait("scalar", "pe", mm_done)
                if jj >= 2:
                    b.wait("scalar", conv_marks[("ceng", jj - 2)], conv_marks[("conv", jj - 2)])
                b.op("scalar", lambda e, rawap=rawap, bank=bank: e.copy(rawap[:, 3:515], bank[:, 0:T]),
                     writes=[u_buf])
                evac_m = b.mark("scalar")
                conv_marks[("evac", jj)] = evac_m
                pbank_free[pb] = ("ac", evac_m)
                ceng = "vector"
                cs = ENG_SEM[ceng]
                b.wait(ceng, "ac", evac_m)
                if jj >= 2:
                    b.wait(ceng, conv_marks[("ceng", jj - 2)], conv_marks[("conv", jj - 2)])
                acc = convacc_[0] if ceng == "vector" else convacc_[1]
                b.op(ceng, lambda e, rawap=rawap, jj_g=jj_g: e.tensor_copy(
                    rawap[:, 0:3], carry[:, jj_g * 3:jj_g * 3 + 3]),
                    reads=[carry], writes=[u_buf])
                b.op(ceng, lambda e, rawap=rawap, jj_g=jj_g, acc=acc: e.tensor_scalar(
                    out=acc[:], in0=rawap[:, 0:T], scalar1=cw_sb[:, jj_g * 4:jj_g * 4 + 1],
                    scalar2=None, op0=OP.mult, op1=OP.bypass),
                    reads=[u_buf, cw_sb], writes=[acc])
                for tau in range(1, 4):
                    b.op(ceng, lambda e, rawap=rawap, jj_g=jj_g, acc=acc, tau=tau: e.scalar_tensor_tensor(
                        out=acc[:], in0=rawap[:, tau:tau + T],
                        scalar=cw_sb[:, jj_g * 4 + tau:jj_g * 4 + tau + 1],
                        in1=acc[:], op0=OP.mult, op1=OP.add),
                        reads=[u_buf, cw_sb, acc], writes=[acc])
                b.op(ceng, lambda e, rawap=rawap, jj_g=jj_g: e.tensor_copy(
                    carry[:, jj_g * 3:jj_g * 3 + 3], rawap[:, 512:515]),
                    reads=[u_buf], writes=[carry])
                cm = b.mark(ceng, cs)
                conv_marks[("conv", jj)] = cm
                conv_marks[("ceng", jj)] = cs
                b.wait("scalar", cs, cm)
                b.op("scalar", lambda e, jj=jj, acc=acc: e.activation(
                    out=qkvc[:, jj * T:(jj + 1) * T], in_=acc[:], func=A.Silu),
                    reads=[acc], writes=[qkvc])
                conv_marks[("silu", jj)] = b.mark("scalar")
                nfill = 1 if jj < 2 else 3
                for _ in range(nfill):
                    f = next(fill_iter, None)
                    if f is not None:
                        f()
            for f in fill_iter:
                f()
            pending = []
            qkv_ready = b.cnt["ac"]

            st_upd_last = [0]
            st_upd_sl = [0, 0, 0, 0]
            prior_finish_pe = [0]
            prior_fin_sl = [0, 0, 0, 0]
            o_done = [0, 0, 0, 0]
            z_evac = [None, None, None, None]

            def emit_zcol(jz):
                wap, wv = wq_consume()
                pb = jz % 2
                bank = P0 if pb == 0 else P1
                b.wait("tensor", wv[0], wv[1])
                if jz < 2:
                    b.wait("tensor", "ac", qkv_ready)
                if pbank_free[pb] is not None:
                    b.wait("tensor", pbank_free[pb][0], pbank_free[pb][1])
                whap, wlap = wap
                for i in range(NI):
                    b.wait("tensor", "dv", lnx_tm[i])
                    wh = whap[:, i * 128:(i + 1) * 128]
                    wl = wlap[:, i * 128:(i + 1) * 128]
                    mh = lnx16h[:, i * T:(i + 1) * T]
                    ml = lnx16l[:, i * T:(i + 1) * T]
                    b.op("tensor", lambda e, wh=wh, mh=mh, bank=bank, i=i: e.matmul(
                        bank[:, 0:T], wh, mh, start=(i == 0), stop=False))
                    b.op("tensor", lambda e, wh=wh, ml=ml, bank=bank: e.matmul(
                        bank[:, 0:T], wh, ml, start=False, stop=False))
                    b.op("tensor", lambda e, wl=wl, mh=mh, bank=bank, i=i: e.matmul(
                        bank[:, 0:T], wl, mh, start=False, stop=(i == NI - 1)))
                mm_done = pemark()
                wq_release("pe", mm_done)
                b.wait("scalar", "pe", mm_done)
                if jz < 2:
                    for jprev in (6, 7):
                        b.wait("scalar", conv_marks[("ceng", jprev)], conv_marks[("conv", jprev)])
                b.op("scalar", lambda e, jz=jz, bank=bank: e.copy(
                    u_buf[:, jz * 520:jz * 520 + 512], bank[:, 0:T]),
                    writes=[u_buf])
                z_evac[jz] = b.mark("scalar")
                pbank_free[pb] = ("ac", z_evac[jz])

            for cc in range(NCC):
                ccol = cc * C
                msq4 = prep_msq4[cc]
                b.wait("vector", "ac", msq4)
                for kh2 in range(2):
                    b.op("vector", lambda e, kh2=kh2: e.tensor_scalar(
                        out=qt_sb[kh2][:], in0=qt_sb[kh2][:], scalar1=rqk[:, 2 * kh2:2 * kh2 + 1],
                        scalar2=None, op0=OP.mult, op1=OP.bypass),
                        reads=[qt_sb[kh2], rqk], writes=[qt_sb[kh2]])
                    b.op("vector", lambda e, kh2=kh2: e.tensor_scalar(
                        out=ktn[kh2][:], in0=kt_sb[kh2][:], scalar1=rqk[:, 2 * kh2 + 1:2 * kh2 + 2],
                        scalar2=None, op0=OP.mult, op1=OP.bypass),
                        reads=[kt_sb[kh2], rqk], writes=[ktn[kh2]])
                mnorm = b.mark("vector")
                b.wait("tensor", "dv", mnorm)
                backb = {}
                for kh2 in range(2):
                    bq2_i, bq2 = acquire_bank()
                    b.op("tensor", lambda e, bq2=bq2, kh2=kh2: e.transpose(
                        bq2[:, 0:C], qt_sb[kh2][:], ident[:]))
                    pq2 = pemark()
                    bk2_i, bk2 = acquire_bank()
                    b.op("tensor", lambda e, bk2=bk2, kh2=kh2: e.transpose(
                        bk2[:, 0:C], ktn[kh2][:], ident[:]))
                    pk2 = pemark()
                    backb[kh2] = (bq2_i, bq2, pq2, bk2_i, bk2, pk2)
                for kh2 in range(2):
                    bq2_i, bq2, pq2, bk2_i, bk2, pk2 = backb[kh2]
                    b.wait("scalar", "pe", pq2)
                    b.op("scalar", lambda e, bq2=bq2, kh2=kh2: e.copy(Qf[kh2][:], bq2[:, 0:C]),
                         writes=[Qf[kh2]])
                    mq = b.mark("scalar")
                    free_bank(bq2_i, "ac", mq)
                    b.wait("scalar", "pe", pk2)
                    b.op("scalar", lambda e, bk2=bk2, kh2=kh2: e.copy(Kf[kh2][:], bk2[:, 0:C]),
                         writes=[Kf[kh2]])
                    mk = b.mark("scalar")
                    free_bank(bk2_i, "ac", mk)
                mqkf = b.cnt["ac"]
                SLK = [(0, 0), (1, 0), (2, 1), (3, 1)]
                gcols = {sl: gct[:, cc * 32 + 4 * g + sl:cc * 32 + 4 * g + sl + 1] for sl in range(4)}
                bcols = {sl: bett[:, cc * 32 + 4 * g + sl:cc * 32 + 4 * g + sl + 1] for sl in range(4)}
                b.wait("vector", "ac", gates_ac)
                b.wait("vector", "dv", gates_dv)
                for sl, kh2 in SLK:
                    b.op("vector", lambda e, sl=sl, gc_=gcols[sl]: e.tensor_scalar(
                        out=e_t[sl][:], in0=ones_all[:], scalar1=gc_, scalar2=None,
                        op0=OP.mult, op1=OP.bypass),
                        reads=[ones_all, gct], writes=[e_t[sl]])
                mgtb = b.mark("vector")
                for sl, kh2 in SLK:
                    b.op("vector", lambda e, sl=sl, bc_=bcols[sl]: e.tensor_scalar(
                        out=curT[sl][:], in0=ones_all[:], scalar1=bc_, scalar2=None,
                        op0=OP.mult, op1=OP.bypass),
                        reads=[ones_all, bett], writes=[curT[sl]])
                mbtb = b.mark("vector")
                b.wait("tensor", "dv", mgtb)
                bcb = {}
                for sl, kh2 in SLK:
                    bBC_i, bBC = acquire_bank()
                    b.op("tensor", lambda e, sl=sl, bBC=bBC: e.transpose(
                        bBC[:, 0:C], e_t[sl][:], ident[:]))
                    bcb[sl] = (bBC_i, bBC, pemark())
                for sl, kh2 in SLK:
                    bBC_i, bBC, pm = bcb[sl]
                    b.wait("vector", "pe", pm)
                    b.op("vector", lambda e, sl=sl, bBC=bBC, gc_=gcols[sl]: e.tensor_scalar(
                        out=e_t[sl][:], in0=bBC[:, 0:C], scalar1=gc_, scalar2=0.0,
                        op0=OP.subtract, op1=OP.min),
                        reads=[gct], writes=[e_t[sl]])
                    me = b.mark("vector")
                    b.wait("scalar", "dv", me)
                    b.op("scalar", lambda e, sl=sl: e.activation(out=Dfull[sl][:], in_=e_t[sl][:],
                                                                 func=A.Exp),
                         reads=[e_t[sl]], writes=[Dfull[sl]])
                    b.op("scalar", lambda e, sl=sl, bBC=bBC: e.activation(out=Dbc[sl][:],
                                                                          in_=bBC[:, 0:C], func=A.Exp),
                         writes=[Dbc[sl]])
                    mexp = b.mark("scalar")
                    free_bank(bBC_i, "ac", mexp)
                    bcb[sl] = mexp
                for sl, kh2 in SLK:
                    b.wait("gpsimd", "ac", bcb[sl])
                    b.op("gpsimd", lambda e, sl=sl: e.affine_select(
                        Dincl[sl][:], Dfull[sl][:], pattern=[[1, 128]], base=0,
                        channel_multiplier=-1, compare_op=OP.is_ge, fill=0.0),
                        reads=[Dfull[sl]], writes=[Dincl[sl]])
                mdincl = b.mark("gpsimd")
                b.wait("tensor", "dv", mbtb)
                bbb = {}
                for sl, kh2 in SLK:
                    bBB_i, bBB = acquire_bank()
                    b.op("tensor", lambda e, sl=sl, bBB=bBB: e.transpose(
                        bBB[:, 0:C], curT[sl][:], ident[:]))
                    bbb[sl] = (bBB_i, bBB, pemark())
                for sl, kh2 in SLK:
                    bBB_i, bBB, pmb = bbb[sl]
                    b.wait("vector", "pe", prior_fin_sl[sl])
                    vraw = qkvc[:, (4 * kh2 + 2 + (sl % 2)) * T + ccol:
                                (4 * kh2 + 2 + (sl % 2)) * T + ccol + C]
                    b.wait("vector", "pe", pmb)
                    b.wait("vector", "ac", bcb[sl])
                    b.wait("vector", "ac", conv_marks[("silu", 4 * kh2 + 2 + (sl % 2))])
                    b.op("vector", lambda e, sl=sl, bBB=bBB: e.tensor_tensor(
                        out=curB[sl][:], in0=Dfull[sl][:], in1=bBB[:, 0:C], op=OP.mult),
                        reads=[Dfull[sl]], writes=[curB[sl]])
                    b.op("vector", lambda e, sl=sl, bBB=bBB: e.tensor_tensor(
                        out=KDf[sl][:], in0=Dbc[sl][:], in1=bBB[:, 0:C], op=OP.mult),
                        reads=[Dbc[sl]], writes=[KDf[sl]])
                    b.op("vector", lambda e, sl=sl, bBB=bBB, vraw=vraw: e.tensor_tensor(
                        out=Bfv[sl][:], in0=vraw, in1=bBB[:, 0:C], op=OP.mult),
                        reads=[qkvc], writes=[Bfv[sl]])
                    mvv = b.mark("vector")
                    free_bank(bBB_i, "dv", mvv)
                b.wait("tensor", "ac", mqkf)
                sc = {}
                for kh2 in range(2):
                    bG_i, bG = acquire_bank()
                    b.op("tensor", lambda e, bG=bG, kh2=kh2: e.matmul(
                        bG[:, 0:C], Kf[kh2][:], Kf[kh2][:], start=True, stop=True))
                    bP_i, bP = acquire_bank()
                    b.op("tensor", lambda e, bP=bP, kh2=kh2: e.matmul(
                        bP[:, 0:C], Kf[kh2][:], Qf[kh2][:], start=True, stop=True))
                    sc[kh2] = (bG_i, bG, bP_i, bP, pemark())
                b.wait("vector", "gp", mdincl)
                mprep_sl = [0, 0, 0, 0]
                for sl, kh2 in SLK:
                    bG_i, bG, bP_i, bP, pm_sc = sc[kh2]
                    b.wait("vector", "pe", pm_sc)
                    b.op("vector", lambda e, sl=sl, bP=bP: e.tensor_tensor(
                        out=Mt[sl][:], in0=bP[:, 0:C], in1=Dincl[sl][:], op=OP.mult),
                        reads=[Dincl[sl]], writes=[Mt[sl]])
                    b.op("vector", lambda e, sl=sl, bG=bG: e.tensor_tensor(
                        out=curT[sl][:], in0=bG[:, 0:C], in1=curB[sl][:], op=OP.mult),
                        reads=[curB[sl], curT[sl]], writes=[curT[sl]])
                    b.op("vector", lambda e, sl=sl: e.tensor_tensor(
                        out=curA[sl][:], in0=curT[sl][:], in1=mS[:], op=OP.mult),
                        reads=[curT[sl], mS], writes=[curA[sl]])
                    b.op("vector", lambda e, sl=sl, kh2=kh2: e.tensor_tensor(
                        out=KDf[sl][:], in0=Kf[kh2][:], in1=KDf[sl][:], op=OP.mult),
                        reads=[Kf[kh2], KDf[sl]], writes=[KDf[sl]])
                    b.op("vector", lambda e, sl=sl, kh2=kh2: e.tensor_tensor(
                        out=QDf[sl][:], in0=Qf[kh2][:], in1=Dbc[sl][:], op=OP.mult),
                        reads=[Qf[kh2], Dbc[sl]], writes=[QDf[sl]])
                    mprep_sl[sl] = b.mark("vector")
                mprep = mprep_sl[3]
                for kh2 in range(2):
                    bG_i, bG, bP_i, bP, _ = sc[kh2]
                    free_bank(bG_i, "dv", mprep_sl[2 * kh2 + 1])
                    free_bank(bP_i, "dv", mprep_sl[2 * kh2 + 1])
                w0b = {}
                for sl, kh2 in SLK:
                    vh = 4 * g + sl
                    b.wait("tensor", "dv", mprep_sl[sl])
                    b.wait("tensor", "dv", st_upd_sl[sl])
                    bW_i, bW = acquire_bank()
                    b.op("tensor", lambda e, bW=bW, vh=vh, sl=sl: e.matmul(
                        bW[:, 0:C], state[:, vh * 128:(vh + 1) * 128], KDf[sl][:],
                        start=True, stop=True))
                    w0b[sl] = (bW_i, bW, pemark())
                mv2_sl = [0, 0, 0, 0]
                for sl, kh2 in SLK:
                    bW_i, bW, pmw = w0b[sl]
                    b.wait("vector", "pe", pmw)
                    b.op("vector", lambda e, sl=sl, bW=bW: e.tensor_tensor(
                        out=Bfv[sl][:], in0=Bfv[sl][:], in1=bW[:, 0:C], op=OP.subtract),
                        reads=[Bfv[sl]], writes=[Bfv[sl]])
                    mv2_sl[sl] = b.mark("vector")
                    free_bank(bW_i, "dv", mv2_sl[sl])
                ut_evac = [0, 0, 0, 0]
                btb_ = {}
                for sl, kh2 in SLK:
                    b.wait("tensor", "dv", mv2_sl[sl])
                    bT_i, bT = acquire_bank()
                    b.op("tensor", lambda e, bT=bT, sl=sl: e.transpose(bT[:, 0:C], Bfv[sl][:],
                                                                       ident[:]))
                    btb_[sl] = (bT_i, bT, pemark())
                for sl, kh2 in SLK:
                    bT_i, bT, pmt = btb_[sl]
                    b.wait("scalar", "pe", pmt)
                    b.op("scalar", lambda e, sl=sl, bT=bT: e.copy(Ut[sl][:], bT[:, 0:C]),
                         writes=[Ut[sl]])
                    ut_evac[sl] = b.mark("scalar")
                    free_bank(bT_i, "ac", ut_evac[sl])
                for sl, kh2 in SLK:
                    b.op("vector", lambda e, sl=sl, kh2=kh2: e.tensor_scalar(
                        out=Kddt[sl][:], in0=ktn[kh2][:], scalar1=Dincl[sl][:, C - 1:C],
                        scalar2=None, op0=OP.mult, op1=OP.bypass),
                        reads=[ktn[kh2], Dincl[sl]], writes=[Kddt[sl]])
                b.mark("vector")

                cur = [curA[i] for i in range(4)]
                nxt = [curB[i] for i in range(4)]
                uadd = [0, 0, 0, 0]
                sqcp = [0, 0, 0, 0]
                for L in range(7):
                    app_banks = []
                    for sl in range(4):
                        bA_i, bA = acquire_bank()
                        if L > 0:
                            b.wait("tensor", "dv", uadd[sl])
                            b.wait("tensor", "ac", sqcp[sl])
                        else:
                            b.wait("tensor", "dv", mprep_sl[sl])
                            b.wait("tensor", "ac", ut_evac[sl])
                        b.op("tensor", lambda e, bA=bA, sl=sl, cur=cur: e.matmul(
                            bA[:, 0:C], cur[sl][:], Ut[sl][:], start=True, stop=True))
                        app_banks.append((bA_i, bA, pemark()))
                    for sl in range(4):
                        bA_i, bA, pm = app_banks[sl]
                        b.wait("vector", "pe", pm)
                        b.op("vector", lambda e, sl=sl, bA=bA: e.tensor_tensor(
                            out=Ut[sl][:], in0=Ut[sl][:], in1=bA[:, 0:C], op=OP.add),
                            reads=[Ut[sl]], writes=[Ut[sl]])
                        uadd[sl] = b.mark("vector")
                        free_bank(bA_i, "dv", uadd[sl])
                    if L < 6:
                        tr_banks = []
                        for sl in range(4):
                            bR_i, bR = acquire_bank()
                            b.op("tensor", lambda e, bR=bR, sl=sl, cur=cur: e.transpose(
                                bR[:, 0:C], cur[sl][:], ident[:]))
                            tr_banks.append((bR_i, bR, pemark()))
                        tr_marks = []
                        for sl in range(4):
                            bR_i, bR, pm = tr_banks[sl]
                            b.wait("scalar", "pe", pm)
                            b.op("scalar", lambda e, sl=sl, bR=bR: e.copy(curT[sl][:], bR[:, 0:C]),
                                 writes=[curT[sl]])
                            mm_ = b.mark("scalar")
                            tr_marks.append(mm_)
                            free_bank(bR_i, "ac", mm_)
                        sq_banks = []
                        for sl in range(4):
                            bS_i, bS = acquire_bank()
                            b.wait("tensor", "ac", tr_marks[sl])
                            b.op("tensor", lambda e, sl=sl, bS=bS, cur=cur: e.matmul(
                                bS[:, 0:C], curT[sl][:], cur[sl][:], start=True, stop=True))
                            sq_banks.append((bS_i, bS, pemark()))
                        for sl in range(4):
                            bS_i, bS, pm = sq_banks[sl]
                            b.wait("scalar", "pe", pm)
                            b.op("scalar", lambda e, sl=sl, bS=bS, nxt=nxt: e.copy(nxt[sl][:],
                                                                                   bS[:, 0:C]),
                                 writes=[nxt[sl]])
                            sqcp[sl] = b.mark("scalar")
                            free_bank(bS_i, "ac", sqcp[sl])
                        cur, nxt = nxt, cur
                for half in range(2):
                    fb = {}
                    for sl in (2 * half, 2 * half + 1):
                        vh = 4 * g + sl
                        b.wait("tensor", "dv", uadd[sl])
                        bW_i, bW = acquire_bank()
                        b.op("tensor", lambda e, sl=sl, bW=bW: e.matmul(
                            bW[:, 0:C], Kddt[sl][:], Ut[sl][:], start=True, stop=True))
                        pm_ds = pemark()
                        bO_i, bO = acquire_bank()
                        b.op("tensor", lambda e, sl=sl, bO=bO: e.matmul(
                            bO[:, 0:C], Ut[sl][:], Mt[sl][:], start=True, stop=False))
                        b.op("tensor", lambda e, sl=sl, vh=vh, bO=bO: e.matmul(
                            bO[:, 0:C], state[:, vh * 128:(vh + 1) * 128], QDf[sl][:],
                            start=False, stop=True))
                        fb[sl] = (bW_i, bW, bO_i, bO, pemark())
                        prior_finish_pe[0] = fb[sl][4]
                        prior_fin_sl[sl] = fb[sl][4]
                    for sl in (2 * half, 2 * half + 1):
                        vh = 4 * g + sl
                        bW_i, bW, bO_i, bO, pm_o = fb[sl]
                        b.wait("vector", "pe", pm_o)
                        b.op("vector", lambda e, sl=sl, vh=vh, bW=bW: e.scalar_tensor_tensor(
                            out=state[:, vh * 128:(vh + 1) * 128],
                            in0=state[:, vh * 128:(vh + 1) * 128],
                            scalar=Dbc[sl][:, C - 1:C], in1=bW[:, 0:C], op0=OP.mult, op1=OP.add),
                            reads=[state, Dbc[sl]], writes=[state])
                        smu = b.mark("vector")
                        st_upd_last[0] = smu
                        st_upd_sl[sl] = smu
                        free_bank(bW_i, "dv", smu)
                        b.wait("scalar", "pe", pm_o)
                        b.op("scalar", lambda e, sl=sl, cc=cc, bO=bO: e.copy(
                            o_buf[:, sl * T + cc * C: sl * T + cc * C + C], bO[:, 0:C]),
                            writes=[o_buf])
                        o_done[sl] = b.mark("scalar")
                        free_bank(bO_i, "ac", o_done[sl])
                if cc + 1 < NCC:
                    emit_qkprep(cc + 1)
                    emit_zcol(cc)
            scan_ac = b.cnt["ac"]
            scan_dv = b.cnt["dv"]

            gw = dict(gate_dv=[], rowbuf_pe=[0], wout_evac=[], wout_pe=[])

            def make_gate(half, g=g, z_evac=z_evac, scan_ac=scan_ac, gw=gw, o_done=tuple(o_done)):
                gate_dv = gw["gate_dv"]
                rowbuf_pe = gw["rowbuf_pe"]
                for _one in [half]:
                        sls = (2 * half, 2 * half + 1)
                        rowb = {}
                        for sl in sls:
                            osl = o_buf[:, sl * T:(sl + 1) * T]
                            sqX = sq1 if sl % 2 == 0 else sq2
                            b.wait("scalar", "ac", o_done[sl])
                            b.op("scalar", lambda e, osl=osl, sqX=sqX: e.activation(
                                out=sqX[:], in_=osl, func=A.Square),
                                reads=[o_buf], writes=[sqX])
                            msq = b.mark("scalar")
                            brow_i, brow = acquire_bank()
                            b.wait("tensor", "ac", msq)
                            b.op("tensor", lambda e, brow=brow, sqX=sqX: e.matmul(
                                brow[0:1, 0:T], onesP[:], sqX[:], start=True, stop=True))
                            rowb[sl] = (brow_i, brow, pemark())
                        bcb2 = {}
                        for sl in sls:
                            brow_i, brow, pm_row = rowb[sl]
                            b.wait("scalar", "pe", pm_row)
                            b.wait("scalar", "pe", rowbuf_pe[0])
                            b.op("scalar", lambda e, brow=brow: e.copy(rowbuf[:], brow[0:1, 0:T]),
                                 writes=[rowbuf])
                            mrb = b.mark("scalar")
                            free_bank(brow_i, "ac", mrb)
                            bb_i, bb = acquire_bank()
                            b.wait("tensor", "ac", mrb)
                            b.op("tensor", lambda e, bb=bb: e.matmul(bb[:, 0:T], ones_all[0:1, :], rowbuf[:],
                                                                     start=True, stop=True))
                            bcb2[sl] = (bb_i, bb, pemark())
                            rowbuf_pe[0] = bcb2[sl][2]
                        for sl in sls:
                            bb_i, bb, pm_bb = bcb2[sl]
                            b.wait("vector", "pe", pm_bb)
                            b.op("vector", lambda e, bb=bb: e.tensor_scalar(
                                out=bb[:, 0:T], in0=bb[:, 0:T], scalar1=1.0 / DV, scalar2=EPS_RMS,
                                op0=OP.mult, op1=OP.add), reads=[bb], writes=[bb])
                        mrec = b.mark("vector")
                        b.wait("scalar", "dv", mrec)
                        for sl in sls:
                            bb_i, bb, _ = bcb2[sl]
                            b.op("scalar", lambda e, bb=bb: e.activation(out=bb[:, 0:T], in_=bb[:, 0:T],
                                                                         func=A.Ln), reads=[bb], writes=[bb])
                        mln2 = b.mark("scalar")
                        b.wait("vector", "ac", mln2)
                        for sl in sls:
                            bb_i, bb, _ = bcb2[sl]
                            b.op("vector", lambda e, bb=bb: e.tensor_scalar(
                                out=bb[:, 0:T], in0=bb[:, 0:T], scalar1=-0.5, scalar2=None,
                                op0=OP.mult, op1=OP.bypass), reads=[bb], writes=[bb])
                        maf2 = b.mark("vector")
                        b.wait("scalar", "dv", maf2)
                        for sl in sls:
                            bb_i, bb, _ = bcb2[sl]
                            zsl = u_buf[:, sl * 520:sl * 520 + 512]
                            b.op("scalar", lambda e, bb=bb: e.activation(out=bb[:, 0:T], in_=bb[:, 0:T],
                                                                         func=A.Exp), reads=[bb], writes=[bb])
                            b.wait("scalar", "ac", z_evac[sl])
                            b.op("scalar", lambda e, zsl=zsl: e.activation(out=zsl, in_=zsl, func=A.Silu),
                                 reads=[u_buf], writes=[u_buf])
                        msil = b.mark("scalar")
                        b.wait("vector", "ac", msil)
                        b.wait("vector", "mdma", ld_misc)
                        for sl in sls:
                            bb_i, bb, _ = bcb2[sl]
                            osl = o_buf[:, sl * T:(sl + 1) * T]
                            zsl = u_buf[:, sl * 520:sl * 520 + 512]
                            b.op("vector", lambda e, osl=osl, bb=bb: e.scalar_tensor_tensor(
                                out=osl, in0=osl, scalar=nw[:], in1=bb[:, 0:T], op0=OP.mult, op1=OP.mult),
                                reads=[o_buf, nw], writes=[o_buf])
                            mg1 = b.mark("vector")
                            free_bank(bb_i, "dv", mg1)
                            b.op("vector", lambda e, osl=osl, zsl=zsl: e.tensor_tensor(
                                out=osl, in0=osl, in1=zsl, op=OP.mult),
                                reads=[o_buf, u_buf], writes=[o_buf])
                            gv = g16[sl]
                            b.op("vector", lambda e, osl=osl, d=gv["h0"]: e.tensor_copy(
                                d[:, 0:256], osl[:, 0:256]), reads=[o_buf], writes=[Mt[sl]])
                            b.op("vector", lambda e, osl=osl, d=gv["h1"]: e.tensor_copy(
                                d[:, 0:256], osl[:, 256:512]), reads=[o_buf], writes=[QDf[sl]])
                            b.op("vector", lambda e, osl=osl, h=gv["h0"], d=gv["l0"]: e.tensor_tensor(
                                out=d[:, 0:256], in0=osl[:, 0:256], in1=h[:, 0:256], op=OP.subtract),
                                reads=[o_buf, Mt[sl]], writes=[Ut[sl]])
                            b.op("vector", lambda e, osl=osl, h=gv["h1"], d=gv["l1"]: e.tensor_tensor(
                                out=d[:, 0:256], in0=osl[:, 256:512], in1=h[:, 0:256], op=OP.subtract),
                                reads=[o_buf, QDf[sl]], writes=[curA[sl]])
                            gate_dv.append(b.mark("vector"))


            def make_wout(j, g=g, gw=gw):
                gate_dv = gw["gate_dv"]
                wout_evac = gw["wout_evac"]
                wout_pe = gw["wout_pe"]
                for _one in [j]:
                        wap, wv = wo_consume()
                        pb = j % 2
                        bank = P0 if pb == 0 else P1
                        b.wait("tensor", wv[0], wv[1])
                        b.wait("tensor", "dv", gate_dv[-1])
                        if pbank_free[pb] is not None:
                            b.wait("tensor", pbank_free[pb][0], pbank_free[pb][1])
                        whap, wlap = wap
                        for half in range(2):
                            nmm = 0
                            for i in range(4):
                                wh = whap[:, i * 128:(i + 1) * 128]
                                wl = wlap[:, i * 128:(i + 1) * 128]
                                mh = (g16[i]["h0"] if half == 0 else g16[i]["h1"])[:, 0:256]
                                ml = (g16[i]["l0"] if half == 0 else g16[i]["l1"])[:, 0:256]
                                hsl = slice(half * 256, half * 256 + 256)
                                b.op("tensor", lambda e, wh=wh, mh=mh, bank=bank, hsl=hsl, first=(nmm == 0):
                                     e.matmul(bank[:, hsl], wh, mh, start=first, stop=False))
                                nmm += 1
                                b.op("tensor", lambda e, wh=wh, ml=ml, bank=bank, hsl=hsl: e.matmul(
                                    bank[:, hsl], wh, ml, start=False, stop=False))
                                nmm += 1
                                b.op("tensor", lambda e, wl=wl, mh=mh, bank=bank, hsl=hsl,
                                     last=(i == 3): e.matmul(bank[:, hsl], wl, mh, start=False, stop=last))
                                nmm += 1
                        mm_done = pemark()
                        wout_pe.append(mm_done)
                        wo_release("pe", mm_done)
                        if g == 0:
                            b.wait("scalar", "pe", mm_done)
                            b.op("scalar", lambda e, j=j, bank=bank: e.copy(
                                attn[:, j * T:(j + 1) * T], bank[:, 0:T]), writes=[attn])
                            mk = ("ac", b.mark("scalar"))
                        else:
                            b.wait("vector", "pe", mm_done)
                            b.op("vector", lambda e, j=j, bank=bank: e.tensor_tensor(
                                out=attn[:, j * T:(j + 1) * T], in0=attn[:, j * T:(j + 1) * T],
                                in1=bank[:, 0:T], op=OP.add),
                                reads=[attn], writes=[attn])
                            mk = ("dv", b.mark("vector"))
                        wout_evac.append(mk)
                        pbank_free[pb] = mk


            make_gate(0)
            emit_zcol(3)
            make_gate(1)
            pending = [(lambda j=j: make_wout(j)) for j in range(16)]
            last_gw = gw
        for f in pending:
            f()
        pending = []
        deps2 = [[last_gw["wout_evac"][j], ("pe", last_gw["wout_pe"][j])] for j in range(16)]
        st2 = ln_stats(attn, tile_deps=deps2)
        ln_apply(attn, x_buf, st2, residual=True)
        phase_barrier()
        st3 = ln_stats(x_buf)
        lnx_ready = ln_apply(x_buf, None, st3, split=(lnx16h, lnx16l))
        lnx_tm = list(ln_apply.tile_marks)
        ffn_evac = []
        for j in range(16):
            wap, wv = wq_consume()
            pb = j % 2
            bank = P0 if pb == 0 else P1
            b.wait("tensor", wv[0], wv[1])
            if pbank_free[pb] is not None:
                b.wait("tensor", pbank_free[pb][0], pbank_free[pb][1])
            whap, wlap = wap
            for i in range(NI):
                b.wait("tensor", "dv", lnx_tm[i])
                wh = whap[:, i * 128:(i + 1) * 128]
                wl = wlap[:, i * 128:(i + 1) * 128]
                mh = lnx16h[:, i * T:(i + 1) * T]
                ml = lnx16l[:, i * T:(i + 1) * T]
                b.op("tensor", lambda e, wh=wh, mh=mh, bank=bank, i=i: e.matmul(
                    bank[:, 0:T], wh, mh, start=(i == 0), stop=False))
                b.op("tensor", lambda e, wh=wh, ml=ml, bank=bank: e.matmul(
                    bank[:, 0:T], wh, ml, start=False, stop=False))
                b.op("tensor", lambda e, wl=wl, mh=mh, bank=bank, i=i: e.matmul(
                    bank[:, 0:T], wl, mh, start=False, stop=(i == NI - 1)))
            mm_done = pemark()
            wq_release("pe", mm_done)
            b.wait("scalar", "pe", mm_done)
            b.op("scalar", lambda e, j=j, bank=bank: e.activation(
                out=attn[:, j * T:(j + 1) * T], in_=bank[:, 0:T], func=A.Identity,
                scale=ffs[:, j:j + 1], bias=ffb[:, j:j + 1]),
                reads=[ffs, ffb], writes=[attn])
            ffn_evac.append(b.mark("scalar"))
            pbank_free[pb] = ("ac", ffn_evac[-1])
        deps4 = [[("ac", ffn_evac[j])] for j in range(16)]
        st4 = ln_stats(attn, tile_deps=deps4)
        res4 = ln_apply(attn, x_buf, st4, residual=True)
        res4_tm = list(ln_apply.tile_marks)
        for i in range(NI):
            b.wait("gpsimd", "dv", res4_tm[i])
            b.op("gpsimd", lambda e, i=i, col0=col0: e.dma_start(
                out=out_d[i * 128:(i + 1) * 128, col0:col0 + T], in_=x_buf[:, i * T:(i + 1) * T]))
            b.mark("gpsimd", "sdma", 16)
        x_store_mark = b.cnt["sdma"]

    b.wait("gpsimd", "sdma", b.cnt["sdma"])

    with nc.Block() as block:
        @block.sync
        def _(e):
            b.replay("sync", e)

        @block.tensor
        def _(e):
            b.replay("tensor", e)

        @block.vector
        def _(e):
            b.replay("vector", e)

        @block.scalar
        def _(e):
            b.replay("scalar", e)

        @block.gpsimd
        def _(e):
            b.replay("gpsimd", e)

    return nc


def host_prep_weights(W_qkvz, W_ba, conv_w, dt_bias, A_log, norm_w, W_out,
                      ffn_w, ffn_scale, ffn_bias):
    order = []
    for g in range(8):
        for kh2 in range(2):
            kh = 2 * g + kh2
            order += [6 * kh + 0, 6 * kh + 1, 6 * kh + 2, 6 * kh + 3]
        order += [6 * (2 * g) + 4, 6 * (2 * g) + 5, 6 * (2 * g + 1) + 4, 6 * (2 * g + 1) + 5]
    wq4 = W_qkvz.reshape(16, 128, 96, 128).transpose(2, 1, 0, 3)
    wq_dv = np.ascontiguousarray(wq4[order]).astype(np.float32)

    wo4 = W_out.reshape(8, 4, 128, 16, 128)
    wo_dv = np.ascontiguousarray(wo4.transpose(0, 3, 2, 1, 4)).astype(np.float32)

    wf_dv = np.ascontiguousarray(
        ffn_w.T.reshape(16, 128, 16, 128).transpose(2, 1, 0, 3)).astype(np.float32)

    perm = []
    for vh in range(32):
        perm.append((vh // 2) * 4 + vh % 2)
    for vh in range(32):
        perm.append((vh // 2) * 4 + 2 + vh % 2)
    wba_dv = np.ascontiguousarray(
        W_ba[:, perm].reshape(16, 128, 64).transpose(1, 0, 2)).astype(np.float32)

    cw_order = []
    for g in range(8):
        cw_order += [2 * g, 16 + 2 * g, 32 + 4 * g, 33 + 4 * g,
                     2 * g + 1, 16 + 2 * g + 1, 34 + 4 * g, 35 + 4 * g]
    cw_dv = np.ascontiguousarray(
        conv_w.reshape(64, 128, 4)[cw_order].transpose(1, 0, 2)).astype(np.float32)

    return dict(
        wq_d=wq_dv, wo_d=wo_dv, wf_d=wf_dv, wba_d=wba_dv, cw_d=cw_dv,
        alog_d=np.ascontiguousarray(A_log.reshape(32, 1)).astype(np.float32),
        dtb_d=np.ascontiguousarray(dt_bias.reshape(32, 1)).astype(np.float32),
        nw_d=np.ascontiguousarray(norm_w.reshape(128, 1)).astype(np.float32),
        fs_d=np.ascontiguousarray(ffn_scale.reshape(16, 128).T).astype(np.float32),
        fb_d=np.ascontiguousarray(ffn_bias.reshape(16, 128).T).astype(np.float32),
    )


from concourse.bass_utils import run_bass_kernel_spmd

_NC_CACHE = {}


def kernel(L0_attn_x, L0_attn_W_qkvz, L0_attn_W_ba, L0_attn_conv_w, L0_attn_dt_bias,
           L0_attn_A_log, L0_attn_norm_w, L0_attn_W_out, L0_ffn_weight,
           L0_ffn_scale_weight, L0_ffn_bias):
    wd = host_prep_weights(
        np.asarray(L0_attn_W_qkvz, dtype=np.float32),
        np.asarray(L0_attn_W_ba, dtype=np.float32),
        np.asarray(L0_attn_conv_w, dtype=np.float32),
        np.asarray(L0_attn_dt_bias, dtype=np.float32),
        np.asarray(L0_attn_A_log, dtype=np.float32),
        np.asarray(L0_attn_norm_w, dtype=np.float32),
        np.asarray(L0_attn_W_out, dtype=np.float32),
        np.asarray(L0_ffn_weight, dtype=np.float32),
        np.asarray(L0_ffn_scale_weight, dtype=np.float32),
        np.asarray(L0_ffn_bias, dtype=np.float32))
    x = np.asarray(L0_attn_x, dtype=np.float32)
    n_cores = 8
    in_maps = []
    for i in range(n_cores):
        m = dict(wd)
        m["x_in"] = np.ascontiguousarray(x[i].T)
        in_maps.append(m)
    nc = build(NT=8)
    res = run_bass_kernel_spmd(nc, in_maps, core_ids=list(range(n_cores)))
    out = np.stack([np.asarray(r["out_d"]).T for r in res.results], axis=0)
    return np.ascontiguousarray(out.astype(np.float32))
